# version 02
# baseline (speedup 1.0000x reference)
import numpy as np
import ml_dtypes
import concourse.bass as bass
import concourse.mybir as mybir
from concourse.bass_utils import run_bass_kernel_spmd

F32 = mybir.dt.float32
BF16 = mybir.dt.bfloat16
AF = mybir.ActivationFunctionType
ALU = mybir.AluOpType
AX = mybir.AxisListType
ET = mybir.EngineType

BF = ml_dtypes.bfloat16
T = 4096
E = 1024
NCORE = 8
EPS = 1e-5
C0 = 30.0


def attn_plan(qb):
    lmin = 256 * (2 * qb + 1)
    ch = [(kc, kc * 128, 0, 512) for kc in range(lmin // 128)]
    ch += [(lmin // 128 + j, lmin + j * 128, 256, 256) for j in range(2)]
    return ch


PTRING = 16


def build_mla():
    nc = bass.Bass("TRN2", target_bir_lowering=False, num_devices=NCORE)
    xT = nc.dram_tensor("xT", [E, T], F32, kind="ExternalInput").ap()
    wq = nc.dram_tensor("wq", [E, 512], F32, kind="ExternalInput").ap()
    wdn = nc.dram_tensor("wdn", [E, 512], F32, kind="ExternalInput").ap()
    wku = nc.dram_tensor("wku", [512, 512], F32, kind="ExternalInput").ap()
    wvu = nc.dram_tensor("wvu", [512, 512], F32, kind="ExternalInput").ap()
    ropeC = nc.dram_tensor("ropeC", [128, T], BF16, kind="ExternalInput").ap()
    ropeS = nc.dram_tensor("ropeS", [128, T], BF16, kind="ExternalInput").ap()
    swp = nc.dram_tensor("swp", [128, 128], BF16, kind="ExternalInput").ap()
    outT_d = nc.dram_tensor("outT", [512, T], BF16, kind="ExternalOutput").ap()

    TB = 256
    NTB = T // TB

    xbfr = [nc.alloc_sbuf_tensor(f"xbfr{c}", [128, TB], BF16) for c in range(8)]
    xnbf = [nc.alloc_sbuf_tensor(f"xnbf{c}", [128, TB], BF16) for c in range(8)]
    rowbuf = nc.alloc_sbuf_tensor("rowbuf", [1, 6 * TB], F32)
    srow = rowbuf[:, 0:TB]
    s2row = rowbuf[:, TB:2 * TB]
    vrow = rowbuf[:, 2 * TB:3 * TB]
    stdrow = rowbuf[:, 3 * TB:4 * TB]
    arow = rowbuf[:, 4 * TB:5 * TB]
    rowbuf_b = nc.alloc_sbuf_tensor("rowbuf_b", [32, 2 * TB], BF16)
    mrow_b = rowbuf_b[:, 0:TB]
    arow_b = rowbuf_b[:, TB:2 * TB]
    ab_sb = nc.alloc_sbuf_tensor("ab_sb", [128, TB], F32)
    onesrow_b = nc.alloc_sbuf_tensor("onesrow_b", [32, 128], BF16)
    onesrow_f = nc.alloc_sbuf_tensor("onesrow_f", [96, 64], F32)
    onescol_b = nc.alloc_sbuf_tensor("onescol_b", [128, 1], BF16)
    negc0 = nc.alloc_sbuf_tensor("negc0", [128, 1], F32)

    wq_sb = [nc.alloc_sbuf_tensor(f"wqs{c}", [128, 512], BF16) for c in range(8)]
    wdn_sb = [nc.alloc_sbuf_tensor(f"wdns{c}", [128, 512], BF16) for c in range(8)]
    wku_sb = [nc.alloc_sbuf_tensor(f"wkus{c}", [128, 512], BF16) for c in range(4)]
    wvu_sb = [nc.alloc_sbuf_tensor(f"wvus{c}", [128, 512], BF16) for c in range(4)]

    qT = [nc.alloc_sbuf_tensor(f"qT{i}", [128, T], BF16) for i in range(4)]
    kT = [nc.alloc_sbuf_tensor(f"kT{i}", [128, T], BF16) for i in range(4)]
    latT = [nc.alloc_sbuf_tensor(f"latT{i}", [128, T], BF16) for i in range(4)]
    vp = [nc.alloc_sbuf_tensor(f"vp{h}", [128, 32 * 65], BF16) for h in range(8)]
    swp_sb = nc.alloc_sbuf_tensor("swp_sb", [128, 128], BF16)
    rtmp0 = nc.alloc_sbuf_tensor("rtmp0", [128, 512], BF16)
    rtmp1 = nc.alloc_sbuf_tensor("rtmp1", [128, 512], BF16)
    pt = nc.alloc_sbuf_tensor("pt", [128, PTRING * 512], BF16)
    invsb = nc.alloc_sbuf_tensor("invsb", [96, 512], F32)
    invb_sb = nc.alloc_sbuf_tensor("invb_sb", [64, 512], F32)
    obuf = [nc.alloc_sbuf_tensor(f"obuf{i}", [64, 512], BF16) for i in range(2)]
    pt_addr = nc.lookup_mloc(pt).addr
    cCb = nc.alloc_sbuf_tensor_at("cCb", [128, T], BF16, offset=pt_addr)
    cSb = nc.alloc_sbuf_tensor_at("cSb", [128, T], BF16, offset=pt_addr + 8192)

    ps_a = nc.alloc_psum_tensor("ps_a", [128, 512], F32)
    ps_b = nc.alloc_psum_tensor("ps_b", [128, 512], F32)
    ps_stat = nc.alloc_psum_tensor("ps_stat", [1, 512], F32)
    ps_bc = nc.alloc_psum_tensor("ps_bc", [128, TB], F32)
    ps_s0 = nc.alloc_psum_tensor("ps_s0", [128, 512], F32)
    ps_s1 = nc.alloc_psum_tensor("ps_s1", [128, 512], F32)
    ps_o = nc.alloc_psum_tensor("ps_o", [128, 512], F32)
    ps_b2 = nc.alloc_psum_tensor("ps_b2", [64, 512], F32)

    S = {n: nc.alloc_semaphore(n) for n in
         ["lw", "gi", "xl", "sq", "st", "rm1", "sqd", "rowm", "bcm",
          "xnd", "fx", "bca", "aok", "qkm", "pj", "kvm", "kvev", "sw", "rpa",
          "rp", "qk", "ex", "pvc", "pv", "nmr", "nmc", "nm", "bcv", "so"]}
    NW = 27

    with nc.Block() as block:

        @block.sync
        def _(sync):
            for h in range(8):
                for qb in range(8):
                    idx = h * 8 + qb
                    sync.wait_ge(S["nm"], idx + 1)
                    sync.dma_start(out=outT_d[h * 64:(h + 1) * 64, qb * 512:(qb + 1) * 512],
                                   in_=obuf[idx % 2][:]).then_inc(S["so"], 16)
            sync.wait_ge(S["so"], 16 * 64)

        @block.gpsimd
        def _(g):
            for c in range(8):
                g.dma_start(out=wq_sb[c][:], in_=wq[c * 128:(c + 1) * 128, :]).then_inc(S["lw"], 16)
            for c in range(8):
                g.dma_start(out=wdn_sb[c][:], in_=wdn[c * 128:(c + 1) * 128, :]).then_inc(S["lw"], 16)
            for c in range(4):
                g.dma_start(out=wku_sb[c][:], in_=wku[c * 128:(c + 1) * 128, :]).then_inc(S["lw"], 16)
            for c in range(4):
                g.dma_start(out=wvu_sb[c][:], in_=wvu[c * 128:(c + 1) * 128, :]).then_inc(S["lw"], 16)
            g.dma_start(out=cCb[:], in_=ropeC[:]).then_inc(S["lw"], 16)
            g.dma_start(out=cSb[:], in_=ropeS[:]).then_inc(S["lw"], 16)
            g.dma_start(out=swp_sb[:], in_=swp[:]).then_inc(S["lw"], 16)
            g.memset(onesrow_b[:], 0.0)
            g.memset(onesrow_b[0:1, :], 1.0).then_inc(S["gi"], 1)
            g.memset(onesrow_f[64:96, :], 0.0)
            g.memset(onesrow_f[64:65, :], 1.0).then_inc(S["gi"], 1)
            g.memset(onescol_b[:], 1.0).then_inc(S["gi"], 1)
            g.memset(negc0[:], -C0).then_inc(S["gi"], 1)
            g.memset(rowbuf_b[:], 0.0).then_inc(S["gi"], 1)
            g.memset(invsb[64:96, :], 0.0).then_inc(S["gi"], 1)
            for h in range(8):
                g.memset(vp[h][:].rearrange("p (a b) -> p a b", b=65)[:, :, 64:65], 1.0
                         ).then_inc(S["gi"], 1)
            for tb in range(NTB):
                if tb > 0:
                    g.wait_ge(S["xnd"], 8 * tb)
                for c in range(8):
                    g.dma_start(out=xbfr[c][:],
                                in_=xT[c * 128:(c + 1) * 128, tb * TB:(tb + 1) * TB]
                                ).then_inc(S["xl"], 16)

        @block.scalar
        def _(sc):
            for tb in range(NTB):
                sc.wait_ge(S["xl"], 128 * (tb + 1))
                if tb > 0:
                    sc.wait_ge(S["qkm"], 8 * tb)
                for c in range(8):
                    sc.activation(xnbf[c][:], xbfr[c][:], AF.Square).then_inc(S["sq"], 1)
                sc.wait_ge(S["rm1"], tb + 1)
                sc.activation(stdrow, s2row, AF.Sqrt).then_inc(S["sqd"], 1)
            sc.wait_ge(S["gi"], 14)
            qk_n = 0
            pvc_n = 0
            for h in range(8):
                for qb in range(8):
                    plan = attn_plan(qb)
                    for (kc, ks, qo, ql) in plan:
                        qk_n += 1
                        sc.wait_ge(S["qk"], qk_n)
                        if kc >= PTRING:
                            sc.wait_ge(S["pvc"], pvc_n + kc - PTRING + 1)
                        bank = ps_s0 if ((qk_n - 1) % 2 == 0) else ps_s1
                        slot = kc % PTRING
                        sc.activation(pt[:, slot * 512 + qo: slot * 512 + qo + ql],
                                      bank[:, qo:qo + ql], AF.Exp,
                                      bias=negc0[:], scale=0.125).then_inc(S["ex"], 1)
                    pvc_n += len(plan)

        @block.tensor
        def _(t):
            t.wait_ge(S["lw"], 16 * NW)
            t.wait_ge(S["gi"], 14)
            pj_n = 0
            for tb in range(NTB):
                t.wait_ge(S["sq"], 8 * tb + 8)
                if tb > 0:
                    t.wait_ge(S["rowm"], 3 * tb)
                for c in range(8):
                    t.matmul(ps_stat[:, 0:TB], onescol_b[:], xbfr[c][:],
                             start=(c == 0), stop=(c == 7))
                for c in range(8):
                    i = t.matmul(ps_stat[:, TB:2 * TB], onescol_b[:], xnbf[c][:],
                                 start=(c == 0), stop=(c == 7))
                i.then_inc(S["st"], 1)
                t.wait_ge(S["rowm"], 3 * tb + 2)
                t.matmul(ps_bc[:], onesrow_b[:, 0:128], mrow_b[0:32, :], start=True, stop=True
                         ).then_inc(S["bcm"], 1)
                t.wait_ge(S["xnd"], 8 * tb + 8)
                t.wait_ge(S["rowm"], 3 * tb + 3)
                t.matmul(ps_bc[:], onesrow_b[:, 0:128], arow_b[0:32, :], start=True, stop=True
                         ).then_inc(S["bca"], 1)
                t.wait_ge(S["aok"], tb + 1)
                for j in range(8):
                    w_sb = wq_sb if j < 4 else wdn_sb
                    e = j % 4
                    bank = ps_a if pj_n % 2 == 0 else ps_b
                    if pj_n >= 2:
                        t.wait_ge(S["pj"], pj_n - 1)
                    for c in range(8):
                        i = t.matmul(bank[:, 0:TB], w_sb[c][:, e * 128:(e + 1) * 128],
                                     xnbf[c][:], start=(c == 0), stop=(c == 7))
                    i.then_inc(S["qkm"], 1)
                    pj_n += 1
            kv_n = 0
            for g in range(64):
                bank = ps_a if kv_n % 2 == 0 else ps_b
                if kv_n < 2:
                    t.wait_ge(S["pj"], 8 * NTB)
                else:
                    t.wait_ge(S["kvev"], kv_n - 1)
                if g < 32:
                    e, tbb = g // 8, g % 8
                    for c in range(4):
                        i = t.matmul(bank[:], wku_sb[c][:, e * 128:(e + 1) * 128],
                                     latT[c][:, tbb * 512:(tbb + 1) * 512],
                                     start=(c == 0), stop=(c == 3))
                else:
                    tc = g - 32
                    for c in range(4):
                        i = t.matmul(bank[:], latT[c][:, tc * 128:(tc + 1) * 128],
                                     wvu_sb[c][:], start=(c == 0), stop=(c == 3))
                i.then_inc(S["kvm"], 1)
                kv_n += 1
            sw_n = 0
            for r in range(64):
                src = qT if r < 32 else kT
                i4, blk = (r % 32) // 8, r % 8
                bank = ps_a if sw_n % 2 == 0 else ps_b
                if sw_n < 2:
                    t.wait_ge(S["kvev"], 64)
                else:
                    t.wait_ge(S["rpa"], sw_n - 1)
                t.matmul(bank[:], swp_sb[:], src[i4][:, blk * 512:(blk + 1) * 512],
                         start=True, stop=True).then_inc(S["sw"], 1)
                sw_n += 1
            t.wait_ge(S["rp"], 64)
            qk_n = 0
            ex_n = 0
            for h in range(8):
                i4, ro = h // 2, (h % 2) * 64
                for qb in range(8):
                    idx = h * 8 + qb
                    plan = attn_plan(qb)
                    npl = len(plan)
                    if idx > 0:
                        t.wait_ge(S["nm"], idx)
                    for pi, (kc, ks, qo, ql) in enumerate(plan):
                        bank = ps_s0 if (qk_n % 2 == 0) else ps_s1
                        if qk_n >= 2:
                            t.wait_ge(S["ex"], qk_n - 1)
                        t.matmul(bank[:, qo:qo + ql],
                                 kT[i4][ro:ro + 64, ks:ks + 128],
                                 qT[i4][ro:ro + 64, qb * 512 + qo: qb * 512 + qo + ql],
                                 start=True, stop=True).then_inc(S["qk"], 1)
                        qk_n += 1
                        if pi >= 1:
                            pkc, pks, pqo, pql = plan[pi - 1]
                            t.wait_ge(S["ex"], ex_n + pi)
                            slot = pkc % PTRING
                            i = t.matmul(ps_o[0:65, pqo:pqo + pql],
                                         vp[h][:, pkc * 65:(pkc + 1) * 65],
                                         pt[:, slot * 512 + pqo: slot * 512 + pqo + pql],
                                         start=(pi - 1 == 0), stop=False)
                            i.then_inc(S["pvc"], 1)
                    pkc, pks, pqo, pql = plan[npl - 1]
                    t.wait_ge(S["ex"], ex_n + npl)
                    slot = pkc % PTRING
                    i = t.matmul(ps_o[0:65, pqo:pqo + pql],
                                 vp[h][:, pkc * 65:(pkc + 1) * 65],
                                 pt[:, slot * 512 + pqo: slot * 512 + pqo + pql],
                                 start=False, stop=True)
                    i.then_inc(S["pvc"], 1)
                    ex_n += npl
                    t.wait_ge(S["nmr"], idx + 1)
                    if idx > 0:
                        t.wait_ge(S["nmc"], idx)
                    t.matmul(ps_b2[:], onesrow_f[64:96, :], invsb[64:96, :],
                             start=True, stop=True).then_inc(S["bcv"], 1)

        @block.vector
        def _(v):
            for tb in range(NTB):
                v.wait_ge(S["st"], tb + 1)
                v.tensor_scalar_mul(srow, ps_stat[:, 0:TB], 1.0 / 1024)
                v.tensor_scalar_mul(s2row, ps_stat[:, TB:2 * TB], 1.0 / 1024).then_inc(S["rowm"], 1)
                v.wait_ge(S["rowm"], 3 * tb + 1)
                v.tensor_mul(vrow, srow, srow)
                v.tensor_sub(s2row, s2row, vrow)
                v.tensor_scalar_add(s2row, s2row, EPS).then_inc(S["rm1"], 1)
                v.wait_ge(S["sqd"], tb + 1)
                v.reciprocal(arow, stdrow)
                v.tensor_copy(mrow_b[0:1, :], srow).then_inc(S["rowm"], 1)
                v.tensor_copy(arow_b[0:1, :], arow).then_inc(S["rowm"], 1)
                v.wait_ge(S["bcm"], tb + 1)
                for c in range(8):
                    v.tensor_sub(xnbf[c][:], xbfr[c][:], ps_bc[:]).then_inc(S["xnd"], 1)
                v.wait_ge(S["bca"], tb + 1)
                v.tensor_copy(ab_sb[:], ps_bc[:]).then_inc(S["aok"], 1)
                for j in range(8):
                    v.wait_ge(S["qkm"], 8 * tb + j + 1)
                    bank = ps_a if (8 * tb + j) % 2 == 0 else ps_b
                    dst = qT if j < 4 else latT
                    e = j % 4
                    v.tensor_mul(dst[e][:, tb * TB:(tb + 1) * TB], bank[:, 0:TB], ab_sb[:]
                                 ).then_inc(S["pj"], 1)
            for g in range(64):
                v.wait_ge(S["kvm"], g + 1)
                bank = ps_a if g % 2 == 0 else ps_b
                if g < 32:
                    e, tbb = g // 8, g % 8
                    i = v.tensor_copy(kT[e][:, tbb * 512:(tbb + 1) * 512], bank[:])
                else:
                    tc = g - 32
                    for h in range(8):
                        i = v.tensor_copy(vp[h][:, tc * 65: tc * 65 + 64],
                                          bank[:, h * 64:(h + 1) * 64])
                i.then_inc(S["kvev"], 1)
            for r in range(64):
                src = qT if r < 32 else kT
                i4, blk = (r % 32) // 8, r % 8
                bank = ps_a if r % 2 == 0 else ps_b
                v.wait_ge(S["sw"], r + 1)
                v.tensor_copy(rtmp0[:], bank[:]).then_inc(S["rpa"], 1)
                v.tensor_mul(rtmp1[:], src[i4][:, blk * 512:(blk + 1) * 512],
                             cCb[:, blk * 512:(blk + 1) * 512])
                v.wait_ge(S["rpa"], r + 1)
                v.tensor_mul(rtmp0[:], rtmp0[:], cSb[:, blk * 512:(blk + 1) * 512])
                v.tensor_add(src[i4][:, blk * 512:(blk + 1) * 512], rtmp0[:], rtmp1[:]
                             ).then_inc(S["rp"], 1)
            cum = 0
            for h in range(8):
                for qb in range(8):
                    idx = h * 8 + qb
                    cum += len(attn_plan(qb))
                    v.wait_ge(S["pvc"], cum)
                    v.reciprocal(invsb[64:65, :], ps_o[64:65, :]).then_inc(S["nmr"], 1)
                    v.wait_ge(S["bcv"], idx + 1)
                    v.tensor_copy(invb_sb[:], ps_b2[:]).then_inc(S["nmc"], 1)
                    v.wait_ge(S["nmc"], idx + 1)
                    if idx >= 2:
                        v.wait_ge(S["so"], 16 * (idx - 1))
                    v.tensor_mul(obuf[idx % 2][:], ps_o[0:64, :], invb_sb[:]
                                 ).then_inc(S["nm"], 1)

    print("MLA sbuf remaining:", nc.sbuf_bytes_remaining)
    return nc



def _mla_rope_tables():
    inv = 1.0 / 10000.0 ** (np.arange(0, 64, 2) / 64.0)
    ang = np.outer(inv, np.arange(T, dtype=np.float64))
    c, s = np.cos(ang), np.sin(ang)
    C = np.empty((64, T), np.float32)
    Sg = np.empty((64, T), np.float32)
    C[0::2] = c
    C[1::2] = c
    Sg[0::2] = -s
    Sg[1::2] = s
    return (np.concatenate([C, C], 0).astype(BF),
            np.concatenate([Sg, Sg], 0).astype(BF))


def _mla_swap():
    m = np.zeros((128, 128), np.float32)
    for i in range(0, 128, 2):
        m[i + 1, i] = 1.0
        m[i, i + 1] = 1.0
    return m.astype(BF)


_CACHE = {}


def _get(name, builder):
    if name not in _CACHE:
        _CACHE[name] = builder()
    return _CACHE[name]


def run_mla(inputs, xT_percore):
    wq = inputs["L0_attn_wq"]
    wdn = inputs["L0_attn_w_down"]
    wku = inputs["L0_attn_wk_up"]
    wvu = inputs["L0_attn_wv_up"]
    C2, S2 = _get("rope", _mla_rope_tables)
    swp = _get("swap", _mla_swap)
    nc = _get("mla", build_mla)
    in_maps = []
    for c in range(NCORE):
        b, p = c // 2, c % 2
        hs = slice(p * 512, (p + 1) * 512)
        in_maps.append({
            "xT": xT_percore[c],
            "wq": np.ascontiguousarray(wq[:, hs]),
            "wdn": np.ascontiguousarray(wdn),
            "wku": np.ascontiguousarray(wku[:, hs]),
            "wvu": np.ascontiguousarray(wvu[:, hs]),
            "ropeC": C2, "ropeS": S2, "swp": swp,
        })
    res = run_bass_kernel_spmd(nc, in_maps, core_ids=list(range(NCORE)))
    return [np.asarray(res.results[c]["outT"]) for c in range(NCORE)]




def build_mlafin():
    nc = bass.Bass("TRN2", target_bir_lowering=False, num_devices=NCORE)
    TH = T // 2
    xT = nc.dram_tensor("xT", [E, TH], F32, kind="ExternalInput").ap()
    moT = nc.dram_tensor("moT", [E, TH], BF16, kind="ExternalInput").ap()
    wo = nc.dram_tensor("wo", [E, E], F32, kind="ExternalInput").ap()
    bo_d = nc.dram_tensor("bo", [128, 8], F32, kind="ExternalInput").ap()
    x1T_d = nc.dram_tensor("x1T", [E, TH], F32, kind="ExternalOutput").ap()
    r0T_d = nc.dram_tensor("r0T", [E, TH], F32, kind="ExternalOutput").ap()

    TB = 512
    NTB = TH // TB

    mo_sb = [nc.alloc_sbuf_tensor(f"mo{c}", [128, TH], BF16) for c in range(8)]
    wo_sb = [nc.alloc_sbuf_tensor(f"wos{c}", [128, E], BF16) for c in range(8)]
    xf = [nc.alloc_sbuf_tensor(f"xf{c}", [128, TB], F32) for c in range(8)]
    yb = [nc.alloc_sbuf_tensor(f"yb{c}", [128, TB], BF16) for c in range(8)]
    ysq = [nc.alloc_sbuf_tensor(f"ysq{c}", [128, TB], BF16) for c in range(8)]
    x1f = [nc.alloc_sbuf_tensor(f"x1f{c}", [128, TB], F32) for c in range(8)]
    x1b = [nc.alloc_sbuf_tensor(f"x1b{c}", [128, TB], BF16) for c in range(8)]
    tmpf = [nc.alloc_sbuf_tensor(f"tmpf{c}", [128, TB], F32) for c in range(8)]
    rowbuf = nc.alloc_sbuf_tensor("rowbuf", [1, 6 * TB], F32)
    srow = rowbuf[:, 0:TB]
    s2row = rowbuf[:, TB:2 * TB]
    vrow = rowbuf[:, 2 * TB:3 * TB]
    stdrow = rowbuf[:, 3 * TB:4 * TB]
    arow = rowbuf[:, 4 * TB:5 * TB]
    rowbuf_b = nc.alloc_sbuf_tensor("rowbuf_b", [32, 2 * TB], BF16)
    mrow_b = rowbuf_b[:, 0:TB]
    arow_b = rowbuf_b[:, TB:2 * TB]
    ab_sb = nc.alloc_sbuf_tensor("ab_sb", [128, TB], F32)
    mb_sb = nc.alloc_sbuf_tensor("mb_sb", [128, TB], F32)
    onesrow_b = nc.alloc_sbuf_tensor("onesrow_b", [32, 128], BF16)
    onescol_b = nc.alloc_sbuf_tensor("onescol_b", [128, 1], BF16)
    bo_sb = nc.alloc_sbuf_tensor("bo_sb", [128, 8], F32)

    ps_a = nc.alloc_psum_tensor("ps_a", [128, 512], F32)
    ps_b = nc.alloc_psum_tensor("ps_b", [128, 512], F32)
    ps_stat = nc.alloc_psum_tensor("ps_stat", [1, 2 * TB], F32)
    ps_bc = nc.alloc_psum_tensor("ps_bc", [128, TB], F32)
    ps_bc2 = nc.alloc_psum_tensor("ps_bc2", [128, TB], F32)

    S = {n: nc.alloc_semaphore(n) for n in
         ["lw", "gi", "xl", "mv", "ye", "sq", "st", "rm1", "sqd", "rowm", "bc",
          "x1d", "x1s", "sq2", "st2", "rm12", "sqd2", "rowm2", "bc2", "r0d",
          "so", "fx"]}

    with nc.Block() as block:

        @block.gpsimd
        def _(g):
            for c in range(8):
                g.dma_start(out=wo_sb[c][:], in_=wo[c * 128:(c + 1) * 128, :]).then_inc(S["lw"], 16)
            for c in range(8):
                g.dma_start(out=mo_sb[c][:], in_=moT[c * 128:(c + 1) * 128, :]).then_inc(S["lw"], 16)
            g.dma_start(out=bo_sb[:], in_=bo_d[:]).then_inc(S["lw"], 16)
            g.memset(onesrow_b[:], 0.0)
            g.memset(onesrow_b[0:1, :], 1.0).then_inc(S["gi"], 1)
            g.memset(onescol_b[:], 1.0).then_inc(S["gi"], 1)
            g.memset(rowbuf_b[:], 0.0).then_inc(S["gi"], 1)

        @block.sync
        def _(sync):
            for tb in range(NTB):
                if tb > 0:
                    sync.wait_ge(S["x1d"], 8 * tb)
                for c in range(8):
                    sync.dma_start(out=xf[c][:],
                                   in_=xT[c * 128:(c + 1) * 128, tb * TB:(tb + 1) * TB]
                                   ).then_inc(S["xl"], 16)
                sync.wait_ge(S["x1d"], 8 * (tb + 1))
                for c in range(8):
                    sync.dma_start(out=x1T_d[c * 128:(c + 1) * 128, tb * TB:(tb + 1) * TB],
                                   in_=x1f[c][:]).then_inc(S["so"], 16)
                sync.wait_ge(S["r0d"], 8 * (tb + 1))
                for c in range(8):
                    sync.dma_start(out=r0T_d[c * 128:(c + 1) * 128, tb * TB:(tb + 1) * TB],
                                   in_=tmpf[c][:]).then_inc(S["so"], 16)
            sync.wait_ge(S["so"], 16 * 16 * NTB)

        @block.tensor
        def _(t):
            t.wait_ge(S["lw"], 16 * 17)
            t.wait_ge(S["gi"], 3)
            pj = 0
            for tb in range(NTB):
                for e in range(8):
                    bank = ps_a if pj % 2 == 0 else ps_b
                    if pj >= 2:
                        t.wait_ge(S["ye"], pj - 1)
                    for c in range(8):
                        i = t.matmul(bank[:], wo_sb[c][:, e * 128:(e + 1) * 128],
                                     mo_sb[c][:, tb * TB:(tb + 1) * TB],
                                     start=(c == 0), stop=(c == 7))
                    i.then_inc(S["mv"], 1)
                    pj += 1
                t.wait_ge(S["sq"], 8 * (tb + 1))
                if tb > 0:
                    t.wait_ge(S["rm12"], tb)
                    t.wait_ge(S["rowm2"], 2 * tb)
                for c in range(8):
                    t.matmul(ps_stat[:, 0:TB], onescol_b[:], yb[c][:],
                             start=(c == 0), stop=(c == 7))
                for c in range(8):
                    i = t.matmul(ps_stat[:, TB:2 * TB], onescol_b[:], ysq[c][:],
                                 start=(c == 0), stop=(c == 7))
                i.then_inc(S["st"], 1)
                t.wait_ge(S["rowm"], 2 * tb + 1)
                t.matmul(ps_bc[:], onesrow_b[:], mrow_b[0:32, :], start=True, stop=True)
                i = t.matmul(ps_bc2[:], onesrow_b[:], arow_b[0:32, :], start=True, stop=True)
                i.then_inc(S["bc"], 1)
                t.wait_ge(S["sq2"], 8 * (tb + 1))
                t.wait_ge(S["rm1"], tb + 1)
                for c in range(8):
                    t.matmul(ps_stat[:, 0:TB], onescol_b[:], x1b[c][:],
                             start=(c == 0), stop=(c == 7))
                for c in range(8):
                    i = t.matmul(ps_stat[:, TB:2 * TB], onescol_b[:], ysq[c][:],
                                 start=(c == 0), stop=(c == 7))
                i.then_inc(S["st2"], 1)
                t.wait_ge(S["rowm"], 2 * tb + 2)
                t.wait_ge(S["rowm2"], 2 * tb + 1)
                t.matmul(ps_bc[:], onesrow_b[:], mrow_b[0:32, :], start=True, stop=True)
                i = t.matmul(ps_bc2[:], onesrow_b[:], arow_b[0:32, :], start=True, stop=True)
                i.then_inc(S["bc2"], 1)

        @block.scalar
        def _(sc):
            for tb in range(NTB):
                sc.wait_ge(S["ye"], 8 * tb + 8)
                for c in range(8):
                    sc.activation(ysq[c][:], yb[c][:], AF.Square).then_inc(S["sq"], 1)
                sc.wait_ge(S["rm1"], tb + 1)
                sc.activation(stdrow, s2row, AF.Sqrt).then_inc(S["sqd"], 1)
                sc.wait_ge(S["x1s"], 8 * (tb + 1))
                for c in range(8):
                    sc.activation(ysq[c][:], x1b[c][:], AF.Square).then_inc(S["sq2"], 1)
                sc.wait_ge(S["rm12"], tb + 1)
                sc.activation(stdrow, s2row, AF.Sqrt).then_inc(S["sqd2"], 1)

        @block.vector
        def _(v):
            for tb in range(NTB):
                if tb > 0:
                    v.wait_ge(S["so"], 16 * 16 * tb)
                for e in range(8):
                    v.wait_ge(S["mv"], 8 * tb + e + 1)
                    bank = ps_a if (8 * tb + e) % 2 == 0 else ps_b
                    v.tensor_scalar(yb[e][:], bank[:], bo_sb[:, e:e + 1], None, ALU.add
                                    ).then_inc(S["ye"], 1)
                v.wait_ge(S["st"], tb + 1)
                v.tensor_scalar_mul(srow, ps_stat[:, 0:TB], 1.0 / 1024)
                v.tensor_scalar_mul(s2row, ps_stat[:, TB:2 * TB], 1.0 / 1024)
                v.tensor_mul(vrow, srow, srow)
                v.tensor_sub(s2row, s2row, vrow)
                v.tensor_scalar_add(s2row, s2row, EPS).then_inc(S["rm1"], 1)
                v.wait_ge(S["sqd"], tb + 1)
                v.reciprocal(arow, stdrow)
                v.tensor_copy(mrow_b[0:1, :], srow)
                v.tensor_copy(arow_b[0:1, :], arow).then_inc(S["rowm"], 1)
                v.wait_ge(S["bc"], tb + 1)
                v.wait_ge(S["xl"], 128 * (tb + 1))
                v.tensor_copy(ab_sb[:], ps_bc2[:])
                v.tensor_copy(mb_sb[:], ps_bc[:]).then_inc(S["rowm"], 1)
                for c in range(8):
                    v.tensor_sub(tmpf[c][:], yb[c][:], mb_sb[:])
                for c in range(8):
                    v.tensor_mul(tmpf[c][:], tmpf[c][:], ab_sb[:])
                for c in range(8):
                    v.tensor_add(x1f[c][:], xf[c][:], tmpf[c][:]).then_inc(S["x1d"], 1)
                for c in range(8):
                    v.tensor_copy(x1b[c][:], x1f[c][:]).then_inc(S["x1s"], 1)
                v.wait_ge(S["st2"], tb + 1)
                v.tensor_scalar_mul(srow, ps_stat[:, 0:TB], 1.0 / 1024)
                v.tensor_scalar_mul(s2row, ps_stat[:, TB:2 * TB], 1.0 / 1024)
                v.tensor_mul(vrow, srow, srow)
                v.tensor_sub(s2row, s2row, vrow)
                v.tensor_scalar_add(s2row, s2row, EPS).then_inc(S["rm12"], 1)
                v.wait_ge(S["sqd2"], tb + 1)
                v.reciprocal(arow, stdrow)
                v.tensor_copy(mrow_b[0:1, :], srow)
                v.tensor_copy(arow_b[0:1, :], arow).then_inc(S["rowm2"], 1)
                v.wait_ge(S["bc2"], tb + 1)
                v.tensor_copy(ab_sb[:], ps_bc2[:])
                v.tensor_copy(mb_sb[:], ps_bc[:]).then_inc(S["rowm2"], 1)
                for c in range(8):
                    v.tensor_sub(tmpf[c][:], x1b[c][:], mb_sb[:])
                for c in range(8):
                    v.tensor_mul(tmpf[c][:], tmpf[c][:], ab_sb[:]).then_inc(S["r0d"], 1)

    print("MLAFIN sbuf remaining:", nc.sbuf_bytes_remaining)
    return nc


def run_addln(xT_half_percore, moT_half_percore, wo, bo=None):
    nc = _get("mlafin", build_mlafin)
    if bo is None:
        bocol = np.zeros((128, 8), np.float32)
    else:
        bocol = np.ascontiguousarray(np.asarray(bo, np.float32).reshape(8, 128).T)
    in_maps = []
    for c in range(NCORE):
        in_maps.append({
            "xT": xT_half_percore[c],
            "moT": moT_half_percore[c],
            "wo": np.ascontiguousarray(np.asarray(wo, np.float32)),
            "bo": bocol,
        })
    res = run_bass_kernel_spmd(nc, in_maps, core_ids=list(range(NCORE)))
    return ([np.asarray(res.results[c]["x1T"]) for c in range(NCORE)],
            [np.asarray(res.results[c]["r0T"]) for c in range(NCORE)])


def run_mlafin(inputs, xT_percore, mla_outs):
    TH = T // 2
    xh, mh = [], []
    for c in range(NCORE):
        b, p = c // 2, c % 2
        moT = np.concatenate([mla_outs[2 * b], mla_outs[2 * b + 1]], axis=0)
        xh.append(np.ascontiguousarray(xT_percore[c][:, p * TH:(p + 1) * TH]))
        mh.append(np.ascontiguousarray(moT[:, p * TH:(p + 1) * TH]))
    return run_addln(xh, mh, inputs["L0_attn_wo"])






IFFN_DEBUG = False


def build_iffn():
    nc = bass.Bass("TRN2", target_bir_lowering=False, num_devices=NCORE)
    TH = 2048
    rpT = nc.dram_tensor("rpT", [E, TH + 1], F32, kind="ExternalInput").ap()
    W1_d = nc.dram_tensor("W1p", [E, 5632], F32, kind="ExternalInput").ap()
    b1_d = nc.dram_tensor("b1bl", [128, 44], F32, kind="ExternalInput").ap()
    W2_d = nc.dram_tensor("W2p", [2816, E], F32, kind="ExternalInput").ap()
    b2_d = nc.dram_tensor("b2c", [128, 8], F32, kind="ExternalInput").ap()
    gc_d = nc.dram_tensor("gc", [128, 8], F32, kind="ExternalInput").ap()
    bb_d = nc.dram_tensor("bb", [128, 8], BF16, kind="ExternalInput").ap()
    grow_d = nc.dram_tensor("g_row", [32, E], BF16, kind="ExternalInput").ap()
    rnT_d = nc.dram_tensor("rnT", [E, TH], F32, kind="ExternalOutput").ap()
    if IFFN_DEBUG:
        dbg_d = nc.dram_tensor("dbg", [128, 1536], F32, kind="ExternalOutput").ap()

    TB = 512
    NTB = TH // TB

    x2b = [nc.alloc_sbuf_tensor(f"fx2{c}", [128, TH], BF16) for c in range(8)]
    xnb = x2b
    xsq = [nc.alloc_sbuf_tensor(f"fsq{c}", [128, TB], BF16) for c in range(8)]
    w1_sb = [nc.alloc_sbuf_tensor(f"fw1{c}", [128, 5632], BF16) for c in range(8)]
    w2buf = [nc.alloc_sbuf_tensor(f"fw2{i}", [128, E], BF16) for i in range(2)]
    pblk = [nc.alloc_sbuf_tensor(f"fp{j}", [128, TB], BF16) for j in range(22)]
    r1blk = [nc.alloc_sbuf_tensor(f"fr1{c}", [128, TB], F32) for c in range(8)]
    hA = nc.alloc_sbuf_tensor("fhA", [128, TB], F32)
    hG = nc.alloc_sbuf_tensor("fhG", [128, TB], F32)
    hG2 = nc.alloc_sbuf_tensor("fhG2", [128, TB], F32)
    rowbuf = nc.alloc_sbuf_tensor("frow", [1, 6 * TB], F32)
    srow = rowbuf[:, 0:TB]
    s2row = rowbuf[:, TB:2 * TB]
    vrow = rowbuf[:, 2 * TB:3 * TB]
    stdrow = rowbuf[:, 3 * TB:4 * TB]
    arow = rowbuf[:, 4 * TB:5 * TB]
    rowbuf_b = nc.alloc_sbuf_tensor("frow_b", [32, 2 * TB], BF16)
    mrow_b = rowbuf_b[:, 0:TB]
    arow_b = rowbuf_b[:, TB:2 * TB]
    gcol_sb = nc.alloc_sbuf_tensor("fgcol", [128, 8], F32)
    b2_sb = nc.alloc_sbuf_tensor("fb2", [128, 8], F32)
    bb_sb = nc.alloc_sbuf_tensor("fbb", [128, 8], BF16)
    grow_sb = nc.alloc_sbuf_tensor("fgrow", [32, E], BF16)
    b1_sb = nc.alloc_sbuf_tensor("fb1", [128, 44], F32)
    bW1_sb = nc.alloc_sbuf_tensor("fbW1", [128, 44], F32)
    btot_sb = nc.alloc_sbuf_tensor("fbtot", [128, 44], F32)
    ab_sb = nc.alloc_sbuf_tensor("fab", [128, TB], F32)
    dbg_sb = nc.alloc_sbuf_tensor("fdbg", [128, 1536], F32) if IFFN_DEBUG else None
    onescol_b = nc.alloc_sbuf_tensor("fones", [128, 1], BF16)
    onesrow_b = nc.alloc_sbuf_tensor("fonesr", [32, 128], BF16)

    B = [nc.alloc_psum_tensor(f"fB{i}", [128, 512], F32) for i in range(8)]

    S = {n: nc.alloc_semaphore("f_" + n) for n in
         ["lw", "gi", "xl", "sq", "st", "rm1", "sqd", "rw", "bcm", "xnd", "bca",
          "aok", "bwm", "bwv", "hm", "he", "gl", "pm", "w2l", "w2m", "ye", "r1l",
          "so", "w1l"]}

    with nc.Block() as block:

        @block.gpsimd
        def _(g):
            for c in range(8):
                off = 1 if c < 4 else 0
                g.dma_start(out=x2b[c][:], in_=rpT[c * 128:(c + 1) * 128, off:off + TH]
                            ).then_inc(S["xl"], 16)
            g.dma_start(out=gcol_sb[:], in_=gc_d[:]).then_inc(S["lw"], 16)
            g.dma_start(out=b2_sb[:], in_=b2_d[:]).then_inc(S["lw"], 16)
            g.dma_start(out=bb_sb[:], in_=bb_d[:]).then_inc(S["lw"], 16)
            g.dma_start(out=grow_sb[:], in_=grow_d[:]).then_inc(S["lw"], 16)
            g.dma_start(out=b1_sb[:], in_=b1_d[:]).then_inc(S["lw"], 16)
            g.memset(onescol_b[:], 1.0)
            if IFFN_DEBUG:
                g.memset(dbg_sb[:], 0.0)
            g.memset(onesrow_b[:], 0.0)
            g.memset(onesrow_b[0:1, :], 1.0)
            g.memset(rowbuf_b[:], 0.0).then_inc(S["gi"], 1)
            for c in range(8):
                g.dma_start(out=w1_sb[c][:], in_=W1_d[c * 128:(c + 1) * 128, :]).then_inc(S["w1l"], 16)
            for tb in range(NTB):
                if tb >= 1:
                    g.wait_ge(S["so"], 128 * tb)
                for c in range(8):
                    g.dma_start(out=r1blk[c][:],
                                in_=rpT[c * 128:(c + 1) * 128, 1 + tb * TB:1 + (tb + 1) * TB]
                                ).then_inc(S["r1l"], 16)
                for j in range(22):
                    gidx = tb * 22 + j
                    if gidx >= 2:
                        g.wait_ge(S["w2m"], gidx - 1)
                    g.dma_start(out=w2buf[gidx % 2][:],
                                in_=W2_d[j * 128:(j + 1) * 128, :]).then_inc(S["w2l"], 16)

        @block.sync
        def _(sync):
            for tb in range(NTB):
                sync.wait_ge(S["ye"], 8 * (tb + 1))
                for c in range(8):
                    sync.dma_start(out=rnT_d[c * 128:(c + 1) * 128, tb * TB:(tb + 1) * TB],
                                   in_=r1blk[c][:]).then_inc(S["so"], 16)
            if IFFN_DEBUG:
                sync.wait_ge(S["so"], 16 * 8 * NTB + 1)
                sync.dma_start(out=dbg_d, in_=dbg_sb[:]).then_inc(S["so"], 16)
                sync.wait_ge(S["so"], 16 * 8 * NTB + 17)
            else:
                sync.wait_ge(S["so"], 16 * 8 * NTB)

        @block.scalar
        def _(sc):
            sc.wait_ge(S["xl"], 16 * 8)
            for tb in range(NTB):
                if tb > 0:
                    sc.wait_ge(S["st"], tb)
                for c in range(8):
                    sc.activation(xsq[c][:], x2b[c][:, tb * TB:(tb + 1) * TB], AF.Square
                                  ).then_inc(S["sq"], 1)
                sc.wait_ge(S["rm1"], tb + 1)
                sc.activation(stdrow, s2row, AF.Sqrt).then_inc(S["sqd"], 1)
                for j in range(22):
                    gidx = tb * 22 + j
                    sc.wait_ge(S["he"], 2 * gidx + 2)
                    sc.activation(hG2[:], hG[:], AF.Gelu, bias=btot_sb[:, 2 * j + 1:2 * j + 2]
                                  ).then_inc(S["gl"], 1)

        @block.tensor
        def _(t):
            t.wait_ge(S["gi"], 1)
            t.wait_ge(S["lw"], 16 * 5)
            t.wait_ge(S["w1l"], 16 * 8)
            for j in range(44):
                for c in range(8):
                    i = t.matmul(B[4][:, j:j + 1], w1_sb[c][:, j * 128:(j + 1) * 128],
                                 bb_sb[:, c:c + 1], start=(c == 0), stop=(c == 7))
            i.then_inc(S["bwm"], 1)
            t.wait_ge(S["xl"], 16 * 8)
            hm_n = 0
            for tb in range(NTB):
                t.wait_ge(S["sq"], 8 * (tb + 1))
                if tb > 0:
                    t.wait_ge(S["rw"], 2 * tb)
                    t.wait_ge(S["ye"], 8 * tb)
                for c in range(8):
                    t.matmul(B[2][0:1, :], onescol_b[:], x2b[c][:, tb * TB:(tb + 1) * TB],
                             start=(c == 0), stop=(c == 7))
                for c in range(8):
                    i = t.matmul(B[3][0:1, :], onescol_b[:], xsq[c][:],
                                 start=(c == 0), stop=(c == 7))
                i.then_inc(S["st"], 1)
                t.wait_ge(S["rw"], 2 * tb + 2)
                if tb == 0:
                    t.wait_ge(S["bwv"], 1)
                else:
                    t.wait_ge(S["aok"], tb)
                for c in range(8):
                    t.wait_ge(S["xnd"], 8 * tb + c)
                    t.matmul(B[4][:, 0:TB], grow_sb[0:32, c * 128:(c + 1) * 128],
                             mrow_b[0:32, :], start=True, stop=True).then_inc(S["bcm"], 1)
                t.wait_ge(S["xnd"], 8 * (tb + 1))
                t.matmul(B[4][:, 0:TB], onesrow_b[:], arow_b[0:32, :], start=True, stop=True
                         ).then_inc(S["bca"], 1)
                t.wait_ge(S["aok"], tb + 1)
                for j in range(22):
                    gidx = tb * 22 + j
                    if gidx >= 1:
                        t.wait_ge(S["he"], 2 * gidx)
                    for c in range(8):
                        t.matmul(B[0][:], w1_sb[c][:, (2 * j) * 128:(2 * j + 1) * 128],
                                 xnb[c][:, tb * TB:(tb + 1) * TB], start=(c == 0), stop=(c == 7))
                    for c in range(8):
                        i = t.matmul(B[1][:], w1_sb[c][:, (2 * j + 1) * 128:(2 * j + 2) * 128],
                                     xnb[c][:, tb * TB:(tb + 1) * TB], start=(c == 0), stop=(c == 7))
                    i.then_inc(S["hm"], 1)
                t.wait_ge(S["pm"], 22 * (tb + 1))
                if tb > 0:
                    t.wait_ge(S["ye"], 8 * tb)
                for j in range(22):
                    gidx = tb * 22 + j
                    t.wait_ge(S["w2l"], 16 * (gidx + 1))
                    for e in range(8):
                        i = t.matmul(B[e][:], w2buf[gidx % 2][:, e * 128:(e + 1) * 128],
                                     pblk[j][:], start=(j == 0), stop=(j == 21))
                    i.then_inc(S["w2m"], 1)

        @block.vector
        def _(v):
            v.wait_ge(S["bwm"], 1)
            v.tensor_copy(bW1_sb[:], B[4][:, 0:44])
            v.wait_ge(S["lw"], 16 * 5)
            v.tensor_add(btot_sb[:], bW1_sb[:], b1_sb[:]).then_inc(S["bwv"], 1)
            for tb in range(NTB):
                v.wait_ge(S["st"], tb + 1)
                v.tensor_scalar_mul(srow, B[2][0:1, :], 1.0 / 1024)
                v.tensor_scalar_mul(s2row, B[3][0:1, :], 1.0 / 1024).then_inc(S["rw"], 1)
                v.wait_ge(S["rw"], 2 * tb + 1)
                v.tensor_mul(vrow, srow, srow)
                v.tensor_sub(s2row, s2row, vrow)
                v.tensor_scalar_add(s2row, s2row, EPS).then_inc(S["rm1"], 1)
                v.wait_ge(S["sqd"], tb + 1)
                v.reciprocal(arow, stdrow)
                v.tensor_copy(mrow_b[0:1, :], srow)
                v.tensor_copy(arow_b[0:1, :], arow).then_inc(S["rw"], 1)
                for c in range(8):
                    v.wait_ge(S["bcm"], 8 * tb + c + 1)
                    if IFFN_DEBUG and tb == 3 and c == 0:
                        v.tensor_copy(dbg_sb[:, 0:512], B[4][:, 0:TB])
                        v.tensor_copy(dbg_sb[0:1, 512:1024], mrow_b[0:1, :])
                        v.tensor_copy(dbg_sb[0:1, 1024:1536], srow).then_inc(S["so"], 1)
                    v.scalar_tensor_tensor(xnb[c][:, tb * TB:(tb + 1) * TB],
                                           x2b[c][:, tb * TB:(tb + 1) * TB],
                                           gcol_sb[:, c:c + 1], B[4][:, 0:TB],
                                           ALU.mult, ALU.subtract).then_inc(S["xnd"], 1)
                v.wait_ge(S["bca"], tb + 1)
                v.tensor_copy(ab_sb[:], B[4][:, 0:TB]).then_inc(S["aok"], 1)
                if tb > 0:
                    v.wait_ge(S["w2m"], 22 * tb)
                for j in range(22):
                    gidx = tb * 22 + j
                    v.wait_ge(S["hm"], gidx + 1)
                    v.tensor_mul(hA[:], B[0][:], ab_sb[:]).then_inc(S["he"], 1)
                    v.tensor_mul(hG[:], B[1][:], ab_sb[:]).then_inc(S["he"], 1)
                    v.wait_ge(S["gl"], gidx + 1)
                    v.scalar_tensor_tensor(pblk[j][:], hA[:], btot_sb[:, 2 * j:2 * j + 1],
                                           hG2[:], ALU.add, ALU.mult).then_inc(S["pm"], 1)
                v.wait_ge(S["w2m"], 22 * (tb + 1))
                v.wait_ge(S["r1l"], 16 * 8 * (tb + 1))
                for e in range(8):
                    v.scalar_tensor_tensor(r1blk[e][:], B[e][:], b2_sb[:, e:e + 1],
                                           r1blk[e][:], ALU.add, ALU.add).then_inc(S["ye"], 1)

    print("IFFN sbuf remaining:", nc.sbuf_bytes_remaining)
    return nc


def _iffn_host(inputs, layer):
    key = ("iffn_host", layer)
    if key in _CACHE:
        return _CACHE[key]
    W1 = np.asarray(inputs["L0_ffn_W1"][layer])
    b1 = np.asarray(inputs["L0_ffn_b1"][layer])
    W2 = np.asarray(inputs["L0_ffn_W2"][layer])
    b2 = np.asarray(inputs["L0_ffn_b2"][layer])
    g = np.asarray(inputs["L0_ffn_ln_g"][layer])
    bl = np.asarray(inputs["L0_ffn_ln_b"][layer])
    DI = 2730
    W1p = np.zeros((E, 5632), np.float32)
    b1bl = np.zeros((128, 44), np.float32)
    for j in range(22):
        lo, hi = j * 128, min((j + 1) * 128, DI)
        n = hi - lo
        if n > 0:
            W1p[:, (2 * j) * 128:(2 * j) * 128 + n] = W1[:, lo:hi]
            W1p[:, (2 * j + 1) * 128:(2 * j + 1) * 128 + n] = W1[:, DI + lo:DI + hi]
            b1bl[0:n, 2 * j] = b1[lo:hi]
            b1bl[0:n, 2 * j + 1] = b1[DI + lo:DI + hi]
    W2p = np.zeros((2816, E), np.float32)
    W2p[0:DI, :] = W2
    r = {
        "W1p": W1p, "b1bl": b1bl, "W2p": W2p,
        "b2c": np.ascontiguousarray(b2.reshape(8, 128).T).astype(np.float32),
        "gc": np.ascontiguousarray(g.reshape(8, 128).T).astype(np.float32),
        "bb": np.ascontiguousarray(bl.reshape(8, 128).T).astype(BF),
        "g_row": np.ascontiguousarray(
            np.concatenate([g[None, :], np.zeros((31, E))], 0)).astype(BF),
    }
    _CACHE[key] = r
    return r


def run_iffn(inputs, layer, rp_full):
    nc = _get("iffn", build_iffn)
    host = _iffn_host(inputs, layer)
    TH = 2048
    in_maps = []
    for c in range(NCORE):
        b, p = c // 2, c % 2
        rpT = np.zeros((E, TH + 1), np.float32)
        rpT[:, 1:] = rp_full[b][:, p * TH:(p + 1) * TH]
        if p == 1:
            rpT[:, 0] = rp_full[b][:, TH - 1]
        m = {"rpT": rpT}
        m.update(host)
        in_maps.append(m)
    res = run_bass_kernel_spmd(nc, in_maps, core_ids=list(range(NCORE)))
    out = np.zeros((4, E, T), np.float32)
    for c in range(NCORE):
        b, p = c // 2, c % 2
        out[b][:, p * TH:(p + 1) * TH] = np.asarray(res.results[c]["rnT"])
    return out




RGA = 2304
NCH = RGA // 128
TBR = 384
NBR = RGA // TBR


IATTN_DEBUG = False


def build_iattn():
    nc = bass.Bass("TRN2", target_bir_lowering=False, num_devices=NCORE)
    rT = nc.dram_tensor("rT", [E, RGA + 1], F32, kind="ExternalInput").ap()
    Wqkv_d = nc.dram_tensor("Wqkv", [E, 3072], F32, kind="ExternalInput").ap()
    CS_d = [nc.dram_tensor(f"CS{i}", [128, RGA], BF16, kind="ExternalInput").ap() for i in range(4)]
    qs_d = nc.dram_tensor("qs_c", [128, 1], F32, kind="ExternalInput").ap()
    ks_d = nc.dram_tensor("ks_c", [128, 1], F32, kind="ExternalInput").ap()
    swp_d = nc.dram_tensor("swp2", [128, 128], BF16, kind="ExternalInput").ap()
    msk_d = nc.dram_tensor("msk", [4 * 128, 128], BF16, kind="ExternalInput").ap()
    blk2r_d = nc.dram_tensor("blk2r", [32, 128], BF16, kind="ExternalInput").ap()
    Wo_d = nc.dram_tensor("Wo", [E, E], F32, kind="ExternalInput").ap()
    rnT_d = nc.dram_tensor("rnT", [E, 2048], F32, kind="ExternalOutput").ap()
    if IATTN_DEBUG:
        dbgo_d = nc.dram_tensor("dbgo", [E, 2048], BF16, kind="ExternalOutput").ap()

    wkv_buf = [nc.alloc_sbuf_tensor(f"awkv{i}", [128, 8 * 512], BF16) for i in range(2)]
    qf = [nc.alloc_sbuf_tensor(f"aqf{i}", [128, RGA], BF16) for i in range(8)]
    kf = [nc.alloc_sbuf_tensor(f"akf{i}", [128, RGA], BF16) for i in range(8)]
    vp = [nc.alloc_sbuf_tensor(f"avp{h}", [128, NCH * 65], BF16) for h in range(16)]
    sqs = nc.alloc_sbuf_tensor("asqs", [128, RGA], BF16)
    rowq = nc.alloc_sbuf_tensor("arowq", [2, TBR], F32)
    stdq = nc.alloc_sbuf_tensor("astdq", [2, TBR], F32)
    rstdb = nc.alloc_sbuf_tensor("arstd", [32, TBR], BF16)
    qs_sb = nc.alloc_sbuf_tensor("aqs", [128, 1], F32)
    ks_sb = nc.alloc_sbuf_tensor("aks", [128, 1], F32)
    swp_sb = nc.alloc_sbuf_tensor("aswp", [128, 128], BF16)
    ident = nc.alloc_sbuf_tensor("aident", [128, 128], BF16)
    blk2 = nc.alloc_sbuf_tensor("ablk2", [128, 2], BF16)
    blk2r = nc.alloc_sbuf_tensor("ablk2r", [32, 128], BF16)
    msk_sb = [nc.alloc_sbuf_tensor(f"amsk{i}", [128, 128], BF16) for i in range(4)]
    rt0 = nc.alloc_sbuf_tensor("art0", [128, TBR], BF16)
    rt1 = nc.alloc_sbuf_tensor("art1", [128, TBR], BF16)
    mcol = nc.alloc_sbuf_tensor("amcol", [128, 1], F32)
    negm8 = nc.alloc_sbuf_tensor("anegm", [128, 1], F32)
    invsb = nc.alloc_sbuf_tensor("ainvsb", [96, 128], F32)
    onesf = nc.alloc_sbuf_tensor("aonesf", [96, 64], F32)
    invb = nc.alloc_sbuf_tensor("ainvb", [64, 128], F32)
    Pe = nc.alloc_sbuf_tensor("aPe", [128, 384], BF16)
    PT = nc.alloc_sbuf_tensor("aPT", [128, 384], BF16)
    rblk = [nc.alloc_sbuf_tensor(f"arb{c}", [128, 512], F32) for c in range(8)]

    rsh = [nc.alloc_sbuf_tensor(f"arsh{c}", [128, RGA], BF16) for c in range(8)]
    rsh_addr = nc.lookup_mloc(rsh[0]).addr
    t_cs = [nc.alloc_sbuf_tensor_at(f"acs{i}", [128, RGA], BF16, offset=rsh_addr + 16384 + i * 4608)
            for i in range(4)]
    wo_sb = [nc.alloc_sbuf_tensor_at(f"awo{c}", [128, E], BF16, offset=rsh_addr + c * 2048)
             for c in range(8)]
    wkv_addr = nc.lookup_mloc(wkv_buf[0]).addr
    oT = [nc.alloc_sbuf_tensor_at(f"aoT{i}", [128, 2048], BF16, offset=wkv_addr + i * 4096)
          for i in range(4)]
    oT += [nc.alloc_sbuf_tensor(f"aoU{i}", [128, 2048], BF16) for i in range(4)]

    ps_a = nc.alloc_psum_tensor("aps_a", [128, 512], F32)
    ps_b = nc.alloc_psum_tensor("aps_b", [128, 512], F32)
    ps_s0 = nc.alloc_psum_tensor("aps_s0", [128, 384], F32)
    ps_s1 = nc.alloc_psum_tensor("aps_s1", [128, 384], F32)
    ps_tr = nc.alloc_psum_tensor("aps_tr", [128, 384], BF16)
    ps_oT = nc.alloc_psum_tensor("aps_oT", [65, 128], F32)
    ps_nx = nc.alloc_psum_tensor("aps_nx", [64, TBR], F32)
    ps_n = ps_nx[0:2, 0:TBR]
    ps_n2 = ps_nx[0:64, 0:128]
    ps_bc = nc.alloc_psum_tensor("aps_bc", [128, TBR], F32)

    S = {n: nc.alloc_semaphore("a_" + n) for n in
         ["lw", "gi", "rl", "qm", "qe", "sq", "nm", "nre", "nv1", "nsq", "ntl",
          "nb", "qn", "sw", "swe", "rp", "sS", "mx", "mr", "ex", "iv", "pn", "tr",
          "te", "pv", "oe", "wom", "woe", "rbl", "rpt", "wol", "so", "gch", "vch",
          "bcv"]}
    gch = [0]
    vch = [0]

    NQK = 4 * 4 * NBR
    NV = NCH * 2
    NTL = 16
    NSUB = 16 * 8 * 2

    def q_tile_done_qe(qk, i8):
        return qk * 48 + (i8 // 4) * 24 + (i8 % 4) * NBR + NBR

    with nc.Block() as block:

        @block.gpsimd
        def _(g):
            for c in range(8):
                off = 1 if c < 4 else 0
                g.dma_start(out=rsh[c][:], in_=rT[c * 128:(c + 1) * 128, off:off + RGA]
                            ).then_inc(S["rl"], 16)
            g.dma_start(out=qs_sb[:], in_=qs_d[:]).then_inc(S["lw"], 16)
            g.dma_start(out=ks_sb[:], in_=ks_d[:]).then_inc(S["lw"], 16)
            g.dma_start(out=swp_sb[:], in_=swp_d[:]).then_inc(S["lw"], 16)
            for i in range(4):
                g.dma_start(out=msk_sb[i][:], in_=msk_d[i * 128:(i + 1) * 128, :]).then_inc(S["lw"], 16)
            g.memset(ident[:], 0.0).then_inc(S["gch"], 1)
            gch[0] += 1
            g.wait_ge(S["gch"], gch[0])
            g.affine_select(out=ident[:], in_=ident[:], compare_op=ALU.not_equal,
                            fill=1.0, base=0, pattern=[[-1, 128]], channel_multiplier=1
                            ).then_inc(S["gi"], 1)
            g.memset(blk2[:], 0.0).then_inc(S["gch"], 1)
            gch[0] += 1
            g.wait_ge(S["gch"], gch[0])
            g.memset(blk2[0:64, 0:1], 1.0).then_inc(S["gch"], 1)
            gch[0] += 1
            g.wait_ge(S["gch"], gch[0])
            g.memset(blk2[64:128, 1:2], 1.0).then_inc(S["gi"], 1)
            g.dma_start(out=blk2r[:], in_=blk2r_d[:]).then_inc(S["lw"], 16)
            g.memset(rowq[:], 0.0).then_inc(S["gi"], 1)
            g.memset(rstdb[:], 0.0).then_inc(S["gi"], 1)
            g.memset(invsb[64:96, :], 0.0).then_inc(S["gi"], 1)
            g.memset(onesf[64:96, :], 0.0).then_inc(S["gch"], 1)
            gch[0] += 1
            g.wait_ge(S["gch"], gch[0])
            g.memset(onesf[64:65, :], 1.0).then_inc(S["gi"], 1)
            for h in range(16):
                g.memset(vp[h][:].rearrange("p (a b) -> p a b", b=65)[:, :, 64:65], 1.0
                         ).then_inc(S["gi"], 1)
            for grp in range(6):
                if grp == 2:
                    g.wait_ge(S["qm"], 24)
                elif grp == 3:
                    g.wait_ge(S["qm"], 48)
                elif grp == 4:
                    g.wait_ge(S["qm"], 72)
                elif grp == 5:
                    g.wait_ge(S["qm"], 96)
                for c in range(8):
                    g.dma_start(out=wkv_buf[grp % 2][:, c * 512:(c + 1) * 512],
                                in_=Wqkv_d[c * 128:(c + 1) * 128, grp * 512:(grp + 1) * 512]
                                ).then_inc(S["lw"], 16)
            g.wait_ge(S["qm"], NQK + NV)
            for i in range(4):
                g.dma_start(out=t_cs[i][:], in_=CS_d[i][:]).then_inc(S["rpt"], 16)
            for c in range(8):
                g.dma_start(out=wo_sb[c][:], in_=Wo_d[c * 128:(c + 1) * 128, :]).then_inc(S["wol"], 16)
            for tb in range(4):
                if tb >= 1:
                    g.wait_ge(S["so"], 128 * tb)
                for c in range(8):
                    g.dma_start(out=rblk[c][:],
                                in_=rT[c * 128:(c + 1) * 128, 257 + tb * 512: 257 + (tb + 1) * 512]
                                ).then_inc(S["rbl"], 16)

        @block.sync
        def _(sync):
            if IATTN_DEBUG:
                sync.wait_ge(S["oe"], NSUB)
                for c in range(8):
                    sync.dma_start(out=dbgo_d[c * 128:(c + 1) * 128, :], in_=oT[c][:]
                                   ).then_inc(S["so"], 16)
            for tb in range(4):
                sync.wait_ge(S["wom"], 8 * (tb + 1))
                for c in range(8):
                    sync.dma_start(out=rnT_d[c * 128:(c + 1) * 128, tb * 512:(tb + 1) * 512],
                                   in_=rblk[c][:]).then_inc(S["so"], 16)
            sync.wait_ge(S["so"], 16 * 32 + (16 * 8 if IATTN_DEBUG else 0))

        @block.scalar
        def _(sc):
            for qk in range(2):
                src_t = qf if qk == 0 else kf
                for i8 in range(8):
                    idx = qk * 8 + i8
                    sc.wait_ge(S["qe"], q_tile_done_qe(qk, i8))
                    if idx > 0:
                        sc.wait_ge(S["nm"], idx * NBR)
                    sc.activation(sqs[:], src_t[i8][:], AF.Square).then_inc(S["sq"], 1)
                    for blk in range(NBR):
                        sc.wait_ge(S["nv1"], idx * NBR + blk + 1)
                        sc.activation(stdq[:], rowq[:], AF.Sqrt).then_inc(S["nsq"], 1)
            for si in range(NSUB):
                sc.wait_ge(S["mx"], si + 1)
                bank = ps_s0 if si % 2 == 0 else ps_s1
                sc.activation(Pe[:], bank[:], AF.Exp, bias=negm8[:], scale=8.0
                              ).then_inc(S["ex"], 1)

        @block.tensor
        def _(t):
            t.wait_ge(S["lw"], 16 * 16)
            t.wait_ge(S["gi"], 22)
            t.wait_ge(S["rl"], 16 * 8)
            qm_n = 0
            for grp in range(4):
                t.wait_ge(S["lw"], 16 * (8 + 8 * (grp + 1)))
                for e in range(4):
                    for tbb in range(NBR):
                        bank = ps_a if qm_n % 2 == 0 else ps_b
                        if qm_n >= 2:
                            t.wait_ge(S["qe"], qm_n - 1)
                        wb = wkv_buf[grp % 2]
                        for c in range(8):
                            i = t.matmul(bank[:, 0:TBR],
                                         wb[:, c * 512 + e * 128: c * 512 + (e + 1) * 128],
                                         rsh[c][:, tbb * TBR:(tbb + 1) * TBR],
                                         start=(c == 0), stop=(c == 7))
                        i.then_inc(S["qm"], 1)
                        qm_n += 1
            for tc in range(NCH):
                for vh in range(2):
                    t.wait_ge(S["lw"], 16 * (8 + 8 * (5 + vh)))
                    bank = ps_a if qm_n % 2 == 0 else ps_b
                    t.wait_ge(S["qe"], qm_n - 1)
                    wb = wkv_buf[vh]
                    for c in range(8):
                        i = t.matmul(bank[:], rsh[c][:, tc * 128:(tc + 1) * 128],
                                     wb[:, c * 512:(c + 1) * 512],
                                     start=(c == 0), stop=(c == 7))
                    i.then_inc(S["qm"], 1)
                    qm_n += 1
            sw_n = 0
            for qk in range(2):
                src_t = qf if qk == 0 else kf
                qsc = qs_sb if qk == 0 else ks_sb
                tb1 = t_cs[0] if qk == 0 else t_cs[2]
                tb2 = t_cs[1] if qk == 0 else t_cs[3]
                for i8 in range(8):
                    idx = qk * 8 + i8
                    t.wait_ge(S["sq"], idx + 1)
                    for blk in range(NBR):
                        bidx = idx * NBR + blk
                        if bidx >= 1:
                            t.wait_ge(S["nre"], bidx)
                        t.matmul(ps_n, blk2[:], sqs[:, blk * TBR:(blk + 1) * TBR],
                                 start=True, stop=True).then_inc(S["nm"], 1)
                        t.wait_ge(S["ntl"], bidx + 1)
                        if bidx >= 1:
                            t.wait_ge(S["qn"], bidx)
                        t.matmul(ps_bc[:], blk2r[:], rstdb[:],
                                 start=True, stop=True).then_inc(S["nb"], 1)
                        t.wait_ge(S["qn"], bidx + 1)
                        bank = ps_a if sw_n % 2 == 0 else ps_b
                        if sw_n >= 2:
                            t.wait_ge(S["swe"], sw_n - 1)
                        t.matmul(bank[:, 0:TBR], swp_sb[:],
                                 src_t[i8][:, blk * TBR:(blk + 1) * TBR],
                                 start=True, stop=True).then_inc(S["sw"], 1)
                        sw_n += 1
            t.wait_ge(S["rp"], NQK)
            t.wait_ge(S["qe"], NQK + NV)
            si = 0
            for h in range(16):
                i8, ro = h // 2, (h % 2) * 64
                for wj in range(1, 9):
                    for s in range(2):
                        ks = (wj - 1) * 256 + s * 128
                        qo = wj * 256 + s * 128
                        bank = ps_s0 if si % 2 == 0 else ps_s1
                        if si >= 2:
                            t.wait_ge(S["ex"], si - 1)
                        t.matmul(bank[:], qf[i8][ro:ro + 64, qo:qo + 128],
                                 kf[i8][ro:ro + 64, ks:ks + 384], start=True, stop=False)
                        m0 = msk_sb[2] if wj == 1 else msk_sb[0]
                        t.matmul(bank[:, 0:128], ident[:], m0[:], start=False, stop=False)
                        if wj == 1 and s == 0:
                            t.matmul(bank[:, 128:256], ident[:], msk_sb[3][:],
                                     start=False, stop=False)
                        i = t.matmul(bank[:, 256:384], ident[:], msk_sb[1][:],
                                     start=False, stop=True)
                        i.then_inc(S["sS"], 1)
                        t.wait_ge(S["ex"], si + 1)
                        if si >= 1:
                            t.wait_ge(S["te"], 3 * si)
                        for ck in range(3):
                            i = t.transpose(ps_tr[:, ck * 128:(ck + 1) * 128],
                                            Pe[:, ck * 128:(ck + 1) * 128], ident[:])
                        i.then_inc(S["tr"], 1)
                        t.wait_ge(S["te"], 3 * si + 3)
                        if si >= 1:
                            t.wait_ge(S["oe"], si)
                        for ck in range(3):
                            kch = ks // 128 + ck
                            i = t.matmul(ps_oT[0:65, :], vp[h][:, kch * 65:(kch + 1) * 65],
                                         PT[:, ck * 128:(ck + 1) * 128],
                                         start=(ck == 0), stop=(ck == 2))
                        i.then_inc(S["pv"], 1)
                        t.wait_ge(S["iv"], si + 1)
                        if si >= 1:
                            t.wait_ge(S["pn"], si)
                        t.matmul(ps_n2,
                                 onesf[64:96, :], invsb[64:96, :],
                                 start=True, stop=True).then_inc(S["bcv"], 1)
                        si += 1
            t.wait_ge(S["wol"], 16 * 8)
            t.wait_ge(S["oe"], NSUB)
            wo_n = 0
            for tb in range(4):
                for e in range(8):
                    bank = ps_a if wo_n % 2 == 0 else ps_b
                    if wo_n >= 2:
                        t.wait_ge(S["woe"], wo_n - 1)
                    for c in range(8):
                        i = t.matmul(bank[:], wo_sb[c][:, e * 128:(e + 1) * 128],
                                     oT[c][:, tb * 512:(tb + 1) * 512],
                                     start=(c == 0), stop=(c == 7))
                    i.then_inc(S["wom"], 1)
                    wo_n += 1

        @block.vector
        def _(v):
            qm_n = 0
            for grp in range(4):
                for e in range(4):
                    for tbb in range(NBR):
                        v.wait_ge(S["qm"], qm_n + 1)
                        bank = ps_a if qm_n % 2 == 0 else ps_b
                        dst = qf if grp < 2 else kf
                        ti = (grp % 2) * 4 + e
                        v.tensor_copy(dst[ti][:, tbb * TBR:(tbb + 1) * TBR], bank[:, 0:TBR]
                                      ).then_inc(S["qe"], 1)
                        qm_n += 1
            for tc in range(NCH):
                for vh in range(2):
                    v.wait_ge(S["qm"], qm_n + 1)
                    bank = ps_a if qm_n % 2 == 0 else ps_b
                    for hh in range(8):
                        i = v.tensor_copy(vp[vh * 8 + hh][:, tc * 65: tc * 65 + 64],
                                          bank[:, hh * 64:(hh + 1) * 64])
                    i.then_inc(S["qe"], 1)
                    qm_n += 1
            v.wait_ge(S["rpt"], 64)
            sw_n = 0
            for qk in range(2):
                src_t = qf if qk == 0 else kf
                qsc = qs_sb if qk == 0 else ks_sb
                tb1 = t_cs[0] if qk == 0 else t_cs[2]
                tb2 = t_cs[1] if qk == 0 else t_cs[3]
                for i8 in range(8):
                    idx = qk * 8 + i8
                    for blk in range(NBR):
                        bidx = idx * NBR + blk
                        sl = slice(blk * TBR, (blk + 1) * TBR)
                        v.wait_ge(S["nm"], bidx + 1)
                        v.tensor_scalar(rowq[:], ps_n, 1e-24, None, ALU.max
                                        ).then_inc(S["nre"], 1)
                        v.wait_ge(S["nre"], bidx + 1)
                        v.engine_nop().then_inc(S["nv1"], 1)
                        v.wait_ge(S["nsq"], bidx + 1)
                        v.reciprocal(rowq[:], stdq[:]).then_inc(S["vch"], 1)
                        vch[0] += 1
                        v.wait_ge(S["vch"], vch[0])
                        v.tensor_copy(rstdb[0:2, :], rowq[:]).then_inc(S["ntl"], 1)
                        v.wait_ge(S["nb"], bidx + 1)
                        v.scalar_tensor_tensor(src_t[i8][:, sl], src_t[i8][:, sl],
                                               qsc[:], ps_bc[:],
                                               ALU.mult, ALU.mult).then_inc(S["qn"], 1)
                        v.wait_ge(S["sw"], sw_n + 1)
                        bank = ps_a if sw_n % 2 == 0 else ps_b
                        v.tensor_copy(rt0[:], bank[:, 0:TBR]).then_inc(S["swe"], 1)
                        v.tensor_mul(rt1[:], src_t[i8][:, sl], tb1[:, sl])
                        v.wait_ge(S["swe"], sw_n + 1)
                        v.tensor_mul(rt0[:], rt0[:], tb2[:, sl]).then_inc(S["vch"], 1)
                        vch[0] += 1
                        v.wait_ge(S["vch"], vch[0])
                        v.tensor_add(src_t[i8][:, sl], rt0[:], rt1[:]).then_inc(S["rp"], 1)
                        sw_n += 1
            for si in range(NSUB):
                v.wait_ge(S["sS"], si + 1)
                bank = ps_s0 if si % 2 == 0 else ps_s1
                v.reduce_max(mcol[:], bank[:], axis=AX.X).then_inc(S["mr"], 1)
                v.wait_ge(S["mr"], si + 1)
                v.tensor_scalar_mul(negm8[:], mcol[:], -8.0).then_inc(S["mx"], 1)
                v.wait_ge(S["tr"], si + 1)
                for ck in range(3):
                    i = v.tensor_copy(PT[:, ck * 128:(ck + 1) * 128],
                                      ps_tr[:, ck * 128:(ck + 1) * 128])
                    i.then_inc(S["te"], 1)
                v.wait_ge(S["pv"], si + 1)
                v.reciprocal(invsb[64:65, :], ps_oT[64:65, :]).then_inc(S["iv"], 1)
                v.wait_ge(S["bcv"], si + 1)
                v.tensor_copy(invb[:], ps_n2).then_inc(S["pn"], 1)
                v.wait_ge(S["pn"], si + 1)
                h = si // 16
                wj = 1 + (si % 16) // 2
                s = si % 2
                i8, ro = h // 2, (h % 2) * 64
                v.tensor_mul(oT[i8][ro:ro + 64,
                                    (wj - 1) * 256 + s * 128: (wj - 1) * 256 + s * 128 + 128],
                             ps_oT[0:64, :], invb[:]).then_inc(S["oe"], 1)
            wo_n = 0
            for tb in range(4):
                v.wait_ge(S["rbl"], 128 * (tb + 1))
                for e in range(8):
                    v.wait_ge(S["wom"], wo_n + 1)
                    bank = ps_a if wo_n % 2 == 0 else ps_b
                    v.tensor_add(rblk[e][:], bank[:], rblk[e][:]).then_inc(S["woe"], 1)
                    wo_n += 1

    print("IATTN sbuf remaining:", nc.sbuf_bytes_remaining)
    return nc


def _iattn_host(inputs, layer):
    key = ("iattn_host", layer)
    if key in _CACHE:
        return _CACHE[key]
    Wqkv = np.ascontiguousarray(np.asarray(inputs["L0_ffn_Wqkv"][layer], np.float32))
    qs = np.asarray(inputs["L0_ffn_q_scale"][layer], np.float32)
    ks = np.asarray(inputs["L0_ffn_k_scale"][layer], np.float32)
    Wo = np.ascontiguousarray(np.asarray(inputs["L0_ffn_Wo"][layer], np.float32))
    r = {
        "Wqkv": Wqkv, "Wo": Wo,
        "qs_c": np.ascontiguousarray(np.tile(qs, 2)[:, None]).astype(np.float32),
        "ks_c": np.ascontiguousarray(np.tile(ks, 2)[:, None]).astype(np.float32),
    }
    _CACHE[key] = r
    return r


def _iattn_const():
    if "iattn_const" in _CACHE:
        return _CACHE["iattn_const"]
    swp = np.zeros((128, 128), np.float32)
    for b0 in range(0, 128, 64):
        for d in range(32):
            swp[b0 + d + 32, b0 + d] = 1.0
            swp[b0 + d, b0 + d + 32] = 1.0
    swp = swp.astype(BF)
    NEG = -1e9
    ir = np.arange(128)[:, None]
    jr = np.arange(128)[None, :]
    m_lo = np.where(jr >= ir, 0.0, NEG).astype(np.float32)
    m_hi = np.where(jr <= ir, 0.0, NEG).astype(np.float32)
    full = np.full((128, 128), NEG, np.float32)
    zero = np.zeros((128, 128), np.float32)
    mask_even = np.concatenate([m_lo, m_hi, full, full], 0).astype(BF)
    mask_odd = np.concatenate([m_lo, m_hi, m_lo, zero], 0).astype(BF)
    tabs = {}
    for p in range(2):
        kvstart = p * 2048 - 256
        t = np.arange(kvstart, kvstart + RGA, dtype=np.float64)
        dd = np.arange(64)
        inv_freq = 1.0 / 10000.0 ** (np.arange(0, 64, 2, dtype=np.float64) / 64.0)
        f = inv_freq[dd % 32]
        base = (np.arange(0, 64, 2, dtype=np.float64) + 0.4 * 64) / (1.4 * 64)
        bb = base[dd % 32]
        ang = f[:, None] * t[None, :]
        power = (t - 2048.0) / 128.0
        scale = bb[:, None] ** power[None, :]
        sig = np.where(dd < 32, -1.0, 1.0)[:, None]
        cs1q = np.cos(ang) * scale
        cs2q = sig * np.sin(ang) * scale
        cs1k = np.cos(ang) / scale
        cs2k = sig * np.sin(ang) / scale
        tabs[p] = [np.concatenate([x, x], 0).astype(BF) for x in (cs1q, cs2q, cs1k, cs2k)]
    r = {"swp2": swp, "mask_even": mask_even, "mask_odd": mask_odd, "tabs": tabs}
    _CACHE["iattn_const"] = r
    return r


def run_iattn(inputs, layer, r_full):
    nc = _get("iattn", build_iattn)
    host = _iattn_host(inputs, layer)
    consts = _iattn_const()
    in_maps = []
    for c in range(NCORE):
        b, p = c // 2, c % 2
        rTc = np.zeros((E, RGA + 1), np.float32)
        if p == 0:
            rTc[:, 257:] = r_full[b][:, 0:2048]
        else:
            rTc[:, 1:] = r_full[b][:, 1792:4096]
            rTc[:, 0] = r_full[b][:, 1791]
        b2r = np.zeros((32, 128), np.float32)
        b2r[0, 0:64] = 1.0
        b2r[1, 64:128] = 1.0
        m = {"rT": rTc, "swp2": consts["swp2"], "blk2r": b2r.astype(BF),
             "msk": consts["mask_even"] if p == 0 else consts["mask_odd"]}
        for i in range(4):
            m[f"CS{i}"] = consts["tabs"][p][i]
        m.update(host)
        in_maps.append(m)
    res = run_bass_kernel_spmd(nc, in_maps, core_ids=list(range(NCORE)))
    out = np.zeros((4, E, T), np.float32)
    for c in range(NCORE):
        b, p = c // 2, c % 2
        out[b][:, p * 2048:(p + 1) * 2048] = np.asarray(res.results[c]["rnT"])
    if IATTN_DEBUG:
        run_iattn.dbg = [np.asarray(res.results[c]["dbgo"]) for c in range(NCORE)]
    return out



NSA_DEBUG = False


def build_nsa():
    nc = bass.Bass("TRN2", target_bir_lowering=False, num_devices=NCORE)
    xnT = nc.dram_tensor("xnT", [E, T], F32, kind="ExternalInput").ap()
    Wq_d = nc.dram_tensor("Wq", [E, 512], F32, kind="ExternalInput").ap()
    Wk_d = nc.dram_tensor("Wk", [E, 512], F32, kind="ExternalInput").ap()
    Wv_d = nc.dram_tensor("Wv", [E, 512], F32, kind="ExternalInput").ap()
    Wg_d = nc.dram_tensor("Wgr", [128, 24], F32, kind="ExternalInput").ap()
    bq_d = nc.dram_tensor("bq", [128, 4], F32, kind="ExternalInput").ap()
    bk_d = nc.dram_tensor("bk", [128, 4], F32, kind="ExternalInput").ap()
    bv_d = nc.dram_tensor("bvp", [32, 512], BF16, kind="ExternalInput").ap()
    bg_d = nc.dram_tensor("bg", [3, 1], F32, kind="ExternalInput").ap()
    smask_d = nc.dram_tensor("smask", [3 * 128, 128], BF16, kind="ExternalInput").ap()
    selm_d = nc.dram_tensor("selm", [32, 192], BF16, kind="ExternalInput").ap()
    nsaT_d = nc.dram_tensor("nsaT", [512, T], BF16, kind="ExternalOutput").ap()
    selk_dram = nc.dram_tensor("selk_scratch", [32, 32], F32, kind="Internal")
    if NSA_DEBUG:
        ndbg_d = nc.dram_tensor("ndbg", [16, 128], F32, kind="ExternalOutput").ap()
        gbdbg_d = nc.dram_tensor("gbdbg", [192, T], BF16, kind="ExternalOutput").ap()
        cmdbg_d = nc.dram_tensor("cmdbg", [512, T], BF16, kind="ExternalOutput").ap()
        pdbg_d = nc.dram_tensor("pdbg", [4 * 128, T], BF16, kind="ExternalOutput").ap()

    NTC = T // 128
    PR = 8

    xnb = [nc.alloc_sbuf_tensor(f"nxn{c}", [128, T], BF16) for c in range(8)]
    qT = [nc.alloc_sbuf_tensor(f"nqT{i}", [128, T], BF16) for i in range(4)]
    kT = [nc.alloc_sbuf_tensor(f"nkT{i}", [128, T], BF16) for i in range(4)]
    vp = [nc.alloc_sbuf_tensor(f"nvp{h}", [128, NTC * 65], BF16) for h in range(8)]
    w_sb = [nc.alloc_sbuf_tensor(f"nwb{c}", [128, 512], BF16) for c in range(8)]
    wg_sb = nc.alloc_sbuf_tensor("nwg", [128, 24], BF16)
    xmean = [nc.alloc_sbuf_tensor(f"nxm{c}", [128, 128], BF16) for c in range(8)]
    xmf = nc.alloc_sbuf_tensor("nxmf", [128, 128], F32)
    ckT = [nc.alloc_sbuf_tensor(f"nck{i}", [128, 128], BF16) for i in range(4)]
    cvp = [nc.alloc_sbuf_tensor(f"ncv{h}", [128, 65], BF16) for h in range(8)]
    gates = nc.alloc_sbuf_tensor("ngate", [32, T], BF16)
    selm_sb = nc.alloc_sbuf_tensor("nselm", [32, 192], BF16)
    bq_sb = nc.alloc_sbuf_tensor("nbq", [128, 4], F32)
    bk_sb = nc.alloc_sbuf_tensor("nbk", [128, 4], F32)
    bv_sb = nc.alloc_sbuf_tensor("nbv", [32, 512], BF16)
    bg_sb = nc.alloc_sbuf_tensor("nbg", [3, 1], F32)
    onesrow_b = nc.alloc_sbuf_tensor("nones", [32, 128], BF16)
    onesf = nc.alloc_sbuf_tensor("nonesf", [96, 128], F32)
    negc0 = nc.alloc_sbuf_tensor("nnegc0", [128, 1], F32)
    invsb = nc.alloc_sbuf_tensor("ninvsb", [96, 512], F32)
    invb = nc.alloc_sbuf_tensor("ninvb", [64, 512], F32)
    invb64 = nc.alloc_sbuf_tensor("ninvb64", [64, 128], F32)
    invb128 = nc.alloc_sbuf_tensor("ninvb128", [128, 128], BF16)
    identb = nc.alloc_sbuf_tensor("nidb", [128, 128], BF16)
    identf = nc.alloc_sbuf_tensor("nidf", [128, 128], F32)
    smask = [nc.alloc_sbuf_tensor(f"nsm{i}", [128, 128], BF16) for i in range(3)]
    bsacc = nc.alloc_sbuf_tensor("nbs", [128, 1], F32)
    bstmp = nc.alloc_sbuf_tensor("nbst", [128, 1], F32)
    bsrow = nc.alloc_sbuf_tensor("nbsr", [1, 128], F32)
    top8 = nc.alloc_sbuf_tensor("ntop", [1, 8], F32)
    sel01 = nc.alloc_sbuf_tensor("nsel", [1, 128], F32)
    selkf = nc.alloc_sbuf_tensor("nselkf", [128, 32], F32)
    prb = nc.alloc_sbuf_tensor("nprb", [128, 128], BF16)
    pc = nc.alloc_sbuf_tensor("npc", [128, 128], BF16)
    pdl = nc.alloc_sbuf_tensor("npdl", [128, 128], BF16)
    pdo = nc.alloc_sbuf_tensor("npdo", [128, 128], BF16)
    slide_st = nc.alloc_sbuf_tensor("nslid", [64, 512], BF16)
    sel_sb = nc.alloc_sbuf_tensor("nselo", [64, 512], BF16)
    osum = nc.alloc_sbuf_tensor("nosum", [64, 128], F32)
    otmp = nc.alloc_sbuf_tensor("notmp", [64, 128], F32)
    obuf = [nc.alloc_sbuf_tensor(f"nob{i}", [64, 512], BF16) for i in range(2)]
    gbT = [nc.alloc_sbuf_tensor_at(f"ngbT{j}", [64, T], BF16,
                                   offset=nc.lookup_mloc(xnb[j]).addr) for j in range(3)]
    comp_st = nc.alloc_sbuf_tensor_at("ncomp", [64, T], BF16,
                                      offset=nc.lookup_mloc(xnb[3]).addr)
    pt = nc.alloc_sbuf_tensor_at("npt", [128, PR * 512], BF16,
                                 offset=nc.lookup_mloc(xnb[4]).addr)
    vselp = nc.alloc_sbuf_tensor_at("nvsel", [128, NTC * 65], BF16,
                                    offset=nc.lookup_mloc(xnb[5]).addr)

    ps_a = nc.alloc_psum_tensor("nps_a", [128, 512], F32)
    ps_b = nc.alloc_psum_tensor("nps_b", [128, 512], F32)
    ps_s0 = nc.alloc_psum_tensor("nps_s0", [128, 512], F32)
    ps_s1 = nc.alloc_psum_tensor("nps_s1", [128, 512], F32)
    ps_o = nc.alloc_psum_tensor("nps_o", [128, 512], F32)
    ps_g = nc.alloc_psum_tensor("nps_g", [64, 512], F32)
    ps_c = nc.alloc_psum_tensor("nps_c", [128, 128], F32)
    ps_o2 = nc.alloc_psum_tensor("nps_o2", [128, 128], F32)

    S = {n: nc.alloc_semaphore("n_" + n) for n in
         ["lw", "cw", "xl", "gi", "gld", "pm", "pe", "gm", "ge", "xm",
          "cqk", "cex", "cpv", "cnr", "cbc", "cnc",
          "bsp", "bse", "thr", "sdm", "sde", "vs",
          "sqk", "sex", "spvc", "snr", "sbc", "snc", "sev",
          "dqk", "dex", "dpv", "dnr", "dbc", "dnc",
          "cmb", "so0", "so1", "vch", "gch", "dbs"]}

    with nc.Block() as block:

        @block.gpsimd
        def _(g):
            gch = 0
            for c in range(8):
                g.dma_start(out=xnb[c][:], in_=xnT[c * 128:(c + 1) * 128, :]).then_inc(S["xl"], 16)
            g.dma_start(out=bq_sb[:], in_=bq_d[:]).then_inc(S["cw"], 16)
            g.dma_start(out=bk_sb[:], in_=bk_d[:]).then_inc(S["cw"], 16)
            g.dma_start(out=bv_sb[:], in_=bv_d[:]).then_inc(S["cw"], 16)
            g.dma_start(out=bg_sb[:], in_=bg_d[:]).then_inc(S["cw"], 16)
            g.dma_start(out=wg_sb[:], in_=Wg_d[:]).then_inc(S["cw"], 16)
            for i in range(3):
                g.dma_start(out=smask[i][:], in_=smask_d[i * 128:(i + 1) * 128, :]
                            ).then_inc(S["cw"], 16)
            g.dma_start(out=selm_sb[:], in_=selm_d[:]).then_inc(S["cw"], 16)
            g.memset(onesrow_b[:], 0.0).then_inc(S["gch"], 1)
            gch += 1
            g.wait_ge(S["gch"], gch)
            g.memset(onesrow_b[0:1, :], 1.0).then_inc(S["gi"], 1)
            g.memset(onesf[:], 0.0).then_inc(S["gch"], 1)
            gch += 1
            g.wait_ge(S["gch"], gch)
            g.memset(onesf[64:65, :], 1.0).then_inc(S["gi"], 1)
            g.memset(invsb[64:96, :], 0.0).then_inc(S["gi"], 1)
            g.memset(identb[:], 0.0).then_inc(S["gch"], 1)
            gch += 1
            g.wait_ge(S["gch"], gch)
            g.affine_select(out=identb[:], in_=identb[:], compare_op=ALU.not_equal,
                            fill=1.0, base=0, pattern=[[-1, 128]], channel_multiplier=1
                            ).then_inc(S["gi"], 1)
            g.memset(identf[:], 0.0).then_inc(S["gch"], 1)
            gch += 1
            g.wait_ge(S["gch"], gch)
            g.affine_select(out=identf[:], in_=identf[:], compare_op=ALU.not_equal,
                            fill=1.0, base=0, pattern=[[-1, 128]], channel_multiplier=1
                            ).then_inc(S["gi"], 1)
            for h in range(8):
                g.memset(vp[h][:].rearrange("p (a b) -> p a b", b=65)[:, :, 64:65], 1.0
                         ).then_inc(S["gi"], 1)
            for h in range(8):
                g.memset(cvp[h][:, 64:65], 1.0).then_inc(S["gi"], 1)
            g.memset(bsacc[:], 0.0).then_inc(S["gi"], 1)
            g.memset(gates[:], 0.0).then_inc(S["gi"], 1)
            g.memset(negc0[:], -C0).then_inc(S["gi"], 1)
            for c in range(8):
                g.dma_start(out=w_sb[c][:], in_=Wq_d[c * 128:(c + 1) * 128, :]).then_inc(S["lw"], 16)
            g.wait_ge(S["pm"], 32)
            for c in range(8):
                g.dma_start(out=w_sb[c][:], in_=Wk_d[c * 128:(c + 1) * 128, :]).then_inc(S["lw"], 16)
            g.wait_ge(S["pm"], 68)
            for c in range(8):
                g.dma_start(out=w_sb[c][:], in_=Wv_d[c * 128:(c + 1) * 128, :]).then_inc(S["lw"], 16)

        @block.sync
        def _(sync):
            dram_ap = selk_dram.ap()
            if NSA_DEBUG:
                sync.wait_ge(S["pe"], 197)
                for j in range(3):
                    sync.dma_start(out=gbdbg_d[j * 64:(j + 1) * 64, :], in_=gbT[j][:]
                                   ).then_inc(S["dbs"], 16)
                sync.dma_start(out=pdbg_d[0:32, :], in_=gates[:]).then_inc(S["dbs"], 16)
                sync.dma_start(out=pdbg_d[128:256, :], in_=qT[0][:]).then_inc(S["dbs"], 16)
                sync.dma_start(out=pdbg_d[256:384, :], in_=kT[0][:]).then_inc(S["dbs"], 16)
                sync.dma_start(out=pdbg_d[384:512, 0:NTC * 65], in_=vp[0][:]).then_inc(S["dbs"], 16)
                sync.dma_start(out=pdbg_d[32:64, 0:128], in_=ckT[0][0:32, :]).then_inc(S["dbs"], 16)
                sync.dma_start(out=pdbg_d[64:96, 0:65], in_=cvp[0][0:32, :]).then_inc(S["dbs"], 16)
                sync.dma_start(out=pdbg_d[96:128, 0:128], in_=xmean[0][0:32, :]).then_inc(S["dbs"], 16)
            for h in range(8):
                sync.wait_ge(S["thr"], h + 1)
                if NSA_DEBUG:
                    sync.dma_start(out=ndbg_d[h:h + 1, :], in_=bsrow[:]).then_inc(S["dbs"], 16)
                    sync.dma_start(out=ndbg_d[8 + h:9 + h, :], in_=sel01[:]).then_inc(S["dbs"], 16)
                    sync.wait_ge(S["cnc"], 32 * (h + 1))
                    sync.dma_start(out=cmdbg_d[h * 64:(h + 1) * 64, :], in_=comp_st[:]
                                   ).then_inc(S["dbs"], 16)
                for a in range(4):
                    src = sel01[:].rearrange("p (t a) -> p a t", a=4)[:, a, :]
                    sync.dma_start(out=dram_ap[h * 4 + a:h * 4 + a + 1, :], in_=src,
                                   allow_slow_non_contiguous=True).then_inc(S["sdm"], 16)
                sync.wait_ge(S["sdm"], 64 * (h + 1))
                for a in range(4):
                    src2 = dram_ap[h * 4 + a:h * 4 + a + 1, :].broadcast_to([32, 32])
                    sync.dma_start(out=selkf[a * 32:(a + 1) * 32, :], in_=src2
                                   ).then_inc(S["sde"], 16)
                for qb in range(8):
                    iq = h * 8 + qb
                    sync.wait_ge(S["cmb"], 4 * (iq + 1))
                    sync.dma_start(out=nsaT_d[h * 64:(h + 1) * 64, qb * 512:(qb + 1) * 512],
                                   in_=obuf[qb % 2][:]).then_inc(S[f"so{iq % 2}"], 16)
            sync.wait_ge(S["so0"], 16 * 32)
            sync.wait_ge(S["so1"], 16 * 32)
            if NSA_DEBUG:
                sync.wait_ge(S["dbs"], 16 * 34)

        @block.tensor
        def _(t):
            t.wait_ge(S["gi"], 24)
            t.wait_ge(S["xl"], 128)
            t.wait_ge(S["cw"], 144)
            for tb in range(8):
                if tb:
                    t.wait_ge(S["ge"], tb)
                for c in range(8):
                    mm = t.matmul(ps_g[0:3, :], wg_sb[:, 3 * c:3 * c + 3],
                                  xnb[c][:, tb * 512:(tb + 1) * 512],
                                  start=(c == 0), stop=(c == 7))
                    if c == 7:
                        mm.then_inc(S["gm"], 1)
            banks = [ps_a, ps_b]
            n = 0
            t.wait_ge(S["lw"], 128)
            for e in range(4):
                for tb in range(8):
                    if n >= 2:
                        t.wait_ge(S["pe"], n - 1)
                    bank = banks[n % 2]
                    for c in range(8):
                        mm = t.matmul(bank[:], w_sb[c][:, e * 128:(e + 1) * 128],
                                      xnb[c][:, tb * 512:(tb + 1) * 512],
                                      start=(c == 0), stop=(c == 7))
                    mm.then_inc(S["pm"], 1)
                    n += 1
            t.wait_ge(S["lw"], 256)
            for e in range(4):
                for tb in range(8):
                    t.wait_ge(S["pe"], n - 1)
                    bank = banks[n % 2]
                    for c in range(8):
                        mm = t.matmul(bank[:], w_sb[c][:, e * 128:(e + 1) * 128],
                                      xnb[c][:, tb * 512:(tb + 1) * 512],
                                      start=(c == 0), stop=(c == 7))
                    mm.then_inc(S["pm"], 1)
                    n += 1
            t.wait_ge(S["xm"], 8)
            for i in range(4):
                t.wait_ge(S["pe"], n - 1)
                bank = banks[n % 2]
                for c in range(8):
                    mm = t.matmul(bank[0:128, 0:128], w_sb[c][:, i * 128:(i + 1) * 128],
                                  xmean[c][:], start=(c == 0), stop=(c == 7))
                mm.then_inc(S["pm"], 1)
                n += 1
            t.wait_ge(S["lw"], 384)
            for tc in range(32):
                t.wait_ge(S["pe"], n - 1)
                bank = banks[n % 2]
                for c in range(8):
                    t.matmul(bank[:], xnb[c][:, tc * 128:(tc + 1) * 128], w_sb[c][:],
                             start=(c == 0), stop=False)
                t.matmul(bank[:], onesrow_b[:], bv_sb[:], start=False, stop=True
                         ).then_inc(S["pm"], 1)
                n += 1
            t.wait_ge(S["pe"], n - 1)
            bank = banks[n % 2]
            for c in range(8):
                t.matmul(bank[:], xmean[c][:], w_sb[c][:], start=(c == 0), stop=False)
            t.matmul(bank[:], onesrow_b[:], bv_sb[:], start=False, stop=True
                     ).then_inc(S["pm"], 1)
            n += 1
            t.wait_ge(S["ge"], 8)
            for qt in range(32):
                for j in range(3):
                    t.wait_ge(S["pe"], n - 1)
                    bank = banks[n % 2]
                    t.matmul(bank[0:64, 0:128], selm_sb[:, j * 64:(j + 1) * 64],
                             gates[:, qt * 128:(qt + 1) * 128],
                             start=True, stop=True).then_inc(S["pm"], 1)
                    n += 1
            t.wait_ge(S["pe"], n)
            for h in range(8):
                i4, ro = h // 2, (h % 2) * 64
                if h:
                    t.wait_ge(S["dnc"], 32 * h)
                    t.wait_ge(S["dex"], 64 * h)
                for qt in range(32):
                    idx = h * 32 + qt
                    if idx:
                        t.wait_ge(S["cex"], idx)
                    t.matmul(ps_c[:], ckT[i4][ro:ro + 64, :],
                             qT[i4][ro:ro + 64, qt * 128:(qt + 1) * 128],
                             start=True, stop=True).then_inc(S["cqk"], 1)
                    t.wait_ge(S["cex"], idx + 1)
                    if idx:
                        t.wait_ge(S["cnc"], idx)
                    t.matmul(ps_o2[0:65, :], cvp[h][:], pc[:], start=True, stop=True
                             ).then_inc(S["cpv"], 1)
                    t.wait_ge(S["cnr"], idx + 1)
                    t.matmul(ps_s1[0:128, 0:128], onesf[64:96, :], invsb[64:96, 0:128],
                             start=True, stop=True).then_inc(S["cbc"], 1)
                t.wait_ge(S["cnc"], 32 * (h + 1))
                t.matmul(ps_c[0:1, 0:128], bsacc[:], identf[:], start=True, stop=True
                         ).then_inc(S["bsp"], 1)
                t.wait_ge(S["vs"], h + 1)
                for qb in range(8):
                    iq = h * 8 + qb
                    if iq:
                        t.wait_ge(S["dnc"], 4 * iq)
                    if iq:
                        t.wait_ge(S["sev"], iq)
                    for kc in range(NTC + 1):
                        if kc < NTC:
                            ns = iq * NTC + kc
                            if ns >= 2:
                                t.wait_ge(S["sex"], ns - 1)
                            sbank = [ps_s0, ps_s1][ns % 2]
                            t.matmul(sbank[:], kT[i4][ro:ro + 64, kc * 128:(kc + 1) * 128],
                                     qT[i4][ro:ro + 64, qb * 512:(qb + 1) * 512],
                                     start=True, stop=True).then_inc(S["sqk"], 1)
                        if kc >= 1:
                            kp = kc - 1
                            np_ = iq * NTC + kp
                            t.wait_ge(S["sex"], np_ + 1)
                            t.matmul(ps_o[0:65, :], vselp[:, kp * 65:(kp + 1) * 65],
                                     pt[:, (np_ % PR) * 512:(np_ % PR) * 512 + 512],
                                     start=(kp == 0), stop=(kp == NTC - 1)).then_inc(S["spvc"], 1)
                    t.wait_ge(S["snr"], iq + 1)
                    if iq:
                        t.wait_ge(S["snc"], iq)
                    t.matmul(ps_g[0:64, :], onesf[64:96, 0:64], invsb[64:96, 0:512],
                             start=True, stop=True).then_inc(S["sbc"], 1)
                    if qb == 0:
                        t.wait_ge(S["bse"], h + 1)
                    for s4 in range(4):
                        qt = qb * 4 + s4
                        i4x = iq * 4 + s4
                        if i4x:
                            t.wait_ge(S["dex"], 2 * i4x)
                        lc = max(qt - 1, 0)
                        mi = 2 if qt == 0 else 0
                        t.matmul(ps_c[:], kT[i4][ro:ro + 64, lc * 128:(lc + 1) * 128],
                                 qT[i4][ro:ro + 64, qt * 128:(qt + 1) * 128],
                                 start=True, stop=False)
                        t.matmul(ps_c[:], identb[:], smask[mi][:], start=False, stop=True
                                 ).then_inc(S["dqk"], 1)
                        t.matmul(ps_a[0:128, 0:128], kT[i4][ro:ro + 64, qt * 128:(qt + 1) * 128],
                                 qT[i4][ro:ro + 64, qt * 128:(qt + 1) * 128],
                                 start=True, stop=False)
                        t.matmul(ps_a[0:128, 0:128], identb[:], smask[1][:], start=False, stop=True
                                 ).then_inc(S["dqk"], 1)
                        if i4x:
                            t.wait_ge(S["dnc"], i4x)
                        t.wait_ge(S["dex"], 2 * i4x + 1)
                        t.matmul(ps_o2[0:65, :], vp[h][:, lc * 65:(lc + 1) * 65], pdl[:],
                                 start=True, stop=False)
                        t.wait_ge(S["dex"], 2 * i4x + 2)
                        t.matmul(ps_o2[0:65, :], vp[h][:, qt * 65:(qt + 1) * 65], pdo[:],
                                 start=False, stop=True).then_inc(S["dpv"], 1)
                        t.wait_ge(S["dnr"], i4x + 1)
                        t.matmul(ps_s1[0:64, 0:128], onesf[64:96, 0:64], invsb[64:96, 0:128],
                                 start=True, stop=True).then_inc(S["dbc"], 1)

        @block.scalar
        def _(sc):
            sc.wait_ge(S["cw"], 144)
            sc.wait_ge(S["gi"], 24)
            for tb in range(8):
                sc.wait_ge(S["gm"], tb + 1)
                sc.activation(gates[0:3, tb * 512:(tb + 1) * 512], ps_g[0:3, :],
                              AF.Sigmoid, bias=bg_sb[:]).then_inc(S["ge"], 1)
            for h in range(8):
                for qt in range(32):
                    idx = h * 32 + qt
                    sc.wait_ge(S["cqk"], idx + 1)
                    if idx:
                        sc.wait_ge(S["cnc"], idx)
                    sc.activation(pc[:], ps_c[:], AF.Exp, bias=negc0[0:128, :], scale=0.125
                                  ).then_inc(S["cex"], 1)
                for qb in range(8):
                    iq = h * 8 + qb
                    for kc in range(NTC):
                        ns = iq * NTC + kc
                        sc.wait_ge(S["sqk"], ns + 1)
                        if ns >= PR:
                            sc.wait_ge(S["spvc"], ns - PR + 1)
                        sbank = [ps_s0, ps_s1][ns % 2]
                        sc.activation(pt[:, (ns % PR) * 512:(ns % PR) * 512 + 512], sbank[:],
                                      AF.Exp, bias=negc0[0:128, :], scale=0.125
                                      ).then_inc(S["sex"], 1)
                    for s4 in range(4):
                        i4x = iq * 4 + s4
                        sc.wait_ge(S["dqk"], 2 * i4x + 1)
                        sc.activation(pdl[:], ps_c[:], AF.Exp, bias=negc0[0:128, :], scale=0.125
                                      ).then_inc(S["dex"], 1)
                        sc.wait_ge(S["dqk"], 2 * i4x + 2)
                        sc.activation(pdo[:], ps_a[0:128, 0:128], AF.Exp,
                                      bias=negc0[0:128, :], scale=0.125).then_inc(S["dex"], 1)

        @block.vector
        def _(v):
            vch = [0]

            def selfw(tag):
                v.wait_ge(S["vch"], vch[0])

            v.wait_ge(S["cw"], 144)
            v.wait_ge(S["xl"], 128)
            v.wait_ge(S["gi"], 24)
            banks = [ps_a, ps_b]
            n = 0
            for e in range(4):
                for tb in range(8):
                    v.wait_ge(S["pm"], n + 1)
                    v.tensor_scalar(qT[e][:, tb * 512:(tb + 1) * 512], banks[n % 2][:],
                                    bq_sb[:, e:e + 1], None, ALU.add).then_inc(S["pe"], 1)
                    n += 1
            for e in range(4):
                for tb in range(8):
                    v.wait_ge(S["pm"], n + 1)
                    v.tensor_scalar(kT[e][:, tb * 512:(tb + 1) * 512], banks[n % 2][:],
                                    bk_sb[:, e:e + 1], None, ALU.add).then_inc(S["pe"], 1)
                    n += 1
            for c in range(8):
                v.reduce_sum(xmf[:].rearrange("p (n o) -> p n o", o=1),
                             xnb[c][:].rearrange("p (n w) -> p n w", w=32),
                             axis=AX.X).then_inc(S["vch"], 1)
                vch[0] += 1
                selfw("xm")
                v.tensor_scalar(xmean[c][:], xmf[:], 1.0 / 32.0, None, ALU.mult
                                ).then_inc(S["xm"], 1)
            for i in range(4):
                v.wait_ge(S["pm"], n + 1)
                v.tensor_scalar(ckT[i][:], banks[n % 2][0:128, 0:128],
                                bk_sb[:, i:i + 1], None, ALU.add).then_inc(S["pe"], 1)
                n += 1
            for tc in range(32):
                v.wait_ge(S["pm"], n + 1)
                for h8 in range(8):
                    cp = v.tensor_copy(vp[h8][:, tc * 65:tc * 65 + 64],
                                       banks[n % 2][:, h8 * 64:(h8 + 1) * 64])
                cp.then_inc(S["pe"], 1)
                n += 1
            v.wait_ge(S["pm"], n + 1)
            for h8 in range(8):
                cp = v.tensor_copy(cvp[h8][:, 0:64], banks[n % 2][:, h8 * 64:(h8 + 1) * 64])
            cp.then_inc(S["pe"], 1)
            n += 1
            for qt in range(32):
                for j in range(3):
                    v.wait_ge(S["pm"], n + 1)
                    v.tensor_copy(gbT[j][:, qt * 128:(qt + 1) * 128], banks[n % 2][0:64, 0:128]
                                  ).then_inc(S["pe"], 1)
                    n += 1
            for h in range(8):
                for qt in range(32):
                    idx = h * 32 + qt
                    v.wait_ge(S["cpv"], idx + 1)
                    v.reciprocal(invsb[64:65, 0:128], ps_o2[64:65, :]).then_inc(S["cnr"], 1)
                    v.wait_ge(S["cbc"], idx + 1)
                    v.tensor_copy(invb64[:], ps_s1[0:64, 0:128])
                    v.tensor_copy(invb128[:], ps_s1[0:128, 0:128]).then_inc(S["vch"], 1)
                    vch[0] += 1
                    selfw("c")
                    v.tensor_mul(comp_st[:, qt * 128:(qt + 1) * 128], ps_o2[0:64, :], invb64[:])
                    v.tensor_mul(prb[:], pc[:], invb128[:]).then_inc(S["vch"], 1)
                    vch[0] += 1
                    selfw("c2")
                    v.reduce_sum(bstmp[:], prb[:], axis=AX.X).then_inc(S["vch"], 1)
                    vch[0] += 1
                    selfw("c3")
                    v.tensor_add(bsacc[:], bstmp[:], bsacc[:]).then_inc(S["cnc"], 1)
                v.wait_ge(S["bsp"], h + 1)
                if h:
                    v.wait_ge(S["sdm"], 64 * h)
                v.tensor_copy(bsrow[:], ps_c[0:1, 0:128]).then_inc(S["bse"], 1)
                v.wait_ge(S["bse"], h + 1)
                v.max(top8[:], bsrow[:]).then_inc(S["vch"], 1)
                vch[0] += 1
                selfw("t8")
                v.tensor_scalar(sel01[:], bsrow[:], top8[0:1, 7:8], None, ALU.is_ge
                                ).then_inc(S["thr"], 1)
                v.tensor_scalar(bsacc[:], bsacc[:], 0.0, None, ALU.mult)
                v.wait_ge(S["sde"], 64 * (h + 1))
                if h:
                    v.wait_ge(S["spvc"], 256 * h)
                for tc in range(32):
                    ts = v.tensor_scalar(vselp[:, tc * 65:(tc + 1) * 65],
                                         vp[h][:, tc * 65:(tc + 1) * 65],
                                         selkf[:, tc:tc + 1], None, ALU.mult)
                ts.then_inc(S["vs"], 1)
                for qb in range(8):
                    iq = h * 8 + qb
                    v.wait_ge(S["spvc"], 32 * (iq + 1))
                    v.reciprocal(invsb[64:65, 0:512], ps_o[64:65, :]).then_inc(S["snr"], 1)
                    v.wait_ge(S["sbc"], iq + 1)
                    v.tensor_copy(invb[:], ps_g[0:64, :]).then_inc(S["snc"], 1)
                    v.wait_ge(S["snc"], iq + 1)
                    v.tensor_mul(sel_sb[:], ps_o[0:65 - 1, :], invb[:]).then_inc(S["sev"], 1)
                    for s4 in range(4):
                        i4x = iq * 4 + s4
                        v.wait_ge(S["dpv"], i4x + 1)
                        v.reciprocal(invsb[64:65, 0:128], ps_o2[64:65, :]).then_inc(S["dnr"], 1)
                        v.wait_ge(S["dbc"], i4x + 1)
                        v.tensor_copy(invb64[:], ps_s1[0:64, 0:128]).then_inc(S["vch"], 1)
                        vch[0] += 1
                        selfw("d")
                        v.tensor_mul(slide_st[:, s4 * 128:(s4 + 1) * 128], ps_o2[0:64, :],
                                     invb64[:]).then_inc(S["dnc"], 1)
                    for s4 in range(4):
                        qt = qb * 4 + s4
                        if iq >= 2 and s4 == 0:
                            v.wait_ge(S[f"so{iq % 2}"], 16 * (iq // 2))
                        v.tensor_mul(osum[:], comp_st[:, qt * 128:(qt + 1) * 128],
                                     gbT[0][:, qt * 128:(qt + 1) * 128])
                        v.tensor_mul(otmp[:], sel_sb[:, s4 * 128:(s4 + 1) * 128],
                                     gbT[1][:, qt * 128:(qt + 1) * 128])
                        v.tensor_add(osum[:], osum[:], otmp[:])
                        v.tensor_mul(otmp[:], slide_st[:, s4 * 128:(s4 + 1) * 128],
                                     gbT[2][:, qt * 128:(qt + 1) * 128])
                        v.tensor_add(obuf[qb % 2][:, s4 * 128:(s4 + 1) * 128], osum[:], otmp[:]
                                     ).then_inc(S["cmb"], 1)

    print("NSA sbuf remaining:", nc.sbuf_bytes_remaining)
    return nc


def _nsa_host(inputs):
    Wq = np.asarray(inputs["L1_attn_Wq"], np.float32)
    Wk = np.asarray(inputs["L1_attn_Wk"], np.float32)
    Wv = np.asarray(inputs["L1_attn_Wv"], np.float32)
    Wg = np.asarray(inputs["L1_attn_Wg"], np.float32)
    bq = np.asarray(inputs["L1_attn_bq"], np.float32)
    bk = np.asarray(inputs["L1_attn_bk"], np.float32)
    bv = np.asarray(inputs["L1_attn_bv"], np.float32)
    bg = np.asarray(inputs["L1_attn_bg"], np.float32)
    wgr = np.ascontiguousarray(Wg.reshape(8, 128, 3).transpose(1, 0, 2).reshape(128, 24))
    kk = np.arange(128)[:, None]
    qq = np.arange(128)[None, :]
    sm = np.zeros((3 * 128, 128), np.float32)
    sm[0:128] = np.where(kk > qq, 0.0, -1e9)
    sm[128:256] = np.where(kk <= qq, 0.0, -1e9)
    sm[256:384] = -1e9
    selm = np.zeros((32, 192), np.float32)
    for j in range(3):
        selm[j, j * 64:(j + 1) * 64] = 1.0
    out = []
    for ph in range(2):
        sl = slice(ph * 512, (ph + 1) * 512)
        bvp = np.zeros((32, 512), BF)
        bvp[0] = bv[sl].astype(BF)
        out.append({
            "Wq": np.ascontiguousarray(Wq[:, sl]),
            "Wk": np.ascontiguousarray(Wk[:, sl]),
            "Wv": np.ascontiguousarray(Wv[:, sl]),
            "Wgr": wgr,
            "bq": np.ascontiguousarray(bq[sl].reshape(4, 128).T),
            "bk": np.ascontiguousarray(bk[sl].reshape(4, 128).T),
            "bvp": bvp,
            "bg": bg.reshape(3, 1),
            "smask": sm.astype(BF),
            "selm": selm.astype(BF),
        })
    return out


def run_nsa(inputs, xnT_percore):
    nc = _get("nsa", build_nsa)
    host = _nsa_host(inputs)
    in_maps = []
    for c in range(NCORE):
        m = {"xnT": xnT_percore[c]}
        m.update(host[c % 2])
        in_maps.append(m)
    res = run_bass_kernel_spmd(nc, in_maps, core_ids=list(range(NCORE)))
    if NSA_DEBUG:
        run_nsa.dbg = [{k: np.asarray(res.results[c][k]) for k in ("ndbg", "gbdbg", "cmdbg", "pdbg")}
                       for c in range(NCORE)]
    return [np.asarray(res.results[c]["nsaT"]) for c in range(NCORE)]



def build_rwkv():
    nc = bass.Bass("TRN2", target_bir_lowering=False, num_devices=NCORE)
    TH = T // 2
    xfT = nc.dram_tensor("xfT", [E, TH + 1], F32, kind="ExternalInput").ap()
    x3T = nc.dram_tensor("x3T", [E, TH], F32, kind="ExternalInput").ap()
    wrT_d = nc.dram_tensor("wrT", [E, E], F32, kind="ExternalInput").ap()
    wkT_d = nc.dram_tensor("wkT", [E, E], F32, kind="ExternalInput").ap()
    wvT_d = nc.dram_tensor("wvT", [E, E], F32, kind="ExternalInput").ap()
    mu_d = nc.dram_tensor("mucols", [128, 32], F32, kind="ExternalInput").ap()
    x4T_d = nc.dram_tensor("x4T", [E, TH], F32, kind="ExternalOutput").ap()

    TB = 512
    NTB = TH // TB

    wrT = [nc.alloc_sbuf_tensor(f"kwr{c}", [128, E], BF16) for c in range(8)]
    wkT = [nc.alloc_sbuf_tensor(f"kwk{c}", [128, E], BF16) for c in range(8)]
    wvT = [nc.alloc_sbuf_tensor(f"kwv{c}", [128, E], BF16) for c in range(8)]
    mu_sb = nc.alloc_sbuf_tensor("kmu", [128, 32], F32)
    xfin = [nc.alloc_sbuf_tensor(f"kxf{c}", [128, TB + 1], F32) for c in range(8)]
    xf3 = [nc.alloc_sbuf_tensor(f"kx3{c}", [128, TB], F32) for c in range(8)]
    xrb = [nc.alloc_sbuf_tensor(f"kxr{c}", [128, TB], BF16) for c in range(8)]
    xkb = [nc.alloc_sbuf_tensor(f"kxk{c}", [128, TB], BF16) for c in range(8)]
    rb = [nc.alloc_sbuf_tensor(f"krb{c}", [128, TB], F32) for c in range(8)]
    krl = [nc.alloc_sbuf_tensor(f"krl{c}", [128, TB], BF16) for c in range(8)]
    kab = [nc.alloc_sbuf_tensor(f"kka{c}", [128, TB], BF16) for c in range(8)]
    yb = [nc.alloc_sbuf_tensor(f"kyb{c}", [128, TB], BF16) for c in range(8)]
    ysq = [nc.alloc_sbuf_tensor(f"kys{c}", [128, TB], BF16) for c in range(8)]
    x4f = [nc.alloc_sbuf_tensor(f"kx4{c}", [128, TB], F32) for c in range(8)]
    tmpm = nc.alloc_sbuf_tensor("ktmp", [128, TB], F32)
    tmp1 = nc.alloc_sbuf_tensor("ktm1", [128, TB], F32)
    rowbuf = nc.alloc_sbuf_tensor("krow", [1, 6 * TB], F32)
    srow = rowbuf[:, 0:TB]
    s2row = rowbuf[:, TB:2 * TB]
    vrow = rowbuf[:, 2 * TB:3 * TB]
    stdrow = rowbuf[:, 3 * TB:4 * TB]
    arow = rowbuf[:, 4 * TB:5 * TB]
    rowbuf_b = nc.alloc_sbuf_tensor("krow_b", [32, 2 * TB], BF16)
    mrow_b = rowbuf_b[:, 0:TB]
    arow_b = rowbuf_b[:, TB:2 * TB]
    ab_sb = nc.alloc_sbuf_tensor("kab_sb", [128, TB], F32)
    mb_sb = nc.alloc_sbuf_tensor("kmb_sb", [128, TB], F32)
    onesrow_b = nc.alloc_sbuf_tensor("kones", [32, 128], BF16)
    onescol_b = nc.alloc_sbuf_tensor("konec", [128, 1], BF16)

    ps_a = nc.alloc_psum_tensor("kps_a", [128, 512], F32)
    ps_b = nc.alloc_psum_tensor("kps_b", [128, 512], F32)
    ps_stat = nc.alloc_psum_tensor("kps_st", [1, 2 * TB], F32)
    ps_bc = nc.alloc_psum_tensor("kps_bc", [128, TB], F32)
    ps_bc2 = nc.alloc_psum_tensor("kps_b2", [128, TB], F32)

    S = {n: nc.alloc_semaphore("k_" + n) for n in
         ["lw", "cw", "gi", "xfl", "x3l", "mx", "mv", "yes", "ksq", "yev",
          "sq", "st", "rm1", "sqd", "rowm", "bc", "x4d", "so"]}

    with nc.Block() as block:

        @block.gpsimd
        def _(g):
            g.dma_start(out=mu_sb[:], in_=mu_d[:]).then_inc(S["cw"], 16)
            for c in range(8):
                g.dma_start(out=wrT[c][:], in_=wrT_d[c * 128:(c + 1) * 128, :]).then_inc(S["lw"], 16)
            for c in range(8):
                g.dma_start(out=wkT[c][:], in_=wkT_d[c * 128:(c + 1) * 128, :]).then_inc(S["lw"], 16)
            for c in range(8):
                g.dma_start(out=wvT[c][:], in_=wvT_d[c * 128:(c + 1) * 128, :]).then_inc(S["lw"], 16)
            g.memset(onesrow_b[:], 0.0)
            g.memset(onesrow_b[0:1, :], 1.0).then_inc(S["gi"], 1)
            g.memset(onescol_b[:], 1.0).then_inc(S["gi"], 1)
            g.memset(rowbuf_b[:], 0.0).then_inc(S["gi"], 1)

        @block.sync
        def _(sync):
            for tb in range(NTB):
                if tb > 0:
                    sync.wait_ge(S["mx"], 8 * tb)
                for c in range(8):
                    sync.dma_start(out=xfin[c][:],
                                   in_=xfT[c * 128:(c + 1) * 128, tb * TB:(tb + 1) * TB + 1]
                                   ).then_inc(S["xfl"], 16)
                if tb > 0:
                    sync.wait_ge(S["x4d"], 8 * tb)
                for c in range(8):
                    sync.dma_start(out=xf3[c][:],
                                   in_=x3T[c * 128:(c + 1) * 128, tb * TB:(tb + 1) * TB]
                                   ).then_inc(S["x3l"], 16)
                sync.wait_ge(S["x4d"], 8 * (tb + 1))
                for c in range(8):
                    sync.dma_start(out=x4T_d[c * 128:(c + 1) * 128, tb * TB:(tb + 1) * TB],
                                   in_=x4f[c][:]).then_inc(S["so"], 16)
            sync.wait_ge(S["so"], 16 * 8 * NTB)

        @block.tensor
        def _(t):
            t.wait_ge(S["lw"], 16 * 24)
            t.wait_ge(S["gi"], 3)
            for tb in range(NTB):
                t.wait_ge(S["mx"], 8 * (tb + 1))
                for e in range(8):
                    if e >= 2:
                        t.wait_ge(S["yes"], 16 * tb + e - 1)
                    elif tb > 0:
                        t.wait_ge(S["yev"], 8 * tb + e - 1)
                    bank = ps_a if (24 * tb + e) % 2 == 0 else ps_b
                    for c in range(8):
                        i = t.matmul(bank[:], wrT[c][:, e * 128:(e + 1) * 128], xrb[c][:],
                                     start=(c == 0), stop=(c == 7))
                    i.then_inc(S["mv"], 1)
                for e in range(8):
                    if e >= 2:
                        t.wait_ge(S["yes"], 16 * tb + 8 + e - 1)
                    else:
                        t.wait_ge(S["yes"], 16 * tb + e + 7)
                    bank = ps_a if (24 * tb + 8 + e) % 2 == 0 else ps_b
                    for c in range(8):
                        i = t.matmul(bank[:], wkT[c][:, e * 128:(e + 1) * 128], xkb[c][:],
                                     start=(c == 0), stop=(c == 7))
                    i.then_inc(S["mv"], 1)
                t.wait_ge(S["ksq"], 8 * (tb + 1))
                for e in range(8):
                    if e >= 2:
                        t.wait_ge(S["yev"], 8 * tb + e - 1)
                    else:
                        t.wait_ge(S["yes"], 16 * tb + e + 15)
                    bank = ps_a if (24 * tb + 16 + e) % 2 == 0 else ps_b
                    for c in range(8):
                        i = t.matmul(bank[:], wvT[c][:, e * 128:(e + 1) * 128], kab[c][:],
                                     start=(c == 0), stop=(c == 7))
                    i.then_inc(S["mv"], 1)
                t.wait_ge(S["sq"], 8 * (tb + 1))
                if tb > 0:
                    t.wait_ge(S["rm1"], tb)
                    t.wait_ge(S["rowm"], 2 * tb)
                for c in range(8):
                    t.matmul(ps_stat[:, 0:TB], onescol_b[:], yb[c][:],
                             start=(c == 0), stop=(c == 7))
                for c in range(8):
                    i = t.matmul(ps_stat[:, TB:2 * TB], onescol_b[:], ysq[c][:],
                                 start=(c == 0), stop=(c == 7))
                i.then_inc(S["st"], 1)
                t.wait_ge(S["rowm"], 2 * tb + 1)
                t.matmul(ps_bc[:], onesrow_b[:], mrow_b[0:32, :], start=True, stop=True)
                i = t.matmul(ps_bc2[:], onesrow_b[:], arow_b[0:32, :], start=True, stop=True)
                i.then_inc(S["bc"], 1)

        @block.scalar
        def _(sc):
            for tb in range(NTB):
                for e in range(8):
                    sc.wait_ge(S["mv"], 24 * tb + e + 1)
                    bank = ps_a if (24 * tb + e) % 2 == 0 else ps_b
                    sc.activation(rb[e][:], bank[:], AF.Sigmoid).then_inc(S["yes"], 1)
                for e in range(8):
                    sc.wait_ge(S["mv"], 24 * tb + 8 + e + 1)
                    bank = ps_a if (24 * tb + 8 + e) % 2 == 0 else ps_b
                    sc.activation(krl[e][:], bank[:], AF.Relu).then_inc(S["yes"], 1)
                    sc.activation(kab[e][:], krl[e][:], AF.Square).then_inc(S["ksq"], 1)
                for e in range(8):
                    sc.wait_ge(S["yev"], 8 * tb + e + 1)
                    sc.activation(ysq[e][:], yb[e][:], AF.Square).then_inc(S["sq"], 1)
                sc.wait_ge(S["rm1"], tb + 1)
                sc.activation(stdrow, s2row, AF.Sqrt).then_inc(S["sqd"], 1)

        @block.vector
        def _(v):
            v.wait_ge(S["cw"], 16)
            v.wait_ge(S["gi"], 3)
            for tb in range(NTB):
                v.wait_ge(S["xfl"], 128 * (tb + 1))
                if tb > 0:
                    v.wait_ge(S["mv"], 24 * (tb - 1) + 16)
                for c in range(8):
                    v.tensor_scalar(tmpm[:], xfin[c][:, 0:TB], mu_sb[:, 8 + c:9 + c],
                                    None, ALU.mult)
                    v.scalar_tensor_tensor(xrb[c][:], xfin[c][:, 1:TB + 1], mu_sb[:, c:c + 1],
                                           tmpm[:], ALU.mult, ALU.add)
                    v.tensor_scalar(tmpm[:], xfin[c][:, 0:TB], mu_sb[:, 24 + c:25 + c],
                                    None, ALU.mult)
                    v.scalar_tensor_tensor(xkb[c][:], xfin[c][:, 1:TB + 1], mu_sb[:, 16 + c:17 + c],
                                           tmpm[:], ALU.mult, ALU.add).then_inc(S["mx"], 1)
                for e in range(8):
                    v.wait_ge(S["mv"], 24 * tb + 16 + e + 1)
                    v.wait_ge(S["yes"], 16 * tb + e + 1)
                    if tb > 0 and e == 0:
                        v.wait_ge(S["so"], 16 * 8 * tb)
                    bank = ps_a if (24 * tb + 16 + e) % 2 == 0 else ps_b
                    v.tensor_mul(yb[e][:], bank[:], rb[e][:]).then_inc(S["yev"], 1)
                v.wait_ge(S["st"], tb + 1)
                v.tensor_scalar_mul(srow, ps_stat[:, 0:TB], 1.0 / 1024)
                v.tensor_scalar_mul(s2row, ps_stat[:, TB:2 * TB], 1.0 / 1024)
                v.tensor_mul(vrow, srow, srow)
                v.tensor_sub(s2row, s2row, vrow)
                v.tensor_scalar_add(s2row, s2row, EPS).then_inc(S["rm1"], 1)
                v.wait_ge(S["sqd"], tb + 1)
                v.reciprocal(arow, stdrow)
                v.tensor_copy(mrow_b[0:1, :], srow)
                v.tensor_copy(arow_b[0:1, :], arow).then_inc(S["rowm"], 1)
                v.wait_ge(S["bc"], tb + 1)
                v.wait_ge(S["x3l"], 128 * (tb + 1))
                v.tensor_copy(ab_sb[:], ps_bc2[:])
                v.tensor_copy(mb_sb[:], ps_bc[:]).then_inc(S["rowm"], 1)
                for c in range(8):
                    v.tensor_sub(tmp1[:], yb[c][:], mb_sb[:])
                    v.tensor_mul(tmp1[:], tmp1[:], ab_sb[:])
                    v.tensor_add(x4f[c][:], xf3[c][:], tmp1[:]).then_inc(S["x4d"], 1)

    print("RWKV sbuf remaining:", nc.sbuf_bytes_remaining)
    return nc


def run_rwkv(inputs, xf_full, x3_full):
    nc = _get("rwkv", build_rwkv)
    TH = T // 2
    mu_r = np.asarray(inputs["L1_ffn_mu_r"], np.float32)
    mu_k = np.asarray(inputs["L1_ffn_mu_k"], np.float32)
    mu = np.zeros((128, 32), np.float32)
    mu[:, 0:8] = mu_r.reshape(8, 128).T
    mu[:, 8:16] = (1.0 - mu_r).reshape(8, 128).T
    mu[:, 16:24] = mu_k.reshape(8, 128).T
    mu[:, 24:32] = (1.0 - mu_k).reshape(8, 128).T
    wrT = np.ascontiguousarray(np.asarray(inputs["L1_ffn_W_r"], np.float32).T)
    wkT = np.ascontiguousarray(np.asarray(inputs["L1_ffn_W_k"], np.float32).T)
    wvT = np.ascontiguousarray(np.asarray(inputs["L1_ffn_W_v"], np.float32).T)
    in_maps = []
    for c in range(NCORE):
        b, p = c // 2, c % 2
        xfTc = np.zeros((E, TH + 1), np.float32)
        xfTc[:, 1:] = xf_full[b][:, p * TH:(p + 1) * TH]
        if p == 1:
            xfTc[:, 0] = xf_full[b][:, TH - 1]
        in_maps.append({
            "xfT": xfTc,
            "x3T": np.ascontiguousarray(x3_full[b][:, p * TH:(p + 1) * TH]),
            "wrT": wrT, "wkT": wkT, "wvT": wvT,
            "mucols": mu,
        })
    res = run_bass_kernel_spmd(nc, in_maps, core_ids=list(range(NCORE)))
    out = np.zeros((4, E, T), np.float32)
    for c in range(NCORE):
        b, p = c // 2, c % 2
        out[b][:, p * TH:(p + 1) * TH] = np.asarray(res.results[c]["x4T"])
    return out


def kernel(**inputs):
    TH = T // 2
    x = np.asarray(inputs["L0_attn_x"], np.float32)
    xT_percore = [np.ascontiguousarray(x[c // 2].T) for c in range(NCORE)]

    mla_outs = run_mla(inputs, xT_percore)
    x1h, r0h = run_mlafin(inputs, xT_percore, mla_outs)

    r = np.zeros((4, E, T), np.float32)
    for c in range(NCORE):
        b, p = c // 2, c % 2
        r[b][:, p * TH:(p + 1) * TH] = r0h[c]
    for L in range(6):
        r = run_iattn(inputs, L, r)
        r = run_iffn(inputs, L, r)

    r6h = [np.ascontiguousarray(r[c // 2][:, (c % 2) * TH:((c % 2) + 1) * TH]).astype(BF)
           for c in range(NCORE)]
    eye = np.eye(E, dtype=np.float32)
    x2h, xn3h = run_addln(x1h, r6h, eye)

    xn3T_percore = [np.ascontiguousarray(np.concatenate([xn3h[2 * (c // 2)],
                                                         xn3h[2 * (c // 2) + 1]], axis=1))
                    for c in range(NCORE)]
    nsa_outs = run_nsa(inputs, xn3T_percore)
    moh = []
    for c in range(NCORE):
        b, p = c // 2, c % 2
        o1T = np.concatenate([nsa_outs[2 * b], nsa_outs[2 * b + 1]], axis=0)
        moh.append(np.ascontiguousarray(o1T[:, p * TH:(p + 1) * TH]))
    x3h, xfh = run_addln(x2h, moh, inputs["L1_attn_Wo"], inputs["L1_attn_bo"])

    xf_full = np.zeros((4, E, T), np.float32)
    x3_full = np.zeros((4, E, T), np.float32)
    for c in range(NCORE):
        b, p = c // 2, c % 2
        xf_full[b][:, p * TH:(p + 1) * TH] = xfh[c]
        x3_full[b][:, p * TH:(p + 1) * TH] = x3h[c]
    x4 = run_rwkv(inputs, xf_full, x3_full)
    return np.ascontiguousarray(x4.transpose(0, 2, 1))




# version 03
# speedup vs baseline: 1.0014x; 1.0014x over previous
import numpy as np
import ml_dtypes
import concourse.bass as bass
import concourse.mybir as mybir
from concourse.bass_utils import run_bass_kernel_spmd

F32 = mybir.dt.float32
BF16 = mybir.dt.bfloat16
AF = mybir.ActivationFunctionType
ALU = mybir.AluOpType
AX = mybir.AxisListType
ET = mybir.EngineType

BF = ml_dtypes.bfloat16
T = 4096
E = 1024
NCORE = 8
EPS = 1e-5
C0 = 30.0


def attn_plan(qb):
    lmin = 256 * (2 * qb + 1)
    ch = [(kc, kc * 128, 0, 512) for kc in range(lmin // 128)]
    ch += [(lmin // 128 + j, lmin + j * 128, 256, 256) for j in range(2)]
    return ch


PTRING = 16


def build_mla():
    nc = bass.Bass("TRN2", target_bir_lowering=False, num_devices=NCORE)
    xT = nc.dram_tensor("xT", [E, T], F32, kind="ExternalInput").ap()
    wq = nc.dram_tensor("wq", [E, 512], F32, kind="ExternalInput").ap()
    wdn = nc.dram_tensor("wdn", [E, 512], F32, kind="ExternalInput").ap()
    wku = nc.dram_tensor("wku", [512, 512], F32, kind="ExternalInput").ap()
    wvu = nc.dram_tensor("wvu", [512, 512], F32, kind="ExternalInput").ap()
    ropeC = nc.dram_tensor("ropeC", [128, T], BF16, kind="ExternalInput").ap()
    ropeS = nc.dram_tensor("ropeS", [128, T], BF16, kind="ExternalInput").ap()
    swp = nc.dram_tensor("swp", [128, 128], BF16, kind="ExternalInput").ap()
    outT_d = nc.dram_tensor("outT", [512, T], BF16, kind="ExternalOutput").ap()

    TB = 256
    NTB = T // TB

    xbfr = [nc.alloc_sbuf_tensor(f"xbfr{c}", [128, TB], BF16) for c in range(8)]
    xnbf = [nc.alloc_sbuf_tensor(f"xnbf{c}", [128, TB], BF16) for c in range(8)]
    rowbuf = nc.alloc_sbuf_tensor("rowbuf", [1, 6 * TB], F32)
    srow = rowbuf[:, 0:TB]
    s2row = rowbuf[:, TB:2 * TB]
    vrow = rowbuf[:, 2 * TB:3 * TB]
    stdrow = rowbuf[:, 3 * TB:4 * TB]
    arow = rowbuf[:, 4 * TB:5 * TB]
    rowbuf_b = nc.alloc_sbuf_tensor("rowbuf_b", [32, 2 * TB], BF16)
    mrow_b = rowbuf_b[:, 0:TB]
    arow_b = rowbuf_b[:, TB:2 * TB]
    ab_sb = nc.alloc_sbuf_tensor("ab_sb", [128, TB], F32)
    onesrow_b = nc.alloc_sbuf_tensor("onesrow_b", [32, 128], BF16)
    onesrow_f = nc.alloc_sbuf_tensor("onesrow_f", [96, 64], F32)
    onescol_b = nc.alloc_sbuf_tensor("onescol_b", [128, 1], BF16)
    negc0 = nc.alloc_sbuf_tensor("negc0", [128, 1], F32)

    wq_sb = [nc.alloc_sbuf_tensor(f"wqs{c}", [128, 512], BF16) for c in range(8)]
    wdn_sb = [nc.alloc_sbuf_tensor(f"wdns{c}", [128, 512], BF16) for c in range(8)]
    wku_sb = [nc.alloc_sbuf_tensor(f"wkus{c}", [128, 512], BF16) for c in range(4)]
    wvu_sb = [nc.alloc_sbuf_tensor(f"wvus{c}", [128, 512], BF16) for c in range(4)]

    qT = [nc.alloc_sbuf_tensor(f"qT{i}", [128, T], BF16) for i in range(4)]
    kT = [nc.alloc_sbuf_tensor(f"kT{i}", [128, T], BF16) for i in range(4)]
    latT = [nc.alloc_sbuf_tensor(f"latT{i}", [128, T], BF16) for i in range(4)]
    vp = [nc.alloc_sbuf_tensor(f"vp{h}", [128, 32 * 65], BF16) for h in range(8)]
    swp_sb = nc.alloc_sbuf_tensor("swp_sb", [128, 128], BF16)
    rtmp0 = nc.alloc_sbuf_tensor("rtmp0", [128, 512], BF16)
    rtmp1 = nc.alloc_sbuf_tensor("rtmp1", [128, 512], BF16)
    pt = nc.alloc_sbuf_tensor("pt", [128, PTRING * 512], BF16)
    invsb = nc.alloc_sbuf_tensor("invsb", [96, 512], F32)
    invb_sb = nc.alloc_sbuf_tensor("invb_sb", [64, 512], F32)
    obuf = [nc.alloc_sbuf_tensor(f"obuf{i}", [64, 512], BF16) for i in range(2)]
    pt_addr = nc.lookup_mloc(pt).addr
    cCb = nc.alloc_sbuf_tensor_at("cCb", [128, T], BF16, offset=pt_addr)
    cSb = nc.alloc_sbuf_tensor_at("cSb", [128, T], BF16, offset=pt_addr + 8192)

    ps_a = nc.alloc_psum_tensor("ps_a", [128, 512], F32)
    ps_b = nc.alloc_psum_tensor("ps_b", [128, 512], F32)
    ps_stat = nc.alloc_psum_tensor("ps_stat", [1, 512], F32)
    ps_bc = nc.alloc_psum_tensor("ps_bc", [128, TB], F32)
    ps_s0 = nc.alloc_psum_tensor("ps_s0", [128, 512], F32)
    ps_s1 = nc.alloc_psum_tensor("ps_s1", [128, 512], F32)
    ps_o = nc.alloc_psum_tensor("ps_o", [128, 512], F32)
    ps_b2 = nc.alloc_psum_tensor("ps_b2", [64, 512], F32)

    S = {n: nc.alloc_semaphore(n) for n in
         ["lw", "gi", "xl", "sq", "st", "rm1", "sqd", "rowm", "bcm",
          "xnd", "fx", "bca", "aok", "qkm", "pj", "kvm", "kvev", "sw", "rpa",
          "rp", "qk", "ex", "pvc", "pv", "nmr", "nmc", "nm", "bcv", "so",
          "so0", "so1"]}
    NW = 27

    with nc.Block() as block:

        @block.sync
        def _(sync):
            for h in range(8):
                for qb in range(8):
                    idx = h * 8 + qb
                    sync.wait_ge(S["nm"], idx + 1)
                    sync.dma_start(out=outT_d[h * 64:(h + 1) * 64, qb * 512:(qb + 1) * 512],
                                   in_=obuf[idx % 2][:]).then_inc(S[f"so{idx % 2}"], 16)
            sync.wait_ge(S["so0"], 16 * 32)
            sync.wait_ge(S["so1"], 16 * 32)

        @block.gpsimd
        def _(g):
            for c in range(8):
                g.dma_start(out=wq_sb[c][:], in_=wq[c * 128:(c + 1) * 128, :]).then_inc(S["lw"], 16)
            for c in range(8):
                g.dma_start(out=wdn_sb[c][:], in_=wdn[c * 128:(c + 1) * 128, :]).then_inc(S["lw"], 16)
            for c in range(4):
                g.dma_start(out=wku_sb[c][:], in_=wku[c * 128:(c + 1) * 128, :]).then_inc(S["lw"], 16)
            for c in range(4):
                g.dma_start(out=wvu_sb[c][:], in_=wvu[c * 128:(c + 1) * 128, :]).then_inc(S["lw"], 16)
            g.dma_start(out=cCb[:], in_=ropeC[:]).then_inc(S["lw"], 16)
            g.dma_start(out=cSb[:], in_=ropeS[:]).then_inc(S["lw"], 16)
            g.dma_start(out=swp_sb[:], in_=swp[:]).then_inc(S["lw"], 16)
            g.memset(onesrow_b[:], 0.0)
            g.memset(onesrow_b[0:1, :], 1.0).then_inc(S["gi"], 1)
            g.memset(onesrow_f[64:96, :], 0.0)
            g.memset(onesrow_f[64:65, :], 1.0).then_inc(S["gi"], 1)
            g.memset(onescol_b[:], 1.0).then_inc(S["gi"], 1)
            g.memset(negc0[:], -C0).then_inc(S["gi"], 1)
            g.memset(rowbuf_b[:], 0.0).then_inc(S["gi"], 1)
            g.memset(invsb[64:96, :], 0.0).then_inc(S["gi"], 1)
            for h in range(8):
                g.memset(vp[h][:].rearrange("p (a b) -> p a b", b=65)[:, :, 64:65], 1.0
                         ).then_inc(S["gi"], 1)
            for tb in range(NTB):
                if tb > 0:
                    g.wait_ge(S["xnd"], 8 * tb)
                for c in range(8):
                    g.dma_start(out=xbfr[c][:],
                                in_=xT[c * 128:(c + 1) * 128, tb * TB:(tb + 1) * TB]
                                ).then_inc(S["xl"], 16)

        @block.scalar
        def _(sc):
            for tb in range(NTB):
                sc.wait_ge(S["xl"], 128 * (tb + 1))
                if tb > 0:
                    sc.wait_ge(S["qkm"], 8 * tb)
                for c in range(8):
                    sc.activation(xnbf[c][:], xbfr[c][:], AF.Square).then_inc(S["sq"], 1)
                sc.wait_ge(S["rm1"], tb + 1)
                sc.activation(stdrow, s2row, AF.Sqrt).then_inc(S["sqd"], 1)
            sc.wait_ge(S["gi"], 14)
            qk_n = 0
            pvc_n = 0
            for h in range(8):
                for qb in range(8):
                    plan = attn_plan(qb)
                    for (kc, ks, qo, ql) in plan:
                        qk_n += 1
                        sc.wait_ge(S["qk"], qk_n)
                        if kc >= PTRING:
                            sc.wait_ge(S["pvc"], pvc_n + kc - PTRING + 1)
                        bank = ps_s0 if ((qk_n - 1) % 2 == 0) else ps_s1
                        slot = kc % PTRING
                        sc.activation(pt[:, slot * 512 + qo: slot * 512 + qo + ql],
                                      bank[:, qo:qo + ql], AF.Exp,
                                      bias=negc0[:], scale=0.125).then_inc(S["ex"], 1)
                    pvc_n += len(plan)

        @block.tensor
        def _(t):
            t.wait_ge(S["lw"], 16 * NW)
            t.wait_ge(S["gi"], 14)
            pj_n = 0
            for tb in range(NTB):
                t.wait_ge(S["sq"], 8 * tb + 8)
                if tb > 0:
                    t.wait_ge(S["rowm"], 3 * tb)
                for c in range(8):
                    t.matmul(ps_stat[:, 0:TB], onescol_b[:], xbfr[c][:],
                             start=(c == 0), stop=(c == 7))
                for c in range(8):
                    i = t.matmul(ps_stat[:, TB:2 * TB], onescol_b[:], xnbf[c][:],
                                 start=(c == 0), stop=(c == 7))
                i.then_inc(S["st"], 1)
                t.wait_ge(S["rowm"], 3 * tb + 2)
                t.matmul(ps_bc[:], onesrow_b[:, 0:128], mrow_b[0:32, :], start=True, stop=True
                         ).then_inc(S["bcm"], 1)
                t.wait_ge(S["xnd"], 8 * tb + 8)
                t.wait_ge(S["rowm"], 3 * tb + 3)
                t.matmul(ps_bc[:], onesrow_b[:, 0:128], arow_b[0:32, :], start=True, stop=True
                         ).then_inc(S["bca"], 1)
                t.wait_ge(S["aok"], tb + 1)
                for j in range(8):
                    w_sb = wq_sb if j < 4 else wdn_sb
                    e = j % 4
                    bank = ps_a if pj_n % 2 == 0 else ps_b
                    if pj_n >= 2:
                        t.wait_ge(S["pj"], pj_n - 1)
                    for c in range(8):
                        i = t.matmul(bank[:, 0:TB], w_sb[c][:, e * 128:(e + 1) * 128],
                                     xnbf[c][:], start=(c == 0), stop=(c == 7))
                    i.then_inc(S["qkm"], 1)
                    pj_n += 1
            kv_n = 0
            for g in range(64):
                bank = ps_a if kv_n % 2 == 0 else ps_b
                if kv_n < 2:
                    t.wait_ge(S["pj"], 8 * NTB)
                else:
                    t.wait_ge(S["kvev"], kv_n - 1)
                if g < 32:
                    e, tbb = g // 8, g % 8
                    for c in range(4):
                        i = t.matmul(bank[:], wku_sb[c][:, e * 128:(e + 1) * 128],
                                     latT[c][:, tbb * 512:(tbb + 1) * 512],
                                     start=(c == 0), stop=(c == 3))
                else:
                    tc = g - 32
                    for c in range(4):
                        i = t.matmul(bank[:], latT[c][:, tc * 128:(tc + 1) * 128],
                                     wvu_sb[c][:], start=(c == 0), stop=(c == 3))
                i.then_inc(S["kvm"], 1)
                kv_n += 1
            sw_n = 0
            for r in range(64):
                src = qT if r < 32 else kT
                i4, blk = (r % 32) // 8, r % 8
                bank = ps_a if sw_n % 2 == 0 else ps_b
                if sw_n < 2:
                    t.wait_ge(S["kvev"], 64)
                else:
                    t.wait_ge(S["rpa"], sw_n - 1)
                t.matmul(bank[:], swp_sb[:], src[i4][:, blk * 512:(blk + 1) * 512],
                         start=True, stop=True).then_inc(S["sw"], 1)
                sw_n += 1
            t.wait_ge(S["rp"], 64)
            qk_n = 0
            ex_n = 0
            for h in range(8):
                i4, ro = h // 2, (h % 2) * 64
                for qb in range(8):
                    idx = h * 8 + qb
                    plan = attn_plan(qb)
                    npl = len(plan)
                    if idx > 0:
                        t.wait_ge(S["nm"], idx)
                    for pi, (kc, ks, qo, ql) in enumerate(plan):
                        bank = ps_s0 if (qk_n % 2 == 0) else ps_s1
                        if qk_n >= 2:
                            t.wait_ge(S["ex"], qk_n - 1)
                        t.matmul(bank[:, qo:qo + ql],
                                 kT[i4][ro:ro + 64, ks:ks + 128],
                                 qT[i4][ro:ro + 64, qb * 512 + qo: qb * 512 + qo + ql],
                                 start=True, stop=True).then_inc(S["qk"], 1)
                        qk_n += 1
                        if pi >= 1:
                            pkc, pks, pqo, pql = plan[pi - 1]
                            t.wait_ge(S["ex"], ex_n + pi)
                            slot = pkc % PTRING
                            i = t.matmul(ps_o[0:65, pqo:pqo + pql],
                                         vp[h][:, pkc * 65:(pkc + 1) * 65],
                                         pt[:, slot * 512 + pqo: slot * 512 + pqo + pql],
                                         start=(pi - 1 == 0), stop=False)
                            i.then_inc(S["pvc"], 1)
                    pkc, pks, pqo, pql = plan[npl - 1]
                    t.wait_ge(S["ex"], ex_n + npl)
                    slot = pkc % PTRING
                    i = t.matmul(ps_o[0:65, pqo:pqo + pql],
                                 vp[h][:, pkc * 65:(pkc + 1) * 65],
                                 pt[:, slot * 512 + pqo: slot * 512 + pqo + pql],
                                 start=False, stop=True)
                    i.then_inc(S["pvc"], 1)
                    ex_n += npl
                    t.wait_ge(S["nmr"], idx + 1)
                    if idx > 0:
                        t.wait_ge(S["nmc"], idx)
                    t.matmul(ps_b2[:], onesrow_f[64:96, :], invsb[64:96, :],
                             start=True, stop=True).then_inc(S["bcv"], 1)

        @block.vector
        def _(v):
            for tb in range(NTB):
                v.wait_ge(S["st"], tb + 1)
                v.tensor_scalar_mul(srow, ps_stat[:, 0:TB], 1.0 / 1024)
                v.tensor_scalar_mul(s2row, ps_stat[:, TB:2 * TB], 1.0 / 1024).then_inc(S["rowm"], 1)
                v.wait_ge(S["rowm"], 3 * tb + 1)
                v.tensor_mul(vrow, srow, srow)
                v.tensor_sub(s2row, s2row, vrow)
                v.tensor_scalar_add(s2row, s2row, EPS).then_inc(S["rm1"], 1)
                v.wait_ge(S["sqd"], tb + 1)
                v.reciprocal(arow, stdrow)
                v.tensor_copy(mrow_b[0:1, :], srow).then_inc(S["rowm"], 1)
                v.tensor_copy(arow_b[0:1, :], arow).then_inc(S["rowm"], 1)
                v.wait_ge(S["bcm"], tb + 1)
                for c in range(8):
                    v.tensor_sub(xnbf[c][:], xbfr[c][:], ps_bc[:]).then_inc(S["xnd"], 1)
                v.wait_ge(S["bca"], tb + 1)
                v.tensor_copy(ab_sb[:], ps_bc[:]).then_inc(S["aok"], 1)
                for j in range(8):
                    v.wait_ge(S["qkm"], 8 * tb + j + 1)
                    bank = ps_a if (8 * tb + j) % 2 == 0 else ps_b
                    dst = qT if j < 4 else latT
                    e = j % 4
                    v.tensor_mul(dst[e][:, tb * TB:(tb + 1) * TB], bank[:, 0:TB], ab_sb[:]
                                 ).then_inc(S["pj"], 1)
            for g in range(64):
                v.wait_ge(S["kvm"], g + 1)
                bank = ps_a if g % 2 == 0 else ps_b
                if g < 32:
                    e, tbb = g // 8, g % 8
                    i = v.tensor_copy(kT[e][:, tbb * 512:(tbb + 1) * 512], bank[:])
                else:
                    tc = g - 32
                    for h in range(8):
                        i = v.tensor_copy(vp[h][:, tc * 65: tc * 65 + 64],
                                          bank[:, h * 64:(h + 1) * 64])
                i.then_inc(S["kvev"], 1)
            for r in range(64):
                src = qT if r < 32 else kT
                i4, blk = (r % 32) // 8, r % 8
                bank = ps_a if r % 2 == 0 else ps_b
                v.wait_ge(S["sw"], r + 1)
                v.tensor_copy(rtmp0[:], bank[:]).then_inc(S["rpa"], 1)
                v.tensor_mul(rtmp1[:], src[i4][:, blk * 512:(blk + 1) * 512],
                             cCb[:, blk * 512:(blk + 1) * 512])
                v.wait_ge(S["rpa"], r + 1)
                v.tensor_mul(rtmp0[:], rtmp0[:], cSb[:, blk * 512:(blk + 1) * 512])
                v.tensor_add(src[i4][:, blk * 512:(blk + 1) * 512], rtmp0[:], rtmp1[:]
                             ).then_inc(S["rp"], 1)
            cum = 0
            for h in range(8):
                for qb in range(8):
                    idx = h * 8 + qb
                    cum += len(attn_plan(qb))
                    v.wait_ge(S["pvc"], cum)
                    v.reciprocal(invsb[64:65, :], ps_o[64:65, :]).then_inc(S["nmr"], 1)
                    v.wait_ge(S["bcv"], idx + 1)
                    v.tensor_copy(invb_sb[:], ps_b2[:]).then_inc(S["nmc"], 1)
                    v.wait_ge(S["nmc"], idx + 1)
                    if idx >= 2:
                        v.wait_ge(S[f"so{idx % 2}"], 16 * (idx // 2))
                    v.tensor_mul(obuf[idx % 2][:], ps_o[0:64, :], invb_sb[:]
                                 ).then_inc(S["nm"], 1)

    print("MLA sbuf remaining:", nc.sbuf_bytes_remaining)
    return nc



def _mla_rope_tables():
    inv = 1.0 / 10000.0 ** (np.arange(0, 64, 2) / 64.0)
    ang = np.outer(inv, np.arange(T, dtype=np.float64))
    c, s = np.cos(ang), np.sin(ang)
    C = np.empty((64, T), np.float32)
    Sg = np.empty((64, T), np.float32)
    C[0::2] = c
    C[1::2] = c
    Sg[0::2] = -s
    Sg[1::2] = s
    return (np.concatenate([C, C], 0).astype(BF),
            np.concatenate([Sg, Sg], 0).astype(BF))


def _mla_swap():
    m = np.zeros((128, 128), np.float32)
    for i in range(0, 128, 2):
        m[i + 1, i] = 1.0
        m[i, i + 1] = 1.0
    return m.astype(BF)


_CACHE = {}


def _get(name, builder):
    if name not in _CACHE:
        _CACHE[name] = builder()
    return _CACHE[name]


def run_mla(inputs, xT_percore):
    wq = inputs["L0_attn_wq"]
    wdn = inputs["L0_attn_w_down"]
    wku = inputs["L0_attn_wk_up"]
    wvu = inputs["L0_attn_wv_up"]
    C2, S2 = _get("rope", _mla_rope_tables)
    swp = _get("swap", _mla_swap)
    nc = _get("mla", build_mla)
    in_maps = []
    for c in range(NCORE):
        b, p = c // 2, c % 2
        hs = slice(p * 512, (p + 1) * 512)
        in_maps.append({
            "xT": xT_percore[c],
            "wq": np.ascontiguousarray(wq[:, hs]),
            "wdn": np.ascontiguousarray(wdn),
            "wku": np.ascontiguousarray(wku[:, hs]),
            "wvu": np.ascontiguousarray(wvu[:, hs]),
            "ropeC": C2, "ropeS": S2, "swp": swp,
        })
    res = run_bass_kernel_spmd(nc, in_maps, core_ids=list(range(NCORE)))
    return [np.asarray(res.results[c]["outT"]) for c in range(NCORE)]




def build_mlafin():
    nc = bass.Bass("TRN2", target_bir_lowering=False, num_devices=NCORE)
    TH = T // 2
    xT = nc.dram_tensor("xT", [E, TH], F32, kind="ExternalInput").ap()
    moT = nc.dram_tensor("moT", [E, TH], BF16, kind="ExternalInput").ap()
    wo = nc.dram_tensor("wo", [E, E], F32, kind="ExternalInput").ap()
    bo_d = nc.dram_tensor("bo", [128, 8], F32, kind="ExternalInput").ap()
    x1T_d = nc.dram_tensor("x1T", [E, TH], F32, kind="ExternalOutput").ap()
    r0T_d = nc.dram_tensor("r0T", [E, TH], F32, kind="ExternalOutput").ap()

    TB = 512
    NTB = TH // TB

    mo_sb = [nc.alloc_sbuf_tensor(f"mo{c}", [128, TH], BF16) for c in range(8)]
    wo_sb = [nc.alloc_sbuf_tensor(f"wos{c}", [128, E], BF16) for c in range(8)]
    xf = [nc.alloc_sbuf_tensor(f"xf{c}", [128, TB], F32) for c in range(8)]
    yb = [nc.alloc_sbuf_tensor(f"yb{c}", [128, TB], BF16) for c in range(8)]
    ysq = [nc.alloc_sbuf_tensor(f"ysq{c}", [128, TB], BF16) for c in range(8)]
    x1f = [nc.alloc_sbuf_tensor(f"x1f{c}", [128, TB], F32) for c in range(8)]
    x1b = [nc.alloc_sbuf_tensor(f"x1b{c}", [128, TB], BF16) for c in range(8)]
    tmpf = [nc.alloc_sbuf_tensor(f"tmpf{c}", [128, TB], F32) for c in range(8)]
    rowbuf = nc.alloc_sbuf_tensor("rowbuf", [1, 6 * TB], F32)
    srow = rowbuf[:, 0:TB]
    s2row = rowbuf[:, TB:2 * TB]
    vrow = rowbuf[:, 2 * TB:3 * TB]
    stdrow = rowbuf[:, 3 * TB:4 * TB]
    arow = rowbuf[:, 4 * TB:5 * TB]
    rowbuf_b = nc.alloc_sbuf_tensor("rowbuf_b", [32, 2 * TB], BF16)
    mrow_b = rowbuf_b[:, 0:TB]
    arow_b = rowbuf_b[:, TB:2 * TB]
    ab_sb = nc.alloc_sbuf_tensor("ab_sb", [128, TB], F32)
    mb_sb = nc.alloc_sbuf_tensor("mb_sb", [128, TB], F32)
    onesrow_b = nc.alloc_sbuf_tensor("onesrow_b", [32, 128], BF16)
    onescol_b = nc.alloc_sbuf_tensor("onescol_b", [128, 1], BF16)
    bo_sb = nc.alloc_sbuf_tensor("bo_sb", [128, 8], F32)

    ps_a = nc.alloc_psum_tensor("ps_a", [128, 512], F32)
    ps_b = nc.alloc_psum_tensor("ps_b", [128, 512], F32)
    ps_stat = nc.alloc_psum_tensor("ps_stat", [1, 2 * TB], F32)
    ps_bc = nc.alloc_psum_tensor("ps_bc", [128, TB], F32)
    ps_bc2 = nc.alloc_psum_tensor("ps_bc2", [128, TB], F32)

    S = {n: nc.alloc_semaphore(n) for n in
         ["lw", "gi", "xl", "mv", "ye", "sq", "st", "rm1", "sqd", "rowm", "bc",
          "x1d", "x1s", "sq2", "st2", "rm12", "sqd2", "rowm2", "bc2", "r0d",
          "so", "fx"]}

    with nc.Block() as block:

        @block.gpsimd
        def _(g):
            for c in range(8):
                g.dma_start(out=wo_sb[c][:], in_=wo[c * 128:(c + 1) * 128, :]).then_inc(S["lw"], 16)
            for c in range(8):
                g.dma_start(out=mo_sb[c][:], in_=moT[c * 128:(c + 1) * 128, :]).then_inc(S["lw"], 16)
            g.dma_start(out=bo_sb[:], in_=bo_d[:]).then_inc(S["lw"], 16)
            g.memset(onesrow_b[:], 0.0)
            g.memset(onesrow_b[0:1, :], 1.0).then_inc(S["gi"], 1)
            g.memset(onescol_b[:], 1.0).then_inc(S["gi"], 1)
            g.memset(rowbuf_b[:], 0.0).then_inc(S["gi"], 1)

        @block.sync
        def _(sync):
            for tb in range(NTB):
                if tb > 0:
                    sync.wait_ge(S["x1d"], 8 * tb)
                for c in range(8):
                    sync.dma_start(out=xf[c][:],
                                   in_=xT[c * 128:(c + 1) * 128, tb * TB:(tb + 1) * TB]
                                   ).then_inc(S["xl"], 16)
                sync.wait_ge(S["x1d"], 8 * (tb + 1))
                for c in range(8):
                    sync.dma_start(out=x1T_d[c * 128:(c + 1) * 128, tb * TB:(tb + 1) * TB],
                                   in_=x1f[c][:]).then_inc(S["so"], 16)
                sync.wait_ge(S["r0d"], 8 * (tb + 1))
                for c in range(8):
                    sync.dma_start(out=r0T_d[c * 128:(c + 1) * 128, tb * TB:(tb + 1) * TB],
                                   in_=tmpf[c][:]).then_inc(S["so"], 16)
            sync.wait_ge(S["so"], 16 * 16 * NTB)

        @block.tensor
        def _(t):
            t.wait_ge(S["lw"], 16 * 17)
            t.wait_ge(S["gi"], 3)
            pj = 0
            for tb in range(NTB):
                for e in range(8):
                    bank = ps_a if pj % 2 == 0 else ps_b
                    if pj >= 2:
                        t.wait_ge(S["ye"], pj - 1)
                    for c in range(8):
                        i = t.matmul(bank[:], wo_sb[c][:, e * 128:(e + 1) * 128],
                                     mo_sb[c][:, tb * TB:(tb + 1) * TB],
                                     start=(c == 0), stop=(c == 7))
                    i.then_inc(S["mv"], 1)
                    pj += 1
                t.wait_ge(S["sq"], 8 * (tb + 1))
                if tb > 0:
                    t.wait_ge(S["rm12"], tb)
                    t.wait_ge(S["rowm2"], 2 * tb)
                for c in range(8):
                    t.matmul(ps_stat[:, 0:TB], onescol_b[:], yb[c][:],
                             start=(c == 0), stop=(c == 7))
                for c in range(8):
                    i = t.matmul(ps_stat[:, TB:2 * TB], onescol_b[:], ysq[c][:],
                                 start=(c == 0), stop=(c == 7))
                i.then_inc(S["st"], 1)
                t.wait_ge(S["rowm"], 2 * tb + 1)
                t.matmul(ps_bc[:], onesrow_b[:], mrow_b[0:32, :], start=True, stop=True)
                i = t.matmul(ps_bc2[:], onesrow_b[:], arow_b[0:32, :], start=True, stop=True)
                i.then_inc(S["bc"], 1)
                t.wait_ge(S["sq2"], 8 * (tb + 1))
                t.wait_ge(S["rm1"], tb + 1)
                for c in range(8):
                    t.matmul(ps_stat[:, 0:TB], onescol_b[:], x1b[c][:],
                             start=(c == 0), stop=(c == 7))
                for c in range(8):
                    i = t.matmul(ps_stat[:, TB:2 * TB], onescol_b[:], ysq[c][:],
                                 start=(c == 0), stop=(c == 7))
                i.then_inc(S["st2"], 1)
                t.wait_ge(S["rowm"], 2 * tb + 2)
                t.wait_ge(S["rowm2"], 2 * tb + 1)
                t.matmul(ps_bc[:], onesrow_b[:], mrow_b[0:32, :], start=True, stop=True)
                i = t.matmul(ps_bc2[:], onesrow_b[:], arow_b[0:32, :], start=True, stop=True)
                i.then_inc(S["bc2"], 1)

        @block.scalar
        def _(sc):
            for tb in range(NTB):
                sc.wait_ge(S["ye"], 8 * tb + 8)
                for c in range(8):
                    sc.activation(ysq[c][:], yb[c][:], AF.Square).then_inc(S["sq"], 1)
                sc.wait_ge(S["rm1"], tb + 1)
                sc.activation(stdrow, s2row, AF.Sqrt).then_inc(S["sqd"], 1)
                sc.wait_ge(S["x1s"], 8 * (tb + 1))
                for c in range(8):
                    sc.activation(ysq[c][:], x1b[c][:], AF.Square).then_inc(S["sq2"], 1)
                sc.wait_ge(S["rm12"], tb + 1)
                sc.activation(stdrow, s2row, AF.Sqrt).then_inc(S["sqd2"], 1)

        @block.vector
        def _(v):
            for tb in range(NTB):
                if tb > 0:
                    v.wait_ge(S["so"], 16 * 16 * tb)
                for e in range(8):
                    v.wait_ge(S["mv"], 8 * tb + e + 1)
                    bank = ps_a if (8 * tb + e) % 2 == 0 else ps_b
                    v.tensor_scalar(yb[e][:], bank[:], bo_sb[:, e:e + 1], None, ALU.add
                                    ).then_inc(S["ye"], 1)
                v.wait_ge(S["st"], tb + 1)
                v.tensor_scalar_mul(srow, ps_stat[:, 0:TB], 1.0 / 1024)
                v.tensor_scalar_mul(s2row, ps_stat[:, TB:2 * TB], 1.0 / 1024)
                v.tensor_mul(vrow, srow, srow)
                v.tensor_sub(s2row, s2row, vrow)
                v.tensor_scalar_add(s2row, s2row, EPS).then_inc(S["rm1"], 1)
                v.wait_ge(S["sqd"], tb + 1)
                v.reciprocal(arow, stdrow)
                v.tensor_copy(mrow_b[0:1, :], srow)
                v.tensor_copy(arow_b[0:1, :], arow).then_inc(S["rowm"], 1)
                v.wait_ge(S["bc"], tb + 1)
                v.wait_ge(S["xl"], 128 * (tb + 1))
                v.tensor_copy(ab_sb[:], ps_bc2[:])
                v.tensor_copy(mb_sb[:], ps_bc[:]).then_inc(S["rowm"], 1)
                for c in range(8):
                    v.tensor_sub(tmpf[c][:], yb[c][:], mb_sb[:])
                for c in range(8):
                    v.tensor_mul(tmpf[c][:], tmpf[c][:], ab_sb[:])
                for c in range(8):
                    v.tensor_add(x1f[c][:], xf[c][:], tmpf[c][:]).then_inc(S["x1d"], 1)
                for c in range(8):
                    v.tensor_copy(x1b[c][:], x1f[c][:]).then_inc(S["x1s"], 1)
                v.wait_ge(S["st2"], tb + 1)
                v.tensor_scalar_mul(srow, ps_stat[:, 0:TB], 1.0 / 1024)
                v.tensor_scalar_mul(s2row, ps_stat[:, TB:2 * TB], 1.0 / 1024)
                v.tensor_mul(vrow, srow, srow)
                v.tensor_sub(s2row, s2row, vrow)
                v.tensor_scalar_add(s2row, s2row, EPS).then_inc(S["rm12"], 1)
                v.wait_ge(S["sqd2"], tb + 1)
                v.reciprocal(arow, stdrow)
                v.tensor_copy(mrow_b[0:1, :], srow)
                v.tensor_copy(arow_b[0:1, :], arow).then_inc(S["rowm2"], 1)
                v.wait_ge(S["bc2"], tb + 1)
                v.tensor_copy(ab_sb[:], ps_bc2[:])
                v.tensor_copy(mb_sb[:], ps_bc[:]).then_inc(S["rowm2"], 1)
                for c in range(8):
                    v.tensor_sub(tmpf[c][:], x1b[c][:], mb_sb[:])
                for c in range(8):
                    v.tensor_mul(tmpf[c][:], tmpf[c][:], ab_sb[:]).then_inc(S["r0d"], 1)

    print("MLAFIN sbuf remaining:", nc.sbuf_bytes_remaining)
    return nc


def run_addln(xT_half_percore, moT_half_percore, wo, bo=None):
    nc = _get("mlafin", build_mlafin)
    if bo is None:
        bocol = np.zeros((128, 8), np.float32)
    else:
        bocol = np.ascontiguousarray(np.asarray(bo, np.float32).reshape(8, 128).T)
    in_maps = []
    for c in range(NCORE):
        in_maps.append({
            "xT": xT_half_percore[c],
            "moT": moT_half_percore[c],
            "wo": np.ascontiguousarray(np.asarray(wo, np.float32)),
            "bo": bocol,
        })
    res = run_bass_kernel_spmd(nc, in_maps, core_ids=list(range(NCORE)))
    return ([np.asarray(res.results[c]["x1T"]) for c in range(NCORE)],
            [np.asarray(res.results[c]["r0T"]) for c in range(NCORE)])


def run_mlafin(inputs, xT_percore, mla_outs):
    TH = T // 2
    xh, mh = [], []
    for c in range(NCORE):
        b, p = c // 2, c % 2
        moT = np.concatenate([mla_outs[2 * b], mla_outs[2 * b + 1]], axis=0)
        xh.append(np.ascontiguousarray(xT_percore[c][:, p * TH:(p + 1) * TH]))
        mh.append(np.ascontiguousarray(moT[:, p * TH:(p + 1) * TH]))
    return run_addln(xh, mh, inputs["L0_attn_wo"])






IFFN_DEBUG = False


def build_iffn():
    nc = bass.Bass("TRN2", target_bir_lowering=False, num_devices=NCORE)
    TH = 2048
    rpT = nc.dram_tensor("rpT", [E, TH + 1], F32, kind="ExternalInput").ap()
    W1_d = nc.dram_tensor("W1p", [E, 5632], F32, kind="ExternalInput").ap()
    b1_d = nc.dram_tensor("b1bl", [128, 44], F32, kind="ExternalInput").ap()
    W2_d = nc.dram_tensor("W2p", [2816, E], F32, kind="ExternalInput").ap()
    b2_d = nc.dram_tensor("b2c", [128, 8], F32, kind="ExternalInput").ap()
    gc_d = nc.dram_tensor("gc", [128, 8], F32, kind="ExternalInput").ap()
    bb_d = nc.dram_tensor("bb", [128, 8], BF16, kind="ExternalInput").ap()
    grow_d = nc.dram_tensor("g_row", [32, E], BF16, kind="ExternalInput").ap()
    rnT_d = nc.dram_tensor("rnT", [E, TH], F32, kind="ExternalOutput").ap()
    if IFFN_DEBUG:
        dbg_d = nc.dram_tensor("dbg", [128, 1536], F32, kind="ExternalOutput").ap()

    TB = 512
    NTB = TH // TB

    x2b = [nc.alloc_sbuf_tensor(f"fx2{c}", [128, TH], BF16) for c in range(8)]
    xnb = x2b
    xsq = [nc.alloc_sbuf_tensor(f"fsq{c}", [128, TB], BF16) for c in range(8)]
    w1_sb = [nc.alloc_sbuf_tensor(f"fw1{c}", [128, 5632], BF16) for c in range(8)]
    w2buf = [nc.alloc_sbuf_tensor(f"fw2{i}", [128, E], BF16) for i in range(2)]
    pblk = [nc.alloc_sbuf_tensor(f"fp{j}", [128, TB], BF16) for j in range(22)]
    r1blk = [nc.alloc_sbuf_tensor(f"fr1{c}", [128, TB], F32) for c in range(8)]
    hA = nc.alloc_sbuf_tensor("fhA", [128, TB], F32)
    hG = nc.alloc_sbuf_tensor("fhG", [128, TB], F32)
    hG2 = nc.alloc_sbuf_tensor("fhG2", [128, TB], F32)
    rowbuf = nc.alloc_sbuf_tensor("frow", [1, 6 * TB], F32)
    srow = rowbuf[:, 0:TB]
    s2row = rowbuf[:, TB:2 * TB]
    vrow = rowbuf[:, 2 * TB:3 * TB]
    stdrow = rowbuf[:, 3 * TB:4 * TB]
    arow = rowbuf[:, 4 * TB:5 * TB]
    rowbuf_b = nc.alloc_sbuf_tensor("frow_b", [32, 2 * TB], BF16)
    mrow_b = rowbuf_b[:, 0:TB]
    arow_b = rowbuf_b[:, TB:2 * TB]
    gcol_sb = nc.alloc_sbuf_tensor("fgcol", [128, 8], F32)
    b2_sb = nc.alloc_sbuf_tensor("fb2", [128, 8], F32)
    bb_sb = nc.alloc_sbuf_tensor("fbb", [128, 8], BF16)
    grow_sb = nc.alloc_sbuf_tensor("fgrow", [32, E], BF16)
    b1_sb = nc.alloc_sbuf_tensor("fb1", [128, 44], F32)
    bW1_sb = nc.alloc_sbuf_tensor("fbW1", [128, 44], F32)
    btot_sb = nc.alloc_sbuf_tensor("fbtot", [128, 44], F32)
    ab_sb = nc.alloc_sbuf_tensor("fab", [128, TB], F32)
    dbg_sb = nc.alloc_sbuf_tensor("fdbg", [128, 1536], F32) if IFFN_DEBUG else None
    onescol_b = nc.alloc_sbuf_tensor("fones", [128, 1], BF16)
    onesrow_b = nc.alloc_sbuf_tensor("fonesr", [32, 128], BF16)

    B = [nc.alloc_psum_tensor(f"fB{i}", [128, 512], F32) for i in range(8)]

    S = {n: nc.alloc_semaphore("f_" + n) for n in
         ["lw", "gi", "xl", "sq", "st", "rm1", "sqd", "rw", "bcm", "xnd", "bca",
          "aok", "bwm", "bwv", "hm", "he", "gl", "pm", "w2l", "w2m", "ye", "r1l",
          "so", "w1l"]}

    with nc.Block() as block:

        @block.gpsimd
        def _(g):
            for c in range(8):
                off = 1 if c < 4 else 0
                g.dma_start(out=x2b[c][:], in_=rpT[c * 128:(c + 1) * 128, off:off + TH]
                            ).then_inc(S["xl"], 16)
            g.dma_start(out=gcol_sb[:], in_=gc_d[:]).then_inc(S["lw"], 16)
            g.dma_start(out=b2_sb[:], in_=b2_d[:]).then_inc(S["lw"], 16)
            g.dma_start(out=bb_sb[:], in_=bb_d[:]).then_inc(S["lw"], 16)
            g.dma_start(out=grow_sb[:], in_=grow_d[:]).then_inc(S["lw"], 16)
            g.dma_start(out=b1_sb[:], in_=b1_d[:]).then_inc(S["lw"], 16)
            g.memset(onescol_b[:], 1.0)
            if IFFN_DEBUG:
                g.memset(dbg_sb[:], 0.0)
            g.memset(onesrow_b[:], 0.0)
            g.memset(onesrow_b[0:1, :], 1.0)
            g.memset(rowbuf_b[:], 0.0).then_inc(S["gi"], 1)
            for c in range(8):
                g.dma_start(out=w1_sb[c][:], in_=W1_d[c * 128:(c + 1) * 128, :]).then_inc(S["w1l"], 16)
            for tb in range(NTB):
                if tb >= 1:
                    g.wait_ge(S["so"], 128 * tb)
                for c in range(8):
                    g.dma_start(out=r1blk[c][:],
                                in_=rpT[c * 128:(c + 1) * 128, 1 + tb * TB:1 + (tb + 1) * TB]
                                ).then_inc(S["r1l"], 16)
                for j in range(22):
                    gidx = tb * 22 + j
                    if gidx >= 2:
                        g.wait_ge(S["w2m"], gidx - 1)
                    g.dma_start(out=w2buf[gidx % 2][:],
                                in_=W2_d[j * 128:(j + 1) * 128, :]).then_inc(S["w2l"], 16)

        @block.sync
        def _(sync):
            for tb in range(NTB):
                sync.wait_ge(S["ye"], 8 * (tb + 1))
                for c in range(8):
                    sync.dma_start(out=rnT_d[c * 128:(c + 1) * 128, tb * TB:(tb + 1) * TB],
                                   in_=r1blk[c][:]).then_inc(S["so"], 16)
            if IFFN_DEBUG:
                sync.wait_ge(S["so"], 16 * 8 * NTB + 1)
                sync.dma_start(out=dbg_d, in_=dbg_sb[:]).then_inc(S["so"], 16)
                sync.wait_ge(S["so"], 16 * 8 * NTB + 17)
            else:
                sync.wait_ge(S["so"], 16 * 8 * NTB)

        @block.scalar
        def _(sc):
            sc.wait_ge(S["xl"], 16 * 8)
            for tb in range(NTB):
                if tb > 0:
                    sc.wait_ge(S["st"], tb)
                for c in range(8):
                    sc.activation(xsq[c][:], x2b[c][:, tb * TB:(tb + 1) * TB], AF.Square
                                  ).then_inc(S["sq"], 1)
                sc.wait_ge(S["rm1"], tb + 1)
                sc.activation(stdrow, s2row, AF.Sqrt).then_inc(S["sqd"], 1)
                for j in range(22):
                    gidx = tb * 22 + j
                    sc.wait_ge(S["he"], 2 * gidx + 2)
                    sc.activation(hG2[:], hG[:], AF.Gelu, bias=btot_sb[:, 2 * j + 1:2 * j + 2]
                                  ).then_inc(S["gl"], 1)

        @block.tensor
        def _(t):
            t.wait_ge(S["gi"], 1)
            t.wait_ge(S["lw"], 16 * 5)
            t.wait_ge(S["w1l"], 16 * 8)
            for j in range(44):
                for c in range(8):
                    i = t.matmul(B[4][:, j:j + 1], w1_sb[c][:, j * 128:(j + 1) * 128],
                                 bb_sb[:, c:c + 1], start=(c == 0), stop=(c == 7))
            i.then_inc(S["bwm"], 1)
            t.wait_ge(S["xl"], 16 * 8)
            hm_n = 0
            for tb in range(NTB):
                t.wait_ge(S["sq"], 8 * (tb + 1))
                if tb > 0:
                    t.wait_ge(S["rw"], 2 * tb)
                    t.wait_ge(S["ye"], 8 * tb)
                for c in range(8):
                    t.matmul(B[2][0:1, :], onescol_b[:], x2b[c][:, tb * TB:(tb + 1) * TB],
                             start=(c == 0), stop=(c == 7))
                for c in range(8):
                    i = t.matmul(B[3][0:1, :], onescol_b[:], xsq[c][:],
                                 start=(c == 0), stop=(c == 7))
                i.then_inc(S["st"], 1)
                t.wait_ge(S["rw"], 2 * tb + 2)
                if tb == 0:
                    t.wait_ge(S["bwv"], 1)
                else:
                    t.wait_ge(S["aok"], tb)
                for c in range(8):
                    t.wait_ge(S["xnd"], 8 * tb + c)
                    t.matmul(B[4][:, 0:TB], grow_sb[0:32, c * 128:(c + 1) * 128],
                             mrow_b[0:32, :], start=True, stop=True).then_inc(S["bcm"], 1)
                t.wait_ge(S["xnd"], 8 * (tb + 1))
                t.matmul(B[4][:, 0:TB], onesrow_b[:], arow_b[0:32, :], start=True, stop=True
                         ).then_inc(S["bca"], 1)
                t.wait_ge(S["aok"], tb + 1)
                for j in range(22):
                    gidx = tb * 22 + j
                    if gidx >= 1:
                        t.wait_ge(S["he"], 2 * gidx)
                    for c in range(8):
                        t.matmul(B[0][:], w1_sb[c][:, (2 * j) * 128:(2 * j + 1) * 128],
                                 xnb[c][:, tb * TB:(tb + 1) * TB], start=(c == 0), stop=(c == 7))
                    for c in range(8):
                        i = t.matmul(B[1][:], w1_sb[c][:, (2 * j + 1) * 128:(2 * j + 2) * 128],
                                     xnb[c][:, tb * TB:(tb + 1) * TB], start=(c == 0), stop=(c == 7))
                    i.then_inc(S["hm"], 1)
                t.wait_ge(S["pm"], 22 * (tb + 1))
                if tb > 0:
                    t.wait_ge(S["ye"], 8 * tb)
                for j in range(22):
                    gidx = tb * 22 + j
                    t.wait_ge(S["w2l"], 16 * (gidx + 1))
                    for e in range(8):
                        i = t.matmul(B[e][:], w2buf[gidx % 2][:, e * 128:(e + 1) * 128],
                                     pblk[j][:], start=(j == 0), stop=(j == 21))
                    i.then_inc(S["w2m"], 1)

        @block.vector
        def _(v):
            v.wait_ge(S["bwm"], 1)
            v.tensor_copy(bW1_sb[:], B[4][:, 0:44])
            v.wait_ge(S["lw"], 16 * 5)
            v.tensor_add(btot_sb[:], bW1_sb[:], b1_sb[:]).then_inc(S["bwv"], 1)
            for tb in range(NTB):
                v.wait_ge(S["st"], tb + 1)
                v.tensor_scalar_mul(srow, B[2][0:1, :], 1.0 / 1024)
                v.tensor_scalar_mul(s2row, B[3][0:1, :], 1.0 / 1024).then_inc(S["rw"], 1)
                v.wait_ge(S["rw"], 2 * tb + 1)
                v.tensor_mul(vrow, srow, srow)
                v.tensor_sub(s2row, s2row, vrow)
                v.tensor_scalar_add(s2row, s2row, EPS).then_inc(S["rm1"], 1)
                v.wait_ge(S["sqd"], tb + 1)
                v.reciprocal(arow, stdrow)
                v.tensor_copy(mrow_b[0:1, :], srow)
                v.tensor_copy(arow_b[0:1, :], arow).then_inc(S["rw"], 1)
                for c in range(8):
                    v.wait_ge(S["bcm"], 8 * tb + c + 1)
                    if IFFN_DEBUG and tb == 3 and c == 0:
                        v.tensor_copy(dbg_sb[:, 0:512], B[4][:, 0:TB])
                        v.tensor_copy(dbg_sb[0:1, 512:1024], mrow_b[0:1, :])
                        v.tensor_copy(dbg_sb[0:1, 1024:1536], srow).then_inc(S["so"], 1)
                    v.scalar_tensor_tensor(xnb[c][:, tb * TB:(tb + 1) * TB],
                                           x2b[c][:, tb * TB:(tb + 1) * TB],
                                           gcol_sb[:, c:c + 1], B[4][:, 0:TB],
                                           ALU.mult, ALU.subtract).then_inc(S["xnd"], 1)
                v.wait_ge(S["bca"], tb + 1)
                v.tensor_copy(ab_sb[:], B[4][:, 0:TB]).then_inc(S["aok"], 1)
                if tb > 0:
                    v.wait_ge(S["w2m"], 22 * tb)
                for j in range(22):
                    gidx = tb * 22 + j
                    v.wait_ge(S["hm"], gidx + 1)
                    v.tensor_mul(hA[:], B[0][:], ab_sb[:]).then_inc(S["he"], 1)
                    v.tensor_mul(hG[:], B[1][:], ab_sb[:]).then_inc(S["he"], 1)
                    v.wait_ge(S["gl"], gidx + 1)
                    v.scalar_tensor_tensor(pblk[j][:], hA[:], btot_sb[:, 2 * j:2 * j + 1],
                                           hG2[:], ALU.add, ALU.mult).then_inc(S["pm"], 1)
                v.wait_ge(S["w2m"], 22 * (tb + 1))
                v.wait_ge(S["r1l"], 16 * 8 * (tb + 1))
                for e in range(8):
                    v.scalar_tensor_tensor(r1blk[e][:], B[e][:], b2_sb[:, e:e + 1],
                                           r1blk[e][:], ALU.add, ALU.add).then_inc(S["ye"], 1)

    print("IFFN sbuf remaining:", nc.sbuf_bytes_remaining)
    return nc


def _iffn_host(inputs, layer):
    key = ("iffn_host", layer)
    if key in _CACHE:
        return _CACHE[key]
    W1 = np.asarray(inputs["L0_ffn_W1"][layer])
    b1 = np.asarray(inputs["L0_ffn_b1"][layer])
    W2 = np.asarray(inputs["L0_ffn_W2"][layer])
    b2 = np.asarray(inputs["L0_ffn_b2"][layer])
    g = np.asarray(inputs["L0_ffn_ln_g"][layer])
    bl = np.asarray(inputs["L0_ffn_ln_b"][layer])
    DI = 2730
    W1p = np.zeros((E, 5632), np.float32)
    b1bl = np.zeros((128, 44), np.float32)
    for j in range(22):
        lo, hi = j * 128, min((j + 1) * 128, DI)
        n = hi - lo
        if n > 0:
            W1p[:, (2 * j) * 128:(2 * j) * 128 + n] = W1[:, lo:hi]
            W1p[:, (2 * j + 1) * 128:(2 * j + 1) * 128 + n] = W1[:, DI + lo:DI + hi]
            b1bl[0:n, 2 * j] = b1[lo:hi]
            b1bl[0:n, 2 * j + 1] = b1[DI + lo:DI + hi]
    W2p = np.zeros((2816, E), np.float32)
    W2p[0:DI, :] = W2
    r = {
        "W1p": W1p, "b1bl": b1bl, "W2p": W2p,
        "b2c": np.ascontiguousarray(b2.reshape(8, 128).T).astype(np.float32),
        "gc": np.ascontiguousarray(g.reshape(8, 128).T).astype(np.float32),
        "bb": np.ascontiguousarray(bl.reshape(8, 128).T).astype(BF),
        "g_row": np.ascontiguousarray(
            np.concatenate([g[None, :], np.zeros((31, E))], 0)).astype(BF),
    }
    _CACHE[key] = r
    return r


def run_iffn(inputs, layer, rp_full):
    nc = _get("iffn", build_iffn)
    host = _iffn_host(inputs, layer)
    TH = 2048
    in_maps = []
    for c in range(NCORE):
        b, p = c // 2, c % 2
        rpT = np.zeros((E, TH + 1), np.float32)
        rpT[:, 1:] = rp_full[b][:, p * TH:(p + 1) * TH]
        if p == 1:
            rpT[:, 0] = rp_full[b][:, TH - 1]
        m = {"rpT": rpT}
        m.update(host)
        in_maps.append(m)
    res = run_bass_kernel_spmd(nc, in_maps, core_ids=list(range(NCORE)))
    out = np.zeros((4, E, T), np.float32)
    for c in range(NCORE):
        b, p = c // 2, c % 2
        out[b][:, p * TH:(p + 1) * TH] = np.asarray(res.results[c]["rnT"])
    return out




RGA = 2304
NCH = RGA // 128
TBR = 384
NBR = RGA // TBR


IATTN_DEBUG = False


def build_iattn():
    nc = bass.Bass("TRN2", target_bir_lowering=False, num_devices=NCORE)
    rT = nc.dram_tensor("rT", [E, RGA + 1], F32, kind="ExternalInput").ap()
    Wqkv_d = nc.dram_tensor("Wqkv", [E, 3072], F32, kind="ExternalInput").ap()
    CS_d = [nc.dram_tensor(f"CS{i}", [128, RGA], BF16, kind="ExternalInput").ap() for i in range(4)]
    qs_d = nc.dram_tensor("qs_c", [128, 1], F32, kind="ExternalInput").ap()
    ks_d = nc.dram_tensor("ks_c", [128, 1], F32, kind="ExternalInput").ap()
    swp_d = nc.dram_tensor("swp2", [128, 128], BF16, kind="ExternalInput").ap()
    msk_d = nc.dram_tensor("msk", [4 * 128, 128], BF16, kind="ExternalInput").ap()
    blk2r_d = nc.dram_tensor("blk2r", [32, 128], BF16, kind="ExternalInput").ap()
    Wo_d = nc.dram_tensor("Wo", [E, E], F32, kind="ExternalInput").ap()
    rnT_d = nc.dram_tensor("rnT", [E, 2048], F32, kind="ExternalOutput").ap()
    if IATTN_DEBUG:
        dbgo_d = nc.dram_tensor("dbgo", [E, 2048], BF16, kind="ExternalOutput").ap()

    wkv_buf = [nc.alloc_sbuf_tensor(f"awkv{i}", [128, 8 * 512], BF16) for i in range(2)]
    qf = [nc.alloc_sbuf_tensor(f"aqf{i}", [128, RGA], BF16) for i in range(8)]
    kf = [nc.alloc_sbuf_tensor(f"akf{i}", [128, RGA], BF16) for i in range(8)]
    vp = [nc.alloc_sbuf_tensor(f"avp{h}", [128, NCH * 65], BF16) for h in range(16)]
    sqs = nc.alloc_sbuf_tensor("asqs", [128, RGA], BF16)
    rowq = nc.alloc_sbuf_tensor("arowq", [2, TBR], F32)
    stdq = nc.alloc_sbuf_tensor("astdq", [2, TBR], F32)
    rstdb = nc.alloc_sbuf_tensor("arstd", [32, TBR], BF16)
    qs_sb = nc.alloc_sbuf_tensor("aqs", [128, 1], F32)
    ks_sb = nc.alloc_sbuf_tensor("aks", [128, 1], F32)
    swp_sb = nc.alloc_sbuf_tensor("aswp", [128, 128], BF16)
    ident = nc.alloc_sbuf_tensor("aident", [128, 128], BF16)
    blk2 = nc.alloc_sbuf_tensor("ablk2", [128, 2], BF16)
    blk2r = nc.alloc_sbuf_tensor("ablk2r", [32, 128], BF16)
    msk_sb = [nc.alloc_sbuf_tensor(f"amsk{i}", [128, 128], BF16) for i in range(4)]
    rt0 = nc.alloc_sbuf_tensor("art0", [128, TBR], BF16)
    rt1 = nc.alloc_sbuf_tensor("art1", [128, TBR], BF16)
    mcol = nc.alloc_sbuf_tensor("amcol", [128, 1], F32)
    negm8 = nc.alloc_sbuf_tensor("anegm", [128, 1], F32)
    invsb = nc.alloc_sbuf_tensor("ainvsb", [96, 128], F32)
    onesf = nc.alloc_sbuf_tensor("aonesf", [96, 64], F32)
    invb = nc.alloc_sbuf_tensor("ainvb", [64, 128], F32)
    Pe = nc.alloc_sbuf_tensor("aPe", [128, 384], BF16)
    PT = nc.alloc_sbuf_tensor("aPT", [128, 384], BF16)
    rblk = [nc.alloc_sbuf_tensor(f"arb{c}", [128, 512], F32) for c in range(8)]

    rsh = [nc.alloc_sbuf_tensor(f"arsh{c}", [128, RGA], BF16) for c in range(8)]
    rsh_addr = nc.lookup_mloc(rsh[0]).addr
    t_cs = [nc.alloc_sbuf_tensor_at(f"acs{i}", [128, RGA], BF16, offset=rsh_addr + 16384 + i * 4608)
            for i in range(4)]
    wo_sb = [nc.alloc_sbuf_tensor_at(f"awo{c}", [128, E], BF16, offset=rsh_addr + c * 2048)
             for c in range(8)]
    wkv_addr = nc.lookup_mloc(wkv_buf[0]).addr
    oT = [nc.alloc_sbuf_tensor_at(f"aoT{i}", [128, 2048], BF16, offset=wkv_addr + i * 4096)
          for i in range(4)]
    oT += [nc.alloc_sbuf_tensor(f"aoU{i}", [128, 2048], BF16) for i in range(4)]

    ps_a = nc.alloc_psum_tensor("aps_a", [128, 512], F32)
    ps_b = nc.alloc_psum_tensor("aps_b", [128, 512], F32)
    ps_s0 = nc.alloc_psum_tensor("aps_s0", [128, 384], F32)
    ps_s1 = nc.alloc_psum_tensor("aps_s1", [128, 384], F32)
    ps_tr = nc.alloc_psum_tensor("aps_tr", [128, 384], BF16)
    ps_oT = nc.alloc_psum_tensor("aps_oT", [65, 128], F32)
    ps_nx = nc.alloc_psum_tensor("aps_nx", [64, TBR], F32)
    ps_n = ps_nx[0:2, 0:TBR]
    ps_n2 = ps_nx[0:64, 0:128]
    ps_bc = nc.alloc_psum_tensor("aps_bc", [128, TBR], F32)

    S = {n: nc.alloc_semaphore("a_" + n) for n in
         ["lw", "gi", "rl", "qm", "qe", "sq", "nm", "nre", "nv1", "nsq", "ntl",
          "nb", "qn", "sw", "swe", "rp", "sS", "mx", "mr", "ex", "iv", "pn", "tr",
          "te", "pv", "oe", "wom", "woe", "rbl", "rpt", "wol", "so", "gch", "vch",
          "bcv"]}
    gch = [0]
    vch = [0]

    NQK = 4 * 4 * NBR
    NV = NCH * 2
    NTL = 16
    NSUB = 16 * 8 * 2

    def q_tile_done_qe(qk, i8):
        return qk * 48 + (i8 // 4) * 24 + (i8 % 4) * NBR + NBR

    with nc.Block() as block:

        @block.gpsimd
        def _(g):
            for c in range(8):
                off = 1 if c < 4 else 0
                g.dma_start(out=rsh[c][:], in_=rT[c * 128:(c + 1) * 128, off:off + RGA]
                            ).then_inc(S["rl"], 16)
            g.dma_start(out=qs_sb[:], in_=qs_d[:]).then_inc(S["lw"], 16)
            g.dma_start(out=ks_sb[:], in_=ks_d[:]).then_inc(S["lw"], 16)
            g.dma_start(out=swp_sb[:], in_=swp_d[:]).then_inc(S["lw"], 16)
            for i in range(4):
                g.dma_start(out=msk_sb[i][:], in_=msk_d[i * 128:(i + 1) * 128, :]).then_inc(S["lw"], 16)
            g.memset(ident[:], 0.0).then_inc(S["gch"], 1)
            gch[0] += 1
            g.wait_ge(S["gch"], gch[0])
            g.affine_select(out=ident[:], in_=ident[:], compare_op=ALU.not_equal,
                            fill=1.0, base=0, pattern=[[-1, 128]], channel_multiplier=1
                            ).then_inc(S["gi"], 1)
            g.memset(blk2[:], 0.0).then_inc(S["gch"], 1)
            gch[0] += 1
            g.wait_ge(S["gch"], gch[0])
            g.memset(blk2[0:64, 0:1], 1.0).then_inc(S["gch"], 1)
            gch[0] += 1
            g.wait_ge(S["gch"], gch[0])
            g.memset(blk2[64:128, 1:2], 1.0).then_inc(S["gi"], 1)
            g.dma_start(out=blk2r[:], in_=blk2r_d[:]).then_inc(S["lw"], 16)
            g.memset(rowq[:], 0.0).then_inc(S["gi"], 1)
            g.memset(rstdb[:], 0.0).then_inc(S["gi"], 1)
            g.memset(invsb[64:96, :], 0.0).then_inc(S["gi"], 1)
            g.memset(onesf[64:96, :], 0.0).then_inc(S["gch"], 1)
            gch[0] += 1
            g.wait_ge(S["gch"], gch[0])
            g.memset(onesf[64:65, :], 1.0).then_inc(S["gi"], 1)
            for h in range(16):
                g.memset(vp[h][:].rearrange("p (a b) -> p a b", b=65)[:, :, 64:65], 1.0
                         ).then_inc(S["gi"], 1)
            for grp in range(6):
                if grp == 2:
                    g.wait_ge(S["qm"], 24)
                elif grp == 3:
                    g.wait_ge(S["qm"], 48)
                elif grp == 4:
                    g.wait_ge(S["qm"], 72)
                elif grp == 5:
                    g.wait_ge(S["qm"], 96)
                for c in range(8):
                    g.dma_start(out=wkv_buf[grp % 2][:, c * 512:(c + 1) * 512],
                                in_=Wqkv_d[c * 128:(c + 1) * 128, grp * 512:(grp + 1) * 512]
                                ).then_inc(S["lw"], 16)
            g.wait_ge(S["qm"], NQK + NV)
            for i in range(4):
                g.dma_start(out=t_cs[i][:], in_=CS_d[i][:]).then_inc(S["rpt"], 16)
            for c in range(8):
                g.dma_start(out=wo_sb[c][:], in_=Wo_d[c * 128:(c + 1) * 128, :]).then_inc(S["wol"], 16)
            for tb in range(4):
                if tb >= 1:
                    g.wait_ge(S["so"], 128 * tb)
                for c in range(8):
                    g.dma_start(out=rblk[c][:],
                                in_=rT[c * 128:(c + 1) * 128, 257 + tb * 512: 257 + (tb + 1) * 512]
                                ).then_inc(S["rbl"], 16)

        @block.sync
        def _(sync):
            if IATTN_DEBUG:
                sync.wait_ge(S["oe"], NSUB)
                for c in range(8):
                    sync.dma_start(out=dbgo_d[c * 128:(c + 1) * 128, :], in_=oT[c][:]
                                   ).then_inc(S["so"], 16)
            for tb in range(4):
                sync.wait_ge(S["wom"], 8 * (tb + 1))
                for c in range(8):
                    sync.dma_start(out=rnT_d[c * 128:(c + 1) * 128, tb * 512:(tb + 1) * 512],
                                   in_=rblk[c][:]).then_inc(S["so"], 16)
            sync.wait_ge(S["so"], 16 * 32 + (16 * 8 if IATTN_DEBUG else 0))

        @block.scalar
        def _(sc):
            for qk in range(2):
                src_t = qf if qk == 0 else kf
                for i8 in range(8):
                    idx = qk * 8 + i8
                    sc.wait_ge(S["qe"], q_tile_done_qe(qk, i8))
                    if idx > 0:
                        sc.wait_ge(S["nm"], idx * NBR)
                    sc.activation(sqs[:], src_t[i8][:], AF.Square).then_inc(S["sq"], 1)
                    for blk in range(NBR):
                        sc.wait_ge(S["nv1"], idx * NBR + blk + 1)
                        sc.activation(stdq[:], rowq[:], AF.Sqrt).then_inc(S["nsq"], 1)
            for si in range(NSUB):
                sc.wait_ge(S["mx"], si + 1)
                bank = ps_s0 if si % 2 == 0 else ps_s1
                sc.activation(Pe[:], bank[:], AF.Exp, bias=negm8[:], scale=8.0
                              ).then_inc(S["ex"], 1)

        @block.tensor
        def _(t):
            t.wait_ge(S["lw"], 16 * 16)
            t.wait_ge(S["gi"], 22)
            t.wait_ge(S["rl"], 16 * 8)
            qm_n = 0
            for grp in range(4):
                t.wait_ge(S["lw"], 16 * (8 + 8 * (grp + 1)))
                for e in range(4):
                    for tbb in range(NBR):
                        bank = ps_a if qm_n % 2 == 0 else ps_b
                        if qm_n >= 2:
                            t.wait_ge(S["qe"], qm_n - 1)
                        wb = wkv_buf[grp % 2]
                        for c in range(8):
                            i = t.matmul(bank[:, 0:TBR],
                                         wb[:, c * 512 + e * 128: c * 512 + (e + 1) * 128],
                                         rsh[c][:, tbb * TBR:(tbb + 1) * TBR],
                                         start=(c == 0), stop=(c == 7))
                        i.then_inc(S["qm"], 1)
                        qm_n += 1
            for tc in range(NCH):
                for vh in range(2):
                    t.wait_ge(S["lw"], 16 * (8 + 8 * (5 + vh)))
                    bank = ps_a if qm_n % 2 == 0 else ps_b
                    t.wait_ge(S["qe"], qm_n - 1)
                    wb = wkv_buf[vh]
                    for c in range(8):
                        i = t.matmul(bank[:], rsh[c][:, tc * 128:(tc + 1) * 128],
                                     wb[:, c * 512:(c + 1) * 512],
                                     start=(c == 0), stop=(c == 7))
                    i.then_inc(S["qm"], 1)
                    qm_n += 1
            sw_n = 0
            for qk in range(2):
                src_t = qf if qk == 0 else kf
                qsc = qs_sb if qk == 0 else ks_sb
                tb1 = t_cs[0] if qk == 0 else t_cs[2]
                tb2 = t_cs[1] if qk == 0 else t_cs[3]
                for i8 in range(8):
                    idx = qk * 8 + i8
                    t.wait_ge(S["sq"], idx + 1)
                    for blk in range(NBR):
                        bidx = idx * NBR + blk
                        if bidx >= 1:
                            t.wait_ge(S["nre"], bidx)
                        t.matmul(ps_n, blk2[:], sqs[:, blk * TBR:(blk + 1) * TBR],
                                 start=True, stop=True).then_inc(S["nm"], 1)
                        t.wait_ge(S["ntl"], bidx + 1)
                        if bidx >= 1:
                            t.wait_ge(S["qn"], bidx)
                        t.matmul(ps_bc[:], blk2r[:], rstdb[:],
                                 start=True, stop=True).then_inc(S["nb"], 1)
                        t.wait_ge(S["qn"], bidx + 1)
                        bank = ps_a if sw_n % 2 == 0 else ps_b
                        if sw_n >= 2:
                            t.wait_ge(S["swe"], sw_n - 1)
                        t.matmul(bank[:, 0:TBR], swp_sb[:],
                                 src_t[i8][:, blk * TBR:(blk + 1) * TBR],
                                 start=True, stop=True).then_inc(S["sw"], 1)
                        sw_n += 1
            t.wait_ge(S["rp"], NQK)
            t.wait_ge(S["qe"], NQK + NV)
            si = 0
            for h in range(16):
                i8, ro = h // 2, (h % 2) * 64
                for wj in range(1, 9):
                    for s in range(2):
                        ks = (wj - 1) * 256 + s * 128
                        qo = wj * 256 + s * 128
                        bank = ps_s0 if si % 2 == 0 else ps_s1
                        if si >= 2:
                            t.wait_ge(S["ex"], si - 1)
                        t.matmul(bank[:], qf[i8][ro:ro + 64, qo:qo + 128],
                                 kf[i8][ro:ro + 64, ks:ks + 384], start=True, stop=False)
                        m0 = msk_sb[2] if wj == 1 else msk_sb[0]
                        t.matmul(bank[:, 0:128], ident[:], m0[:], start=False, stop=False)
                        if wj == 1 and s == 0:
                            t.matmul(bank[:, 128:256], ident[:], msk_sb[3][:],
                                     start=False, stop=False)
                        i = t.matmul(bank[:, 256:384], ident[:], msk_sb[1][:],
                                     start=False, stop=True)
                        i.then_inc(S["sS"], 1)
                        t.wait_ge(S["ex"], si + 1)
                        if si >= 1:
                            t.wait_ge(S["te"], 3 * si)
                        for ck in range(3):
                            i = t.transpose(ps_tr[:, ck * 128:(ck + 1) * 128],
                                            Pe[:, ck * 128:(ck + 1) * 128], ident[:])
                        i.then_inc(S["tr"], 1)
                        t.wait_ge(S["te"], 3 * si + 3)
                        if si >= 1:
                            t.wait_ge(S["oe"], si)
                        for ck in range(3):
                            kch = ks // 128 + ck
                            i = t.matmul(ps_oT[0:65, :], vp[h][:, kch * 65:(kch + 1) * 65],
                                         PT[:, ck * 128:(ck + 1) * 128],
                                         start=(ck == 0), stop=(ck == 2))
                        i.then_inc(S["pv"], 1)
                        t.wait_ge(S["iv"], si + 1)
                        if si >= 1:
                            t.wait_ge(S["pn"], si)
                        t.matmul(ps_n2,
                                 onesf[64:96, :], invsb[64:96, :],
                                 start=True, stop=True).then_inc(S["bcv"], 1)
                        si += 1
            t.wait_ge(S["wol"], 16 * 8)
            t.wait_ge(S["oe"], NSUB)
            wo_n = 0
            for tb in range(4):
                for e in range(8):
                    bank = ps_a if wo_n % 2 == 0 else ps_b
                    if wo_n >= 2:
                        t.wait_ge(S["woe"], wo_n - 1)
                    for c in range(8):
                        i = t.matmul(bank[:], wo_sb[c][:, e * 128:(e + 1) * 128],
                                     oT[c][:, tb * 512:(tb + 1) * 512],
                                     start=(c == 0), stop=(c == 7))
                    i.then_inc(S["wom"], 1)
                    wo_n += 1

        @block.vector
        def _(v):
            qm_n = 0
            for grp in range(4):
                for e in range(4):
                    for tbb in range(NBR):
                        v.wait_ge(S["qm"], qm_n + 1)
                        bank = ps_a if qm_n % 2 == 0 else ps_b
                        dst = qf if grp < 2 else kf
                        ti = (grp % 2) * 4 + e
                        v.tensor_copy(dst[ti][:, tbb * TBR:(tbb + 1) * TBR], bank[:, 0:TBR]
                                      ).then_inc(S["qe"], 1)
                        qm_n += 1
            for tc in range(NCH):
                for vh in range(2):
                    v.wait_ge(S["qm"], qm_n + 1)
                    bank = ps_a if qm_n % 2 == 0 else ps_b
                    for hh in range(8):
                        i = v.tensor_copy(vp[vh * 8 + hh][:, tc * 65: tc * 65 + 64],
                                          bank[:, hh * 64:(hh + 1) * 64])
                    i.then_inc(S["qe"], 1)
                    qm_n += 1
            v.wait_ge(S["rpt"], 64)
            sw_n = 0
            for qk in range(2):
                src_t = qf if qk == 0 else kf
                qsc = qs_sb if qk == 0 else ks_sb
                tb1 = t_cs[0] if qk == 0 else t_cs[2]
                tb2 = t_cs[1] if qk == 0 else t_cs[3]
                for i8 in range(8):
                    idx = qk * 8 + i8
                    for blk in range(NBR):
                        bidx = idx * NBR + blk
                        sl = slice(blk * TBR, (blk + 1) * TBR)
                        v.wait_ge(S["nm"], bidx + 1)
                        v.tensor_scalar(rowq[:], ps_n, 1e-24, None, ALU.max
                                        ).then_inc(S["nre"], 1)
                        v.wait_ge(S["nre"], bidx + 1)
                        v.engine_nop().then_inc(S["nv1"], 1)
                        v.wait_ge(S["nsq"], bidx + 1)
                        v.reciprocal(rowq[:], stdq[:]).then_inc(S["vch"], 1)
                        vch[0] += 1
                        v.wait_ge(S["vch"], vch[0])
                        v.tensor_copy(rstdb[0:2, :], rowq[:]).then_inc(S["ntl"], 1)
                        v.wait_ge(S["nb"], bidx + 1)
                        v.scalar_tensor_tensor(src_t[i8][:, sl], src_t[i8][:, sl],
                                               qsc[:], ps_bc[:],
                                               ALU.mult, ALU.mult).then_inc(S["qn"], 1)
                        v.wait_ge(S["sw"], sw_n + 1)
                        bank = ps_a if sw_n % 2 == 0 else ps_b
                        v.tensor_copy(rt0[:], bank[:, 0:TBR]).then_inc(S["swe"], 1)
                        v.tensor_mul(rt1[:], src_t[i8][:, sl], tb1[:, sl])
                        v.wait_ge(S["swe"], sw_n + 1)
                        v.tensor_mul(rt0[:], rt0[:], tb2[:, sl]).then_inc(S["vch"], 1)
                        vch[0] += 1
                        v.wait_ge(S["vch"], vch[0])
                        v.tensor_add(src_t[i8][:, sl], rt0[:], rt1[:]).then_inc(S["rp"], 1)
                        sw_n += 1
            for si in range(NSUB):
                v.wait_ge(S["sS"], si + 1)
                bank = ps_s0 if si % 2 == 0 else ps_s1
                v.reduce_max(mcol[:], bank[:], axis=AX.X).then_inc(S["mr"], 1)
                v.wait_ge(S["mr"], si + 1)
                v.tensor_scalar_mul(negm8[:], mcol[:], -8.0).then_inc(S["mx"], 1)
                v.wait_ge(S["tr"], si + 1)
                for ck in range(3):
                    i = v.tensor_copy(PT[:, ck * 128:(ck + 1) * 128],
                                      ps_tr[:, ck * 128:(ck + 1) * 128])
                    i.then_inc(S["te"], 1)
                v.wait_ge(S["pv"], si + 1)
                v.reciprocal(invsb[64:65, :], ps_oT[64:65, :]).then_inc(S["iv"], 1)
                v.wait_ge(S["bcv"], si + 1)
                v.tensor_copy(invb[:], ps_n2).then_inc(S["pn"], 1)
                v.wait_ge(S["pn"], si + 1)
                h = si // 16
                wj = 1 + (si % 16) // 2
                s = si % 2
                i8, ro = h // 2, (h % 2) * 64
                v.tensor_mul(oT[i8][ro:ro + 64,
                                    (wj - 1) * 256 + s * 128: (wj - 1) * 256 + s * 128 + 128],
                             ps_oT[0:64, :], invb[:]).then_inc(S["oe"], 1)
            wo_n = 0
            for tb in range(4):
                v.wait_ge(S["rbl"], 128 * (tb + 1))
                for e in range(8):
                    v.wait_ge(S["wom"], wo_n + 1)
                    bank = ps_a if wo_n % 2 == 0 else ps_b
                    v.tensor_add(rblk[e][:], bank[:], rblk[e][:]).then_inc(S["woe"], 1)
                    wo_n += 1

    print("IATTN sbuf remaining:", nc.sbuf_bytes_remaining)
    return nc


def _iattn_host(inputs, layer):
    key = ("iattn_host", layer)
    if key in _CACHE:
        return _CACHE[key]
    Wqkv = np.ascontiguousarray(np.asarray(inputs["L0_ffn_Wqkv"][layer], np.float32))
    qs = np.asarray(inputs["L0_ffn_q_scale"][layer], np.float32)
    ks = np.asarray(inputs["L0_ffn_k_scale"][layer], np.float32)
    Wo = np.ascontiguousarray(np.asarray(inputs["L0_ffn_Wo"][layer], np.float32))
    r = {
        "Wqkv": Wqkv, "Wo": Wo,
        "qs_c": np.ascontiguousarray(np.tile(qs, 2)[:, None]).astype(np.float32),
        "ks_c": np.ascontiguousarray(np.tile(ks, 2)[:, None]).astype(np.float32),
    }
    _CACHE[key] = r
    return r


def _iattn_const():
    if "iattn_const" in _CACHE:
        return _CACHE["iattn_const"]
    swp = np.zeros((128, 128), np.float32)
    for b0 in range(0, 128, 64):
        for d in range(32):
            swp[b0 + d + 32, b0 + d] = 1.0
            swp[b0 + d, b0 + d + 32] = 1.0
    swp = swp.astype(BF)
    NEG = -1e9
    ir = np.arange(128)[:, None]
    jr = np.arange(128)[None, :]
    m_lo = np.where(jr >= ir, 0.0, NEG).astype(np.float32)
    m_hi = np.where(jr <= ir, 0.0, NEG).astype(np.float32)
    full = np.full((128, 128), NEG, np.float32)
    zero = np.zeros((128, 128), np.float32)
    mask_even = np.concatenate([m_lo, m_hi, full, full], 0).astype(BF)
    mask_odd = np.concatenate([m_lo, m_hi, m_lo, zero], 0).astype(BF)
    tabs = {}
    for p in range(2):
        kvstart = p * 2048 - 256
        t = np.arange(kvstart, kvstart + RGA, dtype=np.float64)
        dd = np.arange(64)
        inv_freq = 1.0 / 10000.0 ** (np.arange(0, 64, 2, dtype=np.float64) / 64.0)
        f = inv_freq[dd % 32]
        base = (np.arange(0, 64, 2, dtype=np.float64) + 0.4 * 64) / (1.4 * 64)
        bb = base[dd % 32]
        ang = f[:, None] * t[None, :]
        power = (t - 2048.0) / 128.0
        scale = bb[:, None] ** power[None, :]
        sig = np.where(dd < 32, -1.0, 1.0)[:, None]
        cs1q = np.cos(ang) * scale
        cs2q = sig * np.sin(ang) * scale
        cs1k = np.cos(ang) / scale
        cs2k = sig * np.sin(ang) / scale
        tabs[p] = [np.concatenate([x, x], 0).astype(BF) for x in (cs1q, cs2q, cs1k, cs2k)]
    r = {"swp2": swp, "mask_even": mask_even, "mask_odd": mask_odd, "tabs": tabs}
    _CACHE["iattn_const"] = r
    return r


def run_iattn(inputs, layer, r_full):
    nc = _get("iattn", build_iattn)
    host = _iattn_host(inputs, layer)
    consts = _iattn_const()
    in_maps = []
    for c in range(NCORE):
        b, p = c // 2, c % 2
        rTc = np.zeros((E, RGA + 1), np.float32)
        if p == 0:
            rTc[:, 257:] = r_full[b][:, 0:2048]
        else:
            rTc[:, 1:] = r_full[b][:, 1792:4096]
            rTc[:, 0] = r_full[b][:, 1791]
        b2r = np.zeros((32, 128), np.float32)
        b2r[0, 0:64] = 1.0
        b2r[1, 64:128] = 1.0
        m = {"rT": rTc, "swp2": consts["swp2"], "blk2r": b2r.astype(BF),
             "msk": consts["mask_even"] if p == 0 else consts["mask_odd"]}
        for i in range(4):
            m[f"CS{i}"] = consts["tabs"][p][i]
        m.update(host)
        in_maps.append(m)
    res = run_bass_kernel_spmd(nc, in_maps, core_ids=list(range(NCORE)))
    out = np.zeros((4, E, T), np.float32)
    for c in range(NCORE):
        b, p = c // 2, c % 2
        out[b][:, p * 2048:(p + 1) * 2048] = np.asarray(res.results[c]["rnT"])
    if IATTN_DEBUG:
        run_iattn.dbg = [np.asarray(res.results[c]["dbgo"]) for c in range(NCORE)]
    return out



NSA_DEBUG = False


def build_nsa():
    nc = bass.Bass("TRN2", target_bir_lowering=False, num_devices=NCORE)
    xnT = nc.dram_tensor("xnT", [E, T], F32, kind="ExternalInput").ap()
    Wq_d = nc.dram_tensor("Wq", [E, 512], F32, kind="ExternalInput").ap()
    Wk_d = nc.dram_tensor("Wk", [E, 512], F32, kind="ExternalInput").ap()
    Wv_d = nc.dram_tensor("Wv", [E, 512], F32, kind="ExternalInput").ap()
    Wg_d = nc.dram_tensor("Wgr", [128, 24], F32, kind="ExternalInput").ap()
    bq_d = nc.dram_tensor("bq", [128, 4], F32, kind="ExternalInput").ap()
    bk_d = nc.dram_tensor("bk", [128, 4], F32, kind="ExternalInput").ap()
    bv_d = nc.dram_tensor("bvp", [32, 512], BF16, kind="ExternalInput").ap()
    bg_d = nc.dram_tensor("bg", [3, 1], F32, kind="ExternalInput").ap()
    smask_d = nc.dram_tensor("smask", [3 * 128, 128], BF16, kind="ExternalInput").ap()
    selm_d = nc.dram_tensor("selm", [32, 192], BF16, kind="ExternalInput").ap()
    nsaT_d = nc.dram_tensor("nsaT", [512, T], BF16, kind="ExternalOutput").ap()
    selk_dram = nc.dram_tensor("selk_scratch", [32, 32], F32, kind="Internal")
    if NSA_DEBUG:
        ndbg_d = nc.dram_tensor("ndbg", [16, 128], F32, kind="ExternalOutput").ap()
        gbdbg_d = nc.dram_tensor("gbdbg", [192, T], BF16, kind="ExternalOutput").ap()
        cmdbg_d = nc.dram_tensor("cmdbg", [512, T], BF16, kind="ExternalOutput").ap()
        pdbg_d = nc.dram_tensor("pdbg", [4 * 128, T], BF16, kind="ExternalOutput").ap()

    NTC = T // 128
    PR = 8

    xnb = [nc.alloc_sbuf_tensor(f"nxn{c}", [128, T], BF16) for c in range(8)]
    qT = [nc.alloc_sbuf_tensor(f"nqT{i}", [128, T], BF16) for i in range(4)]
    kT = [nc.alloc_sbuf_tensor(f"nkT{i}", [128, T], BF16) for i in range(4)]
    vp = [nc.alloc_sbuf_tensor(f"nvp{h}", [128, NTC * 65], BF16) for h in range(8)]
    w_sb = [nc.alloc_sbuf_tensor(f"nwb{c}", [128, 512], BF16) for c in range(8)]
    wg_sb = nc.alloc_sbuf_tensor("nwg", [128, 24], BF16)
    xmean = [nc.alloc_sbuf_tensor(f"nxm{c}", [128, 128], BF16) for c in range(8)]
    xmf = nc.alloc_sbuf_tensor("nxmf", [128, 128], F32)
    ckT = [nc.alloc_sbuf_tensor(f"nck{i}", [128, 128], BF16) for i in range(4)]
    cvp = [nc.alloc_sbuf_tensor(f"ncv{h}", [128, 65], BF16) for h in range(8)]
    gates = nc.alloc_sbuf_tensor("ngate", [32, T], BF16)
    selm_sb = nc.alloc_sbuf_tensor("nselm", [32, 192], BF16)
    bq_sb = nc.alloc_sbuf_tensor("nbq", [128, 4], F32)
    bk_sb = nc.alloc_sbuf_tensor("nbk", [128, 4], F32)
    bv_sb = nc.alloc_sbuf_tensor("nbv", [32, 512], BF16)
    bg_sb = nc.alloc_sbuf_tensor("nbg", [3, 1], F32)
    onesrow_b = nc.alloc_sbuf_tensor("nones", [32, 128], BF16)
    onesf = nc.alloc_sbuf_tensor("nonesf", [96, 128], F32)
    negc0 = nc.alloc_sbuf_tensor("nnegc0", [128, 1], F32)
    invsb = nc.alloc_sbuf_tensor("ninvsb", [96, 512], F32)
    invb = nc.alloc_sbuf_tensor("ninvb", [64, 512], F32)
    invb64 = nc.alloc_sbuf_tensor("ninvb64", [64, 128], F32)
    invb128 = nc.alloc_sbuf_tensor("ninvb128", [128, 128], BF16)
    identb = nc.alloc_sbuf_tensor("nidb", [128, 128], BF16)
    identf = nc.alloc_sbuf_tensor("nidf", [128, 128], F32)
    smask = [nc.alloc_sbuf_tensor(f"nsm{i}", [128, 128], BF16) for i in range(3)]
    bsacc = nc.alloc_sbuf_tensor("nbs", [128, 1], F32)
    bstmp = nc.alloc_sbuf_tensor("nbst", [128, 1], F32)
    bsrow = nc.alloc_sbuf_tensor("nbsr", [1, 128], F32)
    top8 = nc.alloc_sbuf_tensor("ntop", [1, 8], F32)
    sel01 = nc.alloc_sbuf_tensor("nsel", [1, 128], F32)
    selkf = nc.alloc_sbuf_tensor("nselkf", [128, 32], F32)
    prb = nc.alloc_sbuf_tensor("nprb", [128, 128], BF16)
    pc = nc.alloc_sbuf_tensor("npc", [128, 128], BF16)
    pdl = nc.alloc_sbuf_tensor("npdl", [128, 128], BF16)
    pdo = nc.alloc_sbuf_tensor("npdo", [128, 128], BF16)
    slide_st = nc.alloc_sbuf_tensor("nslid", [64, 512], BF16)
    sel_sb = nc.alloc_sbuf_tensor("nselo", [64, 512], BF16)
    osum = nc.alloc_sbuf_tensor("nosum", [64, 128], F32)
    otmp = nc.alloc_sbuf_tensor("notmp", [64, 128], F32)
    obuf = [nc.alloc_sbuf_tensor(f"nob{i}", [64, 512], BF16) for i in range(2)]
    gbT = [nc.alloc_sbuf_tensor_at(f"ngbT{j}", [64, T], BF16,
                                   offset=nc.lookup_mloc(xnb[j]).addr) for j in range(3)]
    comp_st = nc.alloc_sbuf_tensor_at("ncomp", [64, T], BF16,
                                      offset=nc.lookup_mloc(xnb[3]).addr)
    pt = nc.alloc_sbuf_tensor_at("npt", [128, PR * 512], BF16,
                                 offset=nc.lookup_mloc(xnb[4]).addr)
    vselp = nc.alloc_sbuf_tensor_at("nvsel", [128, NTC * 65], BF16,
                                    offset=nc.lookup_mloc(xnb[5]).addr)

    ps_a = nc.alloc_psum_tensor("nps_a", [128, 512], F32)
    ps_b = nc.alloc_psum_tensor("nps_b", [128, 512], F32)
    ps_s0 = nc.alloc_psum_tensor("nps_s0", [128, 512], F32)
    ps_s1 = nc.alloc_psum_tensor("nps_s1", [128, 512], F32)
    ps_o = nc.alloc_psum_tensor("nps_o", [128, 512], F32)
    ps_g = nc.alloc_psum_tensor("nps_g", [64, 512], F32)
    ps_c = nc.alloc_psum_tensor("nps_c", [128, 128], F32)
    ps_o2 = nc.alloc_psum_tensor("nps_o2", [128, 128], F32)

    S = {n: nc.alloc_semaphore("n_" + n) for n in
         ["lw", "cw", "xl", "gi", "gld", "pm", "pe", "gm", "ge", "xm",
          "cqk", "cex", "cpv", "cnr", "cbc", "cnc",
          "bsp", "bse", "thr", "sdm", "sde", "vs",
          "sqk", "sex", "spvc", "snr", "sbc", "snc", "sev",
          "dqk", "dex", "dpv", "dnr", "dbc", "dnc",
          "cmb", "so0", "so1", "vch", "gch", "dbs"]}

    with nc.Block() as block:

        @block.gpsimd
        def _(g):
            gch = 0
            for c in range(8):
                g.dma_start(out=xnb[c][:], in_=xnT[c * 128:(c + 1) * 128, :]).then_inc(S["xl"], 16)
            g.dma_start(out=bq_sb[:], in_=bq_d[:]).then_inc(S["cw"], 16)
            g.dma_start(out=bk_sb[:], in_=bk_d[:]).then_inc(S["cw"], 16)
            g.dma_start(out=bv_sb[:], in_=bv_d[:]).then_inc(S["cw"], 16)
            g.dma_start(out=bg_sb[:], in_=bg_d[:]).then_inc(S["cw"], 16)
            g.dma_start(out=wg_sb[:], in_=Wg_d[:]).then_inc(S["cw"], 16)
            for i in range(3):
                g.dma_start(out=smask[i][:], in_=smask_d[i * 128:(i + 1) * 128, :]
                            ).then_inc(S["cw"], 16)
            g.dma_start(out=selm_sb[:], in_=selm_d[:]).then_inc(S["cw"], 16)
            g.memset(onesrow_b[:], 0.0).then_inc(S["gch"], 1)
            gch += 1
            g.wait_ge(S["gch"], gch)
            g.memset(onesrow_b[0:1, :], 1.0).then_inc(S["gi"], 1)
            g.memset(onesf[:], 0.0).then_inc(S["gch"], 1)
            gch += 1
            g.wait_ge(S["gch"], gch)
            g.memset(onesf[64:65, :], 1.0).then_inc(S["gi"], 1)
            g.memset(invsb[64:96, :], 0.0).then_inc(S["gi"], 1)
            g.memset(identb[:], 0.0).then_inc(S["gch"], 1)
            gch += 1
            g.wait_ge(S["gch"], gch)
            g.affine_select(out=identb[:], in_=identb[:], compare_op=ALU.not_equal,
                            fill=1.0, base=0, pattern=[[-1, 128]], channel_multiplier=1
                            ).then_inc(S["gi"], 1)
            g.memset(identf[:], 0.0).then_inc(S["gch"], 1)
            gch += 1
            g.wait_ge(S["gch"], gch)
            g.affine_select(out=identf[:], in_=identf[:], compare_op=ALU.not_equal,
                            fill=1.0, base=0, pattern=[[-1, 128]], channel_multiplier=1
                            ).then_inc(S["gi"], 1)
            for h in range(8):
                g.memset(vp[h][:].rearrange("p (a b) -> p a b", b=65)[:, :, 64:65], 1.0
                         ).then_inc(S["gi"], 1)
            for h in range(8):
                g.memset(cvp[h][:, 64:65], 1.0).then_inc(S["gi"], 1)
            g.memset(bsacc[:], 0.0).then_inc(S["gi"], 1)
            g.memset(gates[:], 0.0).then_inc(S["gi"], 1)
            g.memset(negc0[:], -C0).then_inc(S["gi"], 1)
            for c in range(8):
                g.dma_start(out=w_sb[c][:], in_=Wq_d[c * 128:(c + 1) * 128, :]).then_inc(S["lw"], 16)
            g.wait_ge(S["pm"], 32)
            for c in range(8):
                g.dma_start(out=w_sb[c][:], in_=Wk_d[c * 128:(c + 1) * 128, :]).then_inc(S["lw"], 16)
            g.wait_ge(S["pm"], 68)
            for c in range(8):
                g.dma_start(out=w_sb[c][:], in_=Wv_d[c * 128:(c + 1) * 128, :]).then_inc(S["lw"], 16)

        @block.sync
        def _(sync):
            dram_ap = selk_dram.ap()
            if NSA_DEBUG:
                sync.wait_ge(S["pe"], 197)
                for j in range(3):
                    sync.dma_start(out=gbdbg_d[j * 64:(j + 1) * 64, :], in_=gbT[j][:]
                                   ).then_inc(S["dbs"], 16)
                sync.dma_start(out=pdbg_d[0:32, :], in_=gates[:]).then_inc(S["dbs"], 16)
                sync.dma_start(out=pdbg_d[128:256, :], in_=qT[0][:]).then_inc(S["dbs"], 16)
                sync.dma_start(out=pdbg_d[256:384, :], in_=kT[0][:]).then_inc(S["dbs"], 16)
                sync.dma_start(out=pdbg_d[384:512, 0:NTC * 65], in_=vp[0][:]).then_inc(S["dbs"], 16)
                sync.dma_start(out=pdbg_d[32:64, 0:128], in_=ckT[0][0:32, :]).then_inc(S["dbs"], 16)
                sync.dma_start(out=pdbg_d[64:96, 0:65], in_=cvp[0][0:32, :]).then_inc(S["dbs"], 16)
                sync.dma_start(out=pdbg_d[96:128, 0:128], in_=xmean[0][0:32, :]).then_inc(S["dbs"], 16)
            for h in range(8):
                sync.wait_ge(S["thr"], h + 1)
                if NSA_DEBUG:
                    sync.dma_start(out=ndbg_d[h:h + 1, :], in_=bsrow[:]).then_inc(S["dbs"], 16)
                    sync.dma_start(out=ndbg_d[8 + h:9 + h, :], in_=sel01[:]).then_inc(S["dbs"], 16)
                    sync.wait_ge(S["cnc"], 32 * (h + 1))
                    sync.dma_start(out=cmdbg_d[h * 64:(h + 1) * 64, :], in_=comp_st[:]
                                   ).then_inc(S["dbs"], 16)
                for a in range(4):
                    src = sel01[:].rearrange("p (t a) -> p a t", a=4)[:, a, :]
                    sync.dma_start(out=dram_ap[h * 4 + a:h * 4 + a + 1, :], in_=src,
                                   allow_slow_non_contiguous=True).then_inc(S["sdm"], 16)
                sync.wait_ge(S["sdm"], 64 * (h + 1))
                for a in range(4):
                    src2 = dram_ap[h * 4 + a:h * 4 + a + 1, :].broadcast_to([32, 32])
                    sync.dma_start(out=selkf[a * 32:(a + 1) * 32, :], in_=src2
                                   ).then_inc(S["sde"], 16)
                for qb in range(8):
                    iq = h * 8 + qb
                    sync.wait_ge(S["cmb"], 4 * (iq + 1))
                    sync.dma_start(out=nsaT_d[h * 64:(h + 1) * 64, qb * 512:(qb + 1) * 512],
                                   in_=obuf[qb % 2][:]).then_inc(S[f"so{iq % 2}"], 16)
            sync.wait_ge(S["so0"], 16 * 32)
            sync.wait_ge(S["so1"], 16 * 32)
            if NSA_DEBUG:
                sync.wait_ge(S["dbs"], 16 * 34)

        @block.tensor
        def _(t):
            t.wait_ge(S["gi"], 24)
            t.wait_ge(S["xl"], 128)
            t.wait_ge(S["cw"], 144)
            for tb in range(8):
                if tb:
                    t.wait_ge(S["ge"], tb)
                for c in range(8):
                    mm = t.matmul(ps_g[0:3, :], wg_sb[:, 3 * c:3 * c + 3],
                                  xnb[c][:, tb * 512:(tb + 1) * 512],
                                  start=(c == 0), stop=(c == 7))
                    if c == 7:
                        mm.then_inc(S["gm"], 1)
            banks = [ps_a, ps_b]
            n = 0
            t.wait_ge(S["lw"], 128)
            for e in range(4):
                for tb in range(8):
                    if n >= 2:
                        t.wait_ge(S["pe"], n - 1)
                    bank = banks[n % 2]
                    for c in range(8):
                        mm = t.matmul(bank[:], w_sb[c][:, e * 128:(e + 1) * 128],
                                      xnb[c][:, tb * 512:(tb + 1) * 512],
                                      start=(c == 0), stop=(c == 7))
                    mm.then_inc(S["pm"], 1)
                    n += 1
            t.wait_ge(S["lw"], 256)
            for e in range(4):
                for tb in range(8):
                    t.wait_ge(S["pe"], n - 1)
                    bank = banks[n % 2]
                    for c in range(8):
                        mm = t.matmul(bank[:], w_sb[c][:, e * 128:(e + 1) * 128],
                                      xnb[c][:, tb * 512:(tb + 1) * 512],
                                      start=(c == 0), stop=(c == 7))
                    mm.then_inc(S["pm"], 1)
                    n += 1
            t.wait_ge(S["xm"], 8)
            for i in range(4):
                t.wait_ge(S["pe"], n - 1)
                bank = banks[n % 2]
                for c in range(8):
                    mm = t.matmul(bank[0:128, 0:128], w_sb[c][:, i * 128:(i + 1) * 128],
                                  xmean[c][:], start=(c == 0), stop=(c == 7))
                mm.then_inc(S["pm"], 1)
                n += 1
            t.wait_ge(S["lw"], 384)
            for tc in range(32):
                t.wait_ge(S["pe"], n - 1)
                bank = banks[n % 2]
                for c in range(8):
                    t.matmul(bank[:], xnb[c][:, tc * 128:(tc + 1) * 128], w_sb[c][:],
                             start=(c == 0), stop=False)
                t.matmul(bank[:], onesrow_b[:], bv_sb[:], start=False, stop=True
                         ).then_inc(S["pm"], 1)
                n += 1
            t.wait_ge(S["pe"], n - 1)
            bank = banks[n % 2]
            for c in range(8):
                t.matmul(bank[:], xmean[c][:], w_sb[c][:], start=(c == 0), stop=False)
            t.matmul(bank[:], onesrow_b[:], bv_sb[:], start=False, stop=True
                     ).then_inc(S["pm"], 1)
            n += 1
            t.wait_ge(S["ge"], 8)
            for qt in range(32):
                for j in range(3):
                    t.wait_ge(S["pe"], n - 1)
                    bank = banks[n % 2]
                    t.matmul(bank[0:64, 0:128], selm_sb[:, j * 64:(j + 1) * 64],
                             gates[:, qt * 128:(qt + 1) * 128],
                             start=True, stop=True).then_inc(S["pm"], 1)
                    n += 1
            t.wait_ge(S["pe"], n)
            for h in range(8):
                i4, ro = h // 2, (h % 2) * 64
                if h:
                    t.wait_ge(S["dnc"], 32 * h)
                    t.wait_ge(S["dex"], 64 * h)
                for qt in range(32):
                    idx = h * 32 + qt
                    if idx:
                        t.wait_ge(S["cex"], idx)
                    t.matmul(ps_c[:], ckT[i4][ro:ro + 64, :],
                             qT[i4][ro:ro + 64, qt * 128:(qt + 1) * 128],
                             start=True, stop=True).then_inc(S["cqk"], 1)
                    t.wait_ge(S["cex"], idx + 1)
                    if idx:
                        t.wait_ge(S["cnc"], idx)
                    t.matmul(ps_o2[0:65, :], cvp[h][:], pc[:], start=True, stop=True
                             ).then_inc(S["cpv"], 1)
                    t.wait_ge(S["cnr"], idx + 1)
                    t.matmul(ps_s1[0:128, 0:128], onesf[64:96, :], invsb[64:96, 0:128],
                             start=True, stop=True).then_inc(S["cbc"], 1)
                t.wait_ge(S["cnc"], 32 * (h + 1))
                t.matmul(ps_c[0:1, 0:128], bsacc[:], identf[:], start=True, stop=True
                         ).then_inc(S["bsp"], 1)
                t.wait_ge(S["vs"], h + 1)
                for qb in range(8):
                    iq = h * 8 + qb
                    if iq:
                        t.wait_ge(S["dnc"], 4 * iq)
                    if iq:
                        t.wait_ge(S["sev"], iq)
                    for kc in range(NTC + 1):
                        if kc < NTC:
                            ns = iq * NTC + kc
                            if ns >= 2:
                                t.wait_ge(S["sex"], ns - 1)
                            sbank = [ps_s0, ps_s1][ns % 2]
                            t.matmul(sbank[:], kT[i4][ro:ro + 64, kc * 128:(kc + 1) * 128],
                                     qT[i4][ro:ro + 64, qb * 512:(qb + 1) * 512],
                                     start=True, stop=True).then_inc(S["sqk"], 1)
                        if kc >= 1:
                            kp = kc - 1
                            np_ = iq * NTC + kp
                            t.wait_ge(S["sex"], np_ + 1)
                            t.matmul(ps_o[0:65, :], vselp[:, kp * 65:(kp + 1) * 65],
                                     pt[:, (np_ % PR) * 512:(np_ % PR) * 512 + 512],
                                     start=(kp == 0), stop=(kp == NTC - 1)).then_inc(S["spvc"], 1)
                    t.wait_ge(S["snr"], iq + 1)
                    if iq:
                        t.wait_ge(S["snc"], iq)
                    t.matmul(ps_g[0:64, :], onesf[64:96, 0:64], invsb[64:96, 0:512],
                             start=True, stop=True).then_inc(S["sbc"], 1)
                    if qb == 0:
                        t.wait_ge(S["bse"], h + 1)
                    for s4 in range(4):
                        qt = qb * 4 + s4
                        i4x = iq * 4 + s4
                        if i4x:
                            t.wait_ge(S["dex"], 2 * i4x)
                        lc = max(qt - 1, 0)
                        mi = 2 if qt == 0 else 0
                        t.matmul(ps_c[:], kT[i4][ro:ro + 64, lc * 128:(lc + 1) * 128],
                                 qT[i4][ro:ro + 64, qt * 128:(qt + 1) * 128],
                                 start=True, stop=False)
                        t.matmul(ps_c[:], identb[:], smask[mi][:], start=False, stop=True
                                 ).then_inc(S["dqk"], 1)
                        t.matmul(ps_a[0:128, 0:128], kT[i4][ro:ro + 64, qt * 128:(qt + 1) * 128],
                                 qT[i4][ro:ro + 64, qt * 128:(qt + 1) * 128],
                                 start=True, stop=False)
                        t.matmul(ps_a[0:128, 0:128], identb[:], smask[1][:], start=False, stop=True
                                 ).then_inc(S["dqk"], 1)
                        if i4x:
                            t.wait_ge(S["dnc"], i4x)
                        t.wait_ge(S["dex"], 2 * i4x + 1)
                        t.matmul(ps_o2[0:65, :], vp[h][:, lc * 65:(lc + 1) * 65], pdl[:],
                                 start=True, stop=False)
                        t.wait_ge(S["dex"], 2 * i4x + 2)
                        t.matmul(ps_o2[0:65, :], vp[h][:, qt * 65:(qt + 1) * 65], pdo[:],
                                 start=False, stop=True).then_inc(S["dpv"], 1)
                        t.wait_ge(S["dnr"], i4x + 1)
                        t.matmul(ps_s1[0:64, 0:128], onesf[64:96, 0:64], invsb[64:96, 0:128],
                                 start=True, stop=True).then_inc(S["dbc"], 1)

        @block.scalar
        def _(sc):
            sc.wait_ge(S["cw"], 144)
            sc.wait_ge(S["gi"], 24)
            for tb in range(8):
                sc.wait_ge(S["gm"], tb + 1)
                sc.activation(gates[0:3, tb * 512:(tb + 1) * 512], ps_g[0:3, :],
                              AF.Sigmoid, bias=bg_sb[:]).then_inc(S["ge"], 1)
            for h in range(8):
                for qt in range(32):
                    idx = h * 32 + qt
                    sc.wait_ge(S["cqk"], idx + 1)
                    if idx:
                        sc.wait_ge(S["cnc"], idx)
                    sc.activation(pc[:], ps_c[:], AF.Exp, bias=negc0[0:128, :], scale=0.125
                                  ).then_inc(S["cex"], 1)
                for qb in range(8):
                    iq = h * 8 + qb
                    for kc in range(NTC):
                        ns = iq * NTC + kc
                        sc.wait_ge(S["sqk"], ns + 1)
                        if ns >= PR:
                            sc.wait_ge(S["spvc"], ns - PR + 1)
                        sbank = [ps_s0, ps_s1][ns % 2]
                        sc.activation(pt[:, (ns % PR) * 512:(ns % PR) * 512 + 512], sbank[:],
                                      AF.Exp, bias=negc0[0:128, :], scale=0.125
                                      ).then_inc(S["sex"], 1)
                    for s4 in range(4):
                        i4x = iq * 4 + s4
                        sc.wait_ge(S["dqk"], 2 * i4x + 1)
                        sc.activation(pdl[:], ps_c[:], AF.Exp, bias=negc0[0:128, :], scale=0.125
                                      ).then_inc(S["dex"], 1)
                        sc.wait_ge(S["dqk"], 2 * i4x + 2)
                        sc.activation(pdo[:], ps_a[0:128, 0:128], AF.Exp,
                                      bias=negc0[0:128, :], scale=0.125).then_inc(S["dex"], 1)

        @block.vector
        def _(v):
            vch = [0]

            def selfw(tag):
                v.wait_ge(S["vch"], vch[0])

            v.wait_ge(S["cw"], 144)
            v.wait_ge(S["xl"], 128)
            v.wait_ge(S["gi"], 24)
            banks = [ps_a, ps_b]
            n = 0
            for e in range(4):
                for tb in range(8):
                    v.wait_ge(S["pm"], n + 1)
                    v.tensor_scalar(qT[e][:, tb * 512:(tb + 1) * 512], banks[n % 2][:],
                                    bq_sb[:, e:e + 1], None, ALU.add).then_inc(S["pe"], 1)
                    n += 1
            for e in range(4):
                for tb in range(8):
                    v.wait_ge(S["pm"], n + 1)
                    v.tensor_scalar(kT[e][:, tb * 512:(tb + 1) * 512], banks[n % 2][:],
                                    bk_sb[:, e:e + 1], None, ALU.add).then_inc(S["pe"], 1)
                    n += 1
            for c in range(8):
                v.reduce_sum(xmf[:].rearrange("p (n o) -> p n o", o=1),
                             xnb[c][:].rearrange("p (n w) -> p n w", w=32),
                             axis=AX.X).then_inc(S["vch"], 1)
                vch[0] += 1
                selfw("xm")
                v.tensor_scalar(xmean[c][:], xmf[:], 1.0 / 32.0, None, ALU.mult
                                ).then_inc(S["xm"], 1)
            for i in range(4):
                v.wait_ge(S["pm"], n + 1)
                v.tensor_scalar(ckT[i][:], banks[n % 2][0:128, 0:128],
                                bk_sb[:, i:i + 1], None, ALU.add).then_inc(S["pe"], 1)
                n += 1
            for tc in range(32):
                v.wait_ge(S["pm"], n + 1)
                for h8 in range(8):
                    cp = v.tensor_copy(vp[h8][:, tc * 65:tc * 65 + 64],
                                       banks[n % 2][:, h8 * 64:(h8 + 1) * 64])
                cp.then_inc(S["pe"], 1)
                n += 1
            v.wait_ge(S["pm"], n + 1)
            for h8 in range(8):
                cp = v.tensor_copy(cvp[h8][:, 0:64], banks[n % 2][:, h8 * 64:(h8 + 1) * 64])
            cp.then_inc(S["pe"], 1)
            n += 1
            for qt in range(32):
                for j in range(3):
                    v.wait_ge(S["pm"], n + 1)
                    v.tensor_copy(gbT[j][:, qt * 128:(qt + 1) * 128], banks[n % 2][0:64, 0:128]
                                  ).then_inc(S["pe"], 1)
                    n += 1
            for h in range(8):
                for qt in range(32):
                    idx = h * 32 + qt
                    v.wait_ge(S["cpv"], idx + 1)
                    v.reciprocal(invsb[64:65, 0:128], ps_o2[64:65, :]).then_inc(S["cnr"], 1)
                    v.wait_ge(S["cbc"], idx + 1)
                    v.tensor_copy(invb64[:], ps_s1[0:64, 0:128])
                    v.tensor_copy(invb128[:], ps_s1[0:128, 0:128]).then_inc(S["vch"], 1)
                    vch[0] += 1
                    selfw("c")
                    v.tensor_mul(comp_st[:, qt * 128:(qt + 1) * 128], ps_o2[0:64, :], invb64[:])
                    v.tensor_mul(prb[:], pc[:], invb128[:]).then_inc(S["vch"], 1)
                    vch[0] += 1
                    selfw("c2")
                    v.reduce_sum(bstmp[:], prb[:], axis=AX.X).then_inc(S["vch"], 1)
                    vch[0] += 1
                    selfw("c3")
                    v.tensor_add(bsacc[:], bstmp[:], bsacc[:]).then_inc(S["cnc"], 1)
                v.wait_ge(S["bsp"], h + 1)
                if h:
                    v.wait_ge(S["sdm"], 64 * h)
                v.tensor_copy(bsrow[:], ps_c[0:1, 0:128]).then_inc(S["bse"], 1)
                v.wait_ge(S["bse"], h + 1)
                v.max(top8[:], bsrow[:]).then_inc(S["vch"], 1)
                vch[0] += 1
                selfw("t8")
                v.tensor_scalar(sel01[:], bsrow[:], top8[0:1, 7:8], None, ALU.is_ge
                                ).then_inc(S["thr"], 1)
                v.tensor_scalar(bsacc[:], bsacc[:], 0.0, None, ALU.mult)
                v.wait_ge(S["sde"], 64 * (h + 1))
                if h:
                    v.wait_ge(S["spvc"], 256 * h)
                for tc in range(32):
                    ts = v.tensor_scalar(vselp[:, tc * 65:(tc + 1) * 65],
                                         vp[h][:, tc * 65:(tc + 1) * 65],
                                         selkf[:, tc:tc + 1], None, ALU.mult)
                ts.then_inc(S["vs"], 1)
                for qb in range(8):
                    iq = h * 8 + qb
                    v.wait_ge(S["spvc"], 32 * (iq + 1))
                    v.reciprocal(invsb[64:65, 0:512], ps_o[64:65, :]).then_inc(S["snr"], 1)
                    v.wait_ge(S["sbc"], iq + 1)
                    v.tensor_copy(invb[:], ps_g[0:64, :]).then_inc(S["snc"], 1)
                    v.wait_ge(S["snc"], iq + 1)
                    v.tensor_mul(sel_sb[:], ps_o[0:65 - 1, :], invb[:]).then_inc(S["sev"], 1)
                    for s4 in range(4):
                        i4x = iq * 4 + s4
                        v.wait_ge(S["dpv"], i4x + 1)
                        v.reciprocal(invsb[64:65, 0:128], ps_o2[64:65, :]).then_inc(S["dnr"], 1)
                        v.wait_ge(S["dbc"], i4x + 1)
                        v.tensor_copy(invb64[:], ps_s1[0:64, 0:128]).then_inc(S["vch"], 1)
                        vch[0] += 1
                        selfw("d")
                        v.tensor_mul(slide_st[:, s4 * 128:(s4 + 1) * 128], ps_o2[0:64, :],
                                     invb64[:]).then_inc(S["dnc"], 1)
                    for s4 in range(4):
                        qt = qb * 4 + s4
                        if iq >= 2 and s4 == 0:
                            v.wait_ge(S[f"so{iq % 2}"], 16 * (iq // 2))
                        v.tensor_mul(osum[:], comp_st[:, qt * 128:(qt + 1) * 128],
                                     gbT[0][:, qt * 128:(qt + 1) * 128])
                        v.tensor_mul(otmp[:], sel_sb[:, s4 * 128:(s4 + 1) * 128],
                                     gbT[1][:, qt * 128:(qt + 1) * 128])
                        v.tensor_add(osum[:], osum[:], otmp[:])
                        v.tensor_mul(otmp[:], slide_st[:, s4 * 128:(s4 + 1) * 128],
                                     gbT[2][:, qt * 128:(qt + 1) * 128])
                        v.tensor_add(obuf[qb % 2][:, s4 * 128:(s4 + 1) * 128], osum[:], otmp[:]
                                     ).then_inc(S["cmb"], 1)

    print("NSA sbuf remaining:", nc.sbuf_bytes_remaining)
    return nc


def _nsa_host(inputs):
    Wq = np.asarray(inputs["L1_attn_Wq"], np.float32)
    Wk = np.asarray(inputs["L1_attn_Wk"], np.float32)
    Wv = np.asarray(inputs["L1_attn_Wv"], np.float32)
    Wg = np.asarray(inputs["L1_attn_Wg"], np.float32)
    bq = np.asarray(inputs["L1_attn_bq"], np.float32)
    bk = np.asarray(inputs["L1_attn_bk"], np.float32)
    bv = np.asarray(inputs["L1_attn_bv"], np.float32)
    bg = np.asarray(inputs["L1_attn_bg"], np.float32)
    wgr = np.ascontiguousarray(Wg.reshape(8, 128, 3).transpose(1, 0, 2).reshape(128, 24))
    kk = np.arange(128)[:, None]
    qq = np.arange(128)[None, :]
    sm = np.zeros((3 * 128, 128), np.float32)
    sm[0:128] = np.where(kk > qq, 0.0, -1e9)
    sm[128:256] = np.where(kk <= qq, 0.0, -1e9)
    sm[256:384] = -1e9
    selm = np.zeros((32, 192), np.float32)
    for j in range(3):
        selm[j, j * 64:(j + 1) * 64] = 1.0
    out = []
    for ph in range(2):
        sl = slice(ph * 512, (ph + 1) * 512)
        bvp = np.zeros((32, 512), BF)
        bvp[0] = bv[sl].astype(BF)
        out.append({
            "Wq": np.ascontiguousarray(Wq[:, sl]),
            "Wk": np.ascontiguousarray(Wk[:, sl]),
            "Wv": np.ascontiguousarray(Wv[:, sl]),
            "Wgr": wgr,
            "bq": np.ascontiguousarray(bq[sl].reshape(4, 128).T),
            "bk": np.ascontiguousarray(bk[sl].reshape(4, 128).T),
            "bvp": bvp,
            "bg": bg.reshape(3, 1),
            "smask": sm.astype(BF),
            "selm": selm.astype(BF),
        })
    return out


def run_nsa(inputs, xnT_percore):
    nc = _get("nsa", build_nsa)
    host = _nsa_host(inputs)
    in_maps = []
    for c in range(NCORE):
        m = {"xnT": xnT_percore[c]}
        m.update(host[c % 2])
        in_maps.append(m)
    res = run_bass_kernel_spmd(nc, in_maps, core_ids=list(range(NCORE)))
    if NSA_DEBUG:
        run_nsa.dbg = [{k: np.asarray(res.results[c][k]) for k in ("ndbg", "gbdbg", "cmdbg", "pdbg")}
                       for c in range(NCORE)]
    return [np.asarray(res.results[c]["nsaT"]) for c in range(NCORE)]



def build_rwkv():
    nc = bass.Bass("TRN2", target_bir_lowering=False, num_devices=NCORE)
    TH = T // 2
    xfT = nc.dram_tensor("xfT", [E, TH + 1], F32, kind="ExternalInput").ap()
    x3T = nc.dram_tensor("x3T", [E, TH], F32, kind="ExternalInput").ap()
    wrT_d = nc.dram_tensor("wrT", [E, E], F32, kind="ExternalInput").ap()
    wkT_d = nc.dram_tensor("wkT", [E, E], F32, kind="ExternalInput").ap()
    wvT_d = nc.dram_tensor("wvT", [E, E], F32, kind="ExternalInput").ap()
    mu_d = nc.dram_tensor("mucols", [128, 32], F32, kind="ExternalInput").ap()
    x4T_d = nc.dram_tensor("x4T", [E, TH], F32, kind="ExternalOutput").ap()

    TB = 512
    NTB = TH // TB

    wrT = [nc.alloc_sbuf_tensor(f"kwr{c}", [128, E], BF16) for c in range(8)]
    wkT = [nc.alloc_sbuf_tensor(f"kwk{c}", [128, E], BF16) for c in range(8)]
    wvT = [nc.alloc_sbuf_tensor(f"kwv{c}", [128, E], BF16) for c in range(8)]
    mu_sb = nc.alloc_sbuf_tensor("kmu", [128, 32], F32)
    xfin = [nc.alloc_sbuf_tensor(f"kxf{c}", [128, TB + 1], F32) for c in range(8)]
    xf3 = [nc.alloc_sbuf_tensor(f"kx3{c}", [128, TB], F32) for c in range(8)]
    xrb = [nc.alloc_sbuf_tensor(f"kxr{c}", [128, TB], BF16) for c in range(8)]
    xkb = [nc.alloc_sbuf_tensor(f"kxk{c}", [128, TB], BF16) for c in range(8)]
    rb = [nc.alloc_sbuf_tensor(f"krb{c}", [128, TB], F32) for c in range(8)]
    krl = [nc.alloc_sbuf_tensor(f"krl{c}", [128, TB], BF16) for c in range(8)]
    kab = [nc.alloc_sbuf_tensor(f"kka{c}", [128, TB], BF16) for c in range(8)]
    yb = [nc.alloc_sbuf_tensor(f"kyb{c}", [128, TB], BF16) for c in range(8)]
    ysq = [nc.alloc_sbuf_tensor(f"kys{c}", [128, TB], BF16) for c in range(8)]
    x4f = [nc.alloc_sbuf_tensor(f"kx4{c}", [128, TB], F32) for c in range(8)]
    tmpm = nc.alloc_sbuf_tensor("ktmp", [128, TB], F32)
    tmp1 = nc.alloc_sbuf_tensor("ktm1", [128, TB], F32)
    rowbuf = nc.alloc_sbuf_tensor("krow", [1, 6 * TB], F32)
    srow = rowbuf[:, 0:TB]
    s2row = rowbuf[:, TB:2 * TB]
    vrow = rowbuf[:, 2 * TB:3 * TB]
    stdrow = rowbuf[:, 3 * TB:4 * TB]
    arow = rowbuf[:, 4 * TB:5 * TB]
    rowbuf_b = nc.alloc_sbuf_tensor("krow_b", [32, 2 * TB], BF16)
    mrow_b = rowbuf_b[:, 0:TB]
    arow_b = rowbuf_b[:, TB:2 * TB]
    ab_sb = nc.alloc_sbuf_tensor("kab_sb", [128, TB], F32)
    mb_sb = nc.alloc_sbuf_tensor("kmb_sb", [128, TB], F32)
    onesrow_b = nc.alloc_sbuf_tensor("kones", [32, 128], BF16)
    onescol_b = nc.alloc_sbuf_tensor("konec", [128, 1], BF16)

    ps_a = nc.alloc_psum_tensor("kps_a", [128, 512], F32)
    ps_b = nc.alloc_psum_tensor("kps_b", [128, 512], F32)
    ps_stat = nc.alloc_psum_tensor("kps_st", [1, 2 * TB], F32)
    ps_bc = nc.alloc_psum_tensor("kps_bc", [128, TB], F32)
    ps_bc2 = nc.alloc_psum_tensor("kps_b2", [128, TB], F32)

    S = {n: nc.alloc_semaphore("k_" + n) for n in
         ["lw", "cw", "gi", "xfl", "x3l", "mx", "mv", "yes", "ksq", "yev",
          "sq", "st", "rm1", "sqd", "rowm", "bc", "x4d", "so"]}

    with nc.Block() as block:

        @block.gpsimd
        def _(g):
            g.dma_start(out=mu_sb[:], in_=mu_d[:]).then_inc(S["cw"], 16)
            for c in range(8):
                g.dma_start(out=wrT[c][:], in_=wrT_d[c * 128:(c + 1) * 128, :]).then_inc(S["lw"], 16)
            for c in range(8):
                g.dma_start(out=wkT[c][:], in_=wkT_d[c * 128:(c + 1) * 128, :]).then_inc(S["lw"], 16)
            for c in range(8):
                g.dma_start(out=wvT[c][:], in_=wvT_d[c * 128:(c + 1) * 128, :]).then_inc(S["lw"], 16)
            g.memset(onesrow_b[:], 0.0)
            g.memset(onesrow_b[0:1, :], 1.0).then_inc(S["gi"], 1)
            g.memset(onescol_b[:], 1.0).then_inc(S["gi"], 1)
            g.memset(rowbuf_b[:], 0.0).then_inc(S["gi"], 1)

        @block.sync
        def _(sync):
            for tb in range(NTB):
                if tb > 0:
                    sync.wait_ge(S["mx"], 8 * tb)
                for c in range(8):
                    sync.dma_start(out=xfin[c][:],
                                   in_=xfT[c * 128:(c + 1) * 128, tb * TB:(tb + 1) * TB + 1]
                                   ).then_inc(S["xfl"], 16)
                if tb > 0:
                    sync.wait_ge(S["x4d"], 8 * tb)
                for c in range(8):
                    sync.dma_start(out=xf3[c][:],
                                   in_=x3T[c * 128:(c + 1) * 128, tb * TB:(tb + 1) * TB]
                                   ).then_inc(S["x3l"], 16)
                sync.wait_ge(S["x4d"], 8 * (tb + 1))
                for c in range(8):
                    sync.dma_start(out=x4T_d[c * 128:(c + 1) * 128, tb * TB:(tb + 1) * TB],
                                   in_=x4f[c][:]).then_inc(S["so"], 16)
            sync.wait_ge(S["so"], 16 * 8 * NTB)

        @block.tensor
        def _(t):
            t.wait_ge(S["lw"], 16 * 24)
            t.wait_ge(S["gi"], 3)
            for tb in range(NTB):
                t.wait_ge(S["mx"], 8 * (tb + 1))
                for e in range(8):
                    if e >= 2:
                        t.wait_ge(S["yes"], 16 * tb + e - 1)
                    elif tb > 0:
                        t.wait_ge(S["yev"], 8 * tb + e - 1)
                    bank = ps_a if (24 * tb + e) % 2 == 0 else ps_b
                    for c in range(8):
                        i = t.matmul(bank[:], wrT[c][:, e * 128:(e + 1) * 128], xrb[c][:],
                                     start=(c == 0), stop=(c == 7))
                    i.then_inc(S["mv"], 1)
                for e in range(8):
                    if e >= 2:
                        t.wait_ge(S["yes"], 16 * tb + 8 + e - 1)
                    else:
                        t.wait_ge(S["yes"], 16 * tb + e + 7)
                    bank = ps_a if (24 * tb + 8 + e) % 2 == 0 else ps_b
                    for c in range(8):
                        i = t.matmul(bank[:], wkT[c][:, e * 128:(e + 1) * 128], xkb[c][:],
                                     start=(c == 0), stop=(c == 7))
                    i.then_inc(S["mv"], 1)
                t.wait_ge(S["ksq"], 8 * (tb + 1))
                for e in range(8):
                    if e >= 2:
                        t.wait_ge(S["yev"], 8 * tb + e - 1)
                    else:
                        t.wait_ge(S["yes"], 16 * tb + e + 15)
                    bank = ps_a if (24 * tb + 16 + e) % 2 == 0 else ps_b
                    for c in range(8):
                        i = t.matmul(bank[:], wvT[c][:, e * 128:(e + 1) * 128], kab[c][:],
                                     start=(c == 0), stop=(c == 7))
                    i.then_inc(S["mv"], 1)
                t.wait_ge(S["sq"], 8 * (tb + 1))
                if tb > 0:
                    t.wait_ge(S["rm1"], tb)
                    t.wait_ge(S["rowm"], 2 * tb)
                for c in range(8):
                    t.matmul(ps_stat[:, 0:TB], onescol_b[:], yb[c][:],
                             start=(c == 0), stop=(c == 7))
                for c in range(8):
                    i = t.matmul(ps_stat[:, TB:2 * TB], onescol_b[:], ysq[c][:],
                                 start=(c == 0), stop=(c == 7))
                i.then_inc(S["st"], 1)
                t.wait_ge(S["rowm"], 2 * tb + 1)
                t.matmul(ps_bc[:], onesrow_b[:], mrow_b[0:32, :], start=True, stop=True)
                i = t.matmul(ps_bc2[:], onesrow_b[:], arow_b[0:32, :], start=True, stop=True)
                i.then_inc(S["bc"], 1)

        @block.scalar
        def _(sc):
            for tb in range(NTB):
                for e in range(8):
                    sc.wait_ge(S["mv"], 24 * tb + e + 1)
                    bank = ps_a if (24 * tb + e) % 2 == 0 else ps_b
                    sc.activation(rb[e][:], bank[:], AF.Sigmoid).then_inc(S["yes"], 1)
                for e in range(8):
                    sc.wait_ge(S["mv"], 24 * tb + 8 + e + 1)
                    bank = ps_a if (24 * tb + 8 + e) % 2 == 0 else ps_b
                    sc.activation(krl[e][:], bank[:], AF.Relu).then_inc(S["yes"], 1)
                    sc.activation(kab[e][:], krl[e][:], AF.Square).then_inc(S["ksq"], 1)
                for e in range(8):
                    sc.wait_ge(S["yev"], 8 * tb + e + 1)
                    sc.activation(ysq[e][:], yb[e][:], AF.Square).then_inc(S["sq"], 1)
                sc.wait_ge(S["rm1"], tb + 1)
                sc.activation(stdrow, s2row, AF.Sqrt).then_inc(S["sqd"], 1)

        @block.vector
        def _(v):
            v.wait_ge(S["cw"], 16)
            v.wait_ge(S["gi"], 3)
            for tb in range(NTB):
                v.wait_ge(S["xfl"], 128 * (tb + 1))
                if tb > 0:
                    v.wait_ge(S["mv"], 24 * (tb - 1) + 16)
                for c in range(8):
                    v.tensor_scalar(tmpm[:], xfin[c][:, 0:TB], mu_sb[:, 8 + c:9 + c],
                                    None, ALU.mult)
                    v.scalar_tensor_tensor(xrb[c][:], xfin[c][:, 1:TB + 1], mu_sb[:, c:c + 1],
                                           tmpm[:], ALU.mult, ALU.add)
                    v.tensor_scalar(tmpm[:], xfin[c][:, 0:TB], mu_sb[:, 24 + c:25 + c],
                                    None, ALU.mult)
                    v.scalar_tensor_tensor(xkb[c][:], xfin[c][:, 1:TB + 1], mu_sb[:, 16 + c:17 + c],
                                           tmpm[:], ALU.mult, ALU.add).then_inc(S["mx"], 1)
                for e in range(8):
                    v.wait_ge(S["mv"], 24 * tb + 16 + e + 1)
                    v.wait_ge(S["yes"], 16 * tb + e + 1)
                    if tb > 0 and e == 0:
                        v.wait_ge(S["so"], 16 * 8 * tb)
                    bank = ps_a if (24 * tb + 16 + e) % 2 == 0 else ps_b
                    v.tensor_mul(yb[e][:], bank[:], rb[e][:]).then_inc(S["yev"], 1)
                v.wait_ge(S["st"], tb + 1)
                v.tensor_scalar_mul(srow, ps_stat[:, 0:TB], 1.0 / 1024)
                v.tensor_scalar_mul(s2row, ps_stat[:, TB:2 * TB], 1.0 / 1024)
                v.tensor_mul(vrow, srow, srow)
                v.tensor_sub(s2row, s2row, vrow)
                v.tensor_scalar_add(s2row, s2row, EPS).then_inc(S["rm1"], 1)
                v.wait_ge(S["sqd"], tb + 1)
                v.reciprocal(arow, stdrow)
                v.tensor_copy(mrow_b[0:1, :], srow)
                v.tensor_copy(arow_b[0:1, :], arow).then_inc(S["rowm"], 1)
                v.wait_ge(S["bc"], tb + 1)
                v.wait_ge(S["x3l"], 128 * (tb + 1))
                v.tensor_copy(ab_sb[:], ps_bc2[:])
                v.tensor_copy(mb_sb[:], ps_bc[:]).then_inc(S["rowm"], 1)
                for c in range(8):
                    v.tensor_sub(tmp1[:], yb[c][:], mb_sb[:])
                    v.tensor_mul(tmp1[:], tmp1[:], ab_sb[:])
                    v.tensor_add(x4f[c][:], xf3[c][:], tmp1[:]).then_inc(S["x4d"], 1)

    print("RWKV sbuf remaining:", nc.sbuf_bytes_remaining)
    return nc


def run_rwkv(inputs, xf_full, x3_full):
    nc = _get("rwkv", build_rwkv)
    TH = T // 2
    mu_r = np.asarray(inputs["L1_ffn_mu_r"], np.float32)
    mu_k = np.asarray(inputs["L1_ffn_mu_k"], np.float32)
    mu = np.zeros((128, 32), np.float32)
    mu[:, 0:8] = mu_r.reshape(8, 128).T
    mu[:, 8:16] = (1.0 - mu_r).reshape(8, 128).T
    mu[:, 16:24] = mu_k.reshape(8, 128).T
    mu[:, 24:32] = (1.0 - mu_k).reshape(8, 128).T
    wrT = np.ascontiguousarray(np.asarray(inputs["L1_ffn_W_r"], np.float32).T)
    wkT = np.ascontiguousarray(np.asarray(inputs["L1_ffn_W_k"], np.float32).T)
    wvT = np.ascontiguousarray(np.asarray(inputs["L1_ffn_W_v"], np.float32).T)
    in_maps = []
    for c in range(NCORE):
        b, p = c // 2, c % 2
        xfTc = np.zeros((E, TH + 1), np.float32)
        xfTc[:, 1:] = xf_full[b][:, p * TH:(p + 1) * TH]
        if p == 1:
            xfTc[:, 0] = xf_full[b][:, TH - 1]
        in_maps.append({
            "xfT": xfTc,
            "x3T": np.ascontiguousarray(x3_full[b][:, p * TH:(p + 1) * TH]),
            "wrT": wrT, "wkT": wkT, "wvT": wvT,
            "mucols": mu,
        })
    res = run_bass_kernel_spmd(nc, in_maps, core_ids=list(range(NCORE)))
    out = np.zeros((4, E, T), np.float32)
    for c in range(NCORE):
        b, p = c // 2, c % 2
        out[b][:, p * TH:(p + 1) * TH] = np.asarray(res.results[c]["x4T"])
    return out


def kernel(**inputs):
    TH = T // 2
    x = np.asarray(inputs["L0_attn_x"], np.float32)
    xT_percore = [np.ascontiguousarray(x[c // 2].T) for c in range(NCORE)]

    mla_outs = run_mla(inputs, xT_percore)
    x1h, r0h = run_mlafin(inputs, xT_percore, mla_outs)

    r = np.zeros((4, E, T), np.float32)
    for c in range(NCORE):
        b, p = c // 2, c % 2
        r[b][:, p * TH:(p + 1) * TH] = r0h[c]
    for L in range(6):
        r = run_iattn(inputs, L, r)
        r = run_iffn(inputs, L, r)

    r6h = [np.ascontiguousarray(r[c // 2][:, (c % 2) * TH:((c % 2) + 1) * TH]).astype(BF)
           for c in range(NCORE)]
    eye = np.eye(E, dtype=np.float32)
    x2h, xn3h = run_addln(x1h, r6h, eye)

    xn3T_percore = [np.ascontiguousarray(np.concatenate([xn3h[2 * (c // 2)],
                                                         xn3h[2 * (c // 2) + 1]], axis=1))
                    for c in range(NCORE)]
    nsa_outs = run_nsa(inputs, xn3T_percore)
    moh = []
    for c in range(NCORE):
        b, p = c // 2, c % 2
        o1T = np.concatenate([nsa_outs[2 * b], nsa_outs[2 * b + 1]], axis=0)
        moh.append(np.ascontiguousarray(o1T[:, p * TH:(p + 1) * TH]))
    x3h, xfh = run_addln(x2h, moh, inputs["L1_attn_Wo"], inputs["L1_attn_bo"])

    xf_full = np.zeros((4, E, T), np.float32)
    x3_full = np.zeros((4, E, T), np.float32)
    for c in range(NCORE):
        b, p = c // 2, c % 2
        xf_full[b][:, p * TH:(p + 1) * TH] = xfh[c]
        x3_full[b][:, p * TH:(p + 1) * TH] = x3h[c]
    x4 = run_rwkv(inputs, xf_full, x3_full)
    return np.ascontiguousarray(x4.transpose(0, 2, 1))




# version 04 (final)
# speedup vs baseline: 1.1835x; 1.1819x over previous
import numpy as np
import ml_dtypes
import concourse.bass as bass
import concourse.mybir as mybir
from concourse.bass_utils import run_bass_kernel_spmd

F32 = mybir.dt.float32
BF16 = mybir.dt.bfloat16
AF = mybir.ActivationFunctionType
ALU = mybir.AluOpType
AX = mybir.AxisListType
ET = mybir.EngineType

BF = ml_dtypes.bfloat16
T = 4096
E = 1024
NCORE = 8
EPS = 1e-5
C0 = 30.0


def attn_plan(qb):
    lmin = 256 * (2 * qb + 1)
    ch = [(kc, kc * 128, 0, 512) for kc in range(lmin // 128)]
    ch += [(lmin // 128 + j, lmin + j * 128, 256, 256) for j in range(2)]
    return ch


PTRING = 16


def build_mla():
    nc = bass.Bass("TRN2", target_bir_lowering=False, num_devices=NCORE)
    xT = nc.dram_tensor("xT", [E, T], F32, kind="ExternalInput").ap()
    wq = nc.dram_tensor("wq", [E, 512], F32, kind="ExternalInput").ap()
    wdn = nc.dram_tensor("wdn", [E, 512], F32, kind="ExternalInput").ap()
    wku = nc.dram_tensor("wku", [512, 512], F32, kind="ExternalInput").ap()
    wvu = nc.dram_tensor("wvu", [512, 512], F32, kind="ExternalInput").ap()
    ropeC = nc.dram_tensor("ropeC", [128, T], BF16, kind="ExternalInput").ap()
    ropeS = nc.dram_tensor("ropeS", [128, T], BF16, kind="ExternalInput").ap()
    swp = nc.dram_tensor("swp", [128, 128], BF16, kind="ExternalInput").ap()
    outT_d = nc.dram_tensor("outT", [512, T], BF16, kind="ExternalOutput").ap()

    TB = 256
    NTB = T // TB

    xbfr = [nc.alloc_sbuf_tensor(f"xbfr{c}", [128, TB], BF16) for c in range(8)]
    xnbf = [nc.alloc_sbuf_tensor(f"xnbf{c}", [128, TB], BF16) for c in range(8)]
    rowbuf = nc.alloc_sbuf_tensor("rowbuf", [1, 6 * TB], F32)
    srow = rowbuf[:, 0:TB]
    s2row = rowbuf[:, TB:2 * TB]
    vrow = rowbuf[:, 2 * TB:3 * TB]
    stdrow = rowbuf[:, 3 * TB:4 * TB]
    arow = rowbuf[:, 4 * TB:5 * TB]
    rowbuf_b = nc.alloc_sbuf_tensor("rowbuf_b", [32, 2 * TB], BF16)
    mrow_b = rowbuf_b[:, 0:TB]
    arow_b = rowbuf_b[:, TB:2 * TB]
    ab_sb = nc.alloc_sbuf_tensor("ab_sb", [128, TB], F32)
    onesrow_b = nc.alloc_sbuf_tensor("onesrow_b", [32, 128], BF16)
    onesrow_f = nc.alloc_sbuf_tensor("onesrow_f", [96, 64], F32)
    onescol_b = nc.alloc_sbuf_tensor("onescol_b", [128, 1], BF16)
    negc0 = nc.alloc_sbuf_tensor("negc0", [128, 1], F32)

    wq_sb = [nc.alloc_sbuf_tensor(f"wqs{c}", [128, 512], BF16) for c in range(8)]
    wdn_sb = [nc.alloc_sbuf_tensor(f"wdns{c}", [128, 512], BF16) for c in range(8)]
    wku_sb = [nc.alloc_sbuf_tensor(f"wkus{c}", [128, 512], BF16) for c in range(4)]
    wvu_sb = [nc.alloc_sbuf_tensor(f"wvus{c}", [128, 512], BF16) for c in range(4)]

    qT = [nc.alloc_sbuf_tensor(f"qT{i}", [128, T], BF16) for i in range(4)]
    kT = [nc.alloc_sbuf_tensor(f"kT{i}", [128, T], BF16) for i in range(4)]
    latT = [nc.alloc_sbuf_tensor(f"latT{i}", [128, T], BF16) for i in range(4)]
    vp = [nc.alloc_sbuf_tensor(f"vp{h}", [128, 32 * 65], BF16) for h in range(8)]
    swp_sb = nc.alloc_sbuf_tensor("swp_sb", [128, 128], BF16)
    rtmp0 = nc.alloc_sbuf_tensor("rtmp0", [128, 512], BF16)
    rtmp1 = nc.alloc_sbuf_tensor("rtmp1", [128, 512], BF16)
    pt = nc.alloc_sbuf_tensor("pt", [128, PTRING * 512], BF16)
    invsb = nc.alloc_sbuf_tensor("invsb", [96, 512], F32)
    invb_sb = nc.alloc_sbuf_tensor("invb_sb", [64, 512], F32)
    obuf = [nc.alloc_sbuf_tensor(f"obuf{i}", [64, 512], BF16) for i in range(2)]
    pt_addr = nc.lookup_mloc(pt).addr
    cCb = nc.alloc_sbuf_tensor_at("cCb", [128, T], BF16, offset=pt_addr)
    cSb = nc.alloc_sbuf_tensor_at("cSb", [128, T], BF16, offset=pt_addr + 8192)

    ps_a = nc.alloc_psum_tensor("ps_a", [128, 512], F32)
    ps_b = nc.alloc_psum_tensor("ps_b", [128, 512], F32)
    ps_stat = nc.alloc_psum_tensor("ps_stat", [1, 512], F32)
    ps_bc = nc.alloc_psum_tensor("ps_bc", [128, TB], F32)
    ps_s0 = nc.alloc_psum_tensor("ps_s0", [128, 512], F32)
    ps_s1 = nc.alloc_psum_tensor("ps_s1", [128, 512], F32)
    ps_o = nc.alloc_psum_tensor("ps_o", [128, 512], F32)
    ps_b2 = nc.alloc_psum_tensor("ps_b2", [64, 512], F32)

    S = {n: nc.alloc_semaphore(n) for n in
         ["lw", "gi", "xl", "sq", "st", "rm1", "sqd", "rowm", "bcm",
          "xnd", "fx", "bca", "aok", "qkm", "pj", "kvm", "kvev", "sw", "rpa",
          "rp", "qk", "ex", "pvc", "pv", "nmr", "nmc", "nm", "bcv", "so",
          "so0", "so1", "gch"]}
    NW = 27

    with nc.Block() as block:

        @block.sync
        def _(sync):
            for h in range(8):
                for qb in range(8):
                    idx = h * 8 + qb
                    sync.wait_ge(S["nm"], idx + 1)
                    sync.dma_start(out=outT_d[h * 64:(h + 1) * 64, qb * 512:(qb + 1) * 512],
                                   in_=obuf[idx % 2][:]).then_inc(S[f"so{idx % 2}"], 16)
            sync.wait_ge(S["so0"], 16 * 32)
            sync.wait_ge(S["so1"], 16 * 32)

        @block.gpsimd
        def _(g):
            for c in range(8):
                g.dma_start(out=wq_sb[c][:], in_=wq[c * 128:(c + 1) * 128, :]).then_inc(S["lw"], 16)
            for c in range(8):
                g.dma_start(out=wdn_sb[c][:], in_=wdn[c * 128:(c + 1) * 128, :]).then_inc(S["lw"], 16)
            for c in range(4):
                g.dma_start(out=wku_sb[c][:], in_=wku[c * 128:(c + 1) * 128, :]).then_inc(S["lw"], 16)
            for c in range(4):
                g.dma_start(out=wvu_sb[c][:], in_=wvu[c * 128:(c + 1) * 128, :]).then_inc(S["lw"], 16)
            g.dma_start(out=cCb[:], in_=ropeC[:]).then_inc(S["lw"], 16)
            g.dma_start(out=cSb[:], in_=ropeS[:]).then_inc(S["lw"], 16)
            g.dma_start(out=swp_sb[:], in_=swp[:]).then_inc(S["lw"], 16)
            g.memset(onesrow_b[:], 0.0).then_inc(S["gch"], 1)
            g.wait_ge(S["gch"], 1)
            g.memset(onesrow_b[0:1, :], 1.0).then_inc(S["gi"], 1)
            g.memset(onesrow_f[64:96, :], 0.0).then_inc(S["gch"], 1)
            g.wait_ge(S["gch"], 2)
            g.memset(onesrow_f[64:65, :], 1.0).then_inc(S["gi"], 1)
            g.memset(onescol_b[:], 1.0).then_inc(S["gi"], 1)
            g.memset(negc0[:], -C0).then_inc(S["gi"], 1)
            g.memset(rowbuf_b[:], 0.0).then_inc(S["gi"], 1)
            g.memset(invsb[64:96, :], 0.0).then_inc(S["gi"], 1)
            for h in range(8):
                g.memset(vp[h][:].rearrange("p (a b) -> p a b", b=65)[:, :, 64:65], 1.0
                         ).then_inc(S["gi"], 1)
            for tb in range(NTB):
                if tb > 0:
                    g.wait_ge(S["xnd"], 8 * tb)
                for c in range(8):
                    g.dma_start(out=xbfr[c][:],
                                in_=xT[c * 128:(c + 1) * 128, tb * TB:(tb + 1) * TB]
                                ).then_inc(S["xl"], 16)

        @block.scalar
        def _(sc):
            for tb in range(NTB):
                sc.wait_ge(S["xl"], 128 * (tb + 1))
                if tb > 0:
                    sc.wait_ge(S["qkm"], 8 * tb)
                for c in range(8):
                    sc.activation(xnbf[c][:], xbfr[c][:], AF.Square).then_inc(S["sq"], 1)
                sc.wait_ge(S["rm1"], tb + 1)
                sc.activation(stdrow, s2row, AF.Sqrt).then_inc(S["sqd"], 1)
            sc.wait_ge(S["gi"], 14)
            qk_n = 0
            pvc_n = 0
            for h in range(8):
                for qb in range(8):
                    plan = attn_plan(qb)
                    for (kc, ks, qo, ql) in plan:
                        qk_n += 1
                        sc.wait_ge(S["qk"], qk_n)
                        if kc >= PTRING:
                            sc.wait_ge(S["pvc"], pvc_n + kc - PTRING + 1)
                        bank = ps_s0 if ((qk_n - 1) % 2 == 0) else ps_s1
                        slot = kc % PTRING
                        sc.activation(pt[:, slot * 512 + qo: slot * 512 + qo + ql],
                                      bank[:, qo:qo + ql], AF.Exp,
                                      bias=negc0[:], scale=0.125).then_inc(S["ex"], 1)
                    pvc_n += len(plan)

        @block.tensor
        def _(t):
            t.wait_ge(S["lw"], 16 * NW)
            t.wait_ge(S["gi"], 14)
            pj_n = 0
            for tb in range(NTB):
                t.wait_ge(S["sq"], 8 * tb + 8)
                if tb > 0:
                    t.wait_ge(S["rowm"], 3 * tb)
                for c in range(8):
                    t.matmul(ps_stat[:, 0:TB], onescol_b[:], xbfr[c][:],
                             start=(c == 0), stop=(c == 7))
                for c in range(8):
                    i = t.matmul(ps_stat[:, TB:2 * TB], onescol_b[:], xnbf[c][:],
                                 start=(c == 0), stop=(c == 7))
                i.then_inc(S["st"], 1)
                t.wait_ge(S["rowm"], 3 * tb + 2)
                t.matmul(ps_bc[:], onesrow_b[:, 0:128], mrow_b[0:32, :], start=True, stop=True
                         ).then_inc(S["bcm"], 1)
                t.wait_ge(S["xnd"], 8 * tb + 8)
                t.wait_ge(S["rowm"], 3 * tb + 3)
                t.matmul(ps_bc[:], onesrow_b[:, 0:128], arow_b[0:32, :], start=True, stop=True
                         ).then_inc(S["bca"], 1)
                t.wait_ge(S["aok"], tb + 1)
                for j in range(8):
                    w_sb = wq_sb if j < 4 else wdn_sb
                    e = j % 4
                    bank = ps_a if pj_n % 2 == 0 else ps_b
                    if pj_n >= 2:
                        t.wait_ge(S["pj"], pj_n - 1)
                    for c in range(8):
                        i = t.matmul(bank[:, 0:TB], w_sb[c][:, e * 128:(e + 1) * 128],
                                     xnbf[c][:], start=(c == 0), stop=(c == 7))
                    i.then_inc(S["qkm"], 1)
                    pj_n += 1
            kv_n = 0
            for g in range(64):
                bank = ps_a if kv_n % 2 == 0 else ps_b
                if kv_n < 2:
                    t.wait_ge(S["pj"], 8 * NTB)
                else:
                    t.wait_ge(S["kvev"], kv_n - 1)
                if g < 32:
                    e, tbb = g // 8, g % 8
                    for c in range(4):
                        i = t.matmul(bank[:], wku_sb[c][:, e * 128:(e + 1) * 128],
                                     latT[c][:, tbb * 512:(tbb + 1) * 512],
                                     start=(c == 0), stop=(c == 3))
                else:
                    tc = g - 32
                    for c in range(4):
                        i = t.matmul(bank[:], latT[c][:, tc * 128:(tc + 1) * 128],
                                     wvu_sb[c][:], start=(c == 0), stop=(c == 3))
                i.then_inc(S["kvm"], 1)
                kv_n += 1
            sw_n = 0
            for r in range(64):
                src = qT if r < 32 else kT
                i4, blk = (r % 32) // 8, r % 8
                bank = ps_a if sw_n % 2 == 0 else ps_b
                if sw_n < 2:
                    t.wait_ge(S["kvev"], 64)
                else:
                    t.wait_ge(S["rpa"], sw_n - 1)
                t.matmul(bank[:], swp_sb[:], src[i4][:, blk * 512:(blk + 1) * 512],
                         start=True, stop=True).then_inc(S["sw"], 1)
                sw_n += 1
            t.wait_ge(S["rp"], 64)
            qk_n = 0
            ex_n = 0
            for h in range(8):
                i4, ro = h // 2, (h % 2) * 64
                for qb in range(8):
                    idx = h * 8 + qb
                    plan = attn_plan(qb)
                    npl = len(plan)
                    if idx > 0:
                        t.wait_ge(S["nm"], idx)
                    for pi, (kc, ks, qo, ql) in enumerate(plan):
                        bank = ps_s0 if (qk_n % 2 == 0) else ps_s1
                        if qk_n >= 2:
                            t.wait_ge(S["ex"], qk_n - 1)
                        t.matmul(bank[:, qo:qo + ql],
                                 kT[i4][ro:ro + 64, ks:ks + 128],
                                 qT[i4][ro:ro + 64, qb * 512 + qo: qb * 512 + qo + ql],
                                 start=True, stop=True).then_inc(S["qk"], 1)
                        qk_n += 1
                        if pi >= 1:
                            pkc, pks, pqo, pql = plan[pi - 1]
                            t.wait_ge(S["ex"], ex_n + pi)
                            slot = pkc % PTRING
                            i = t.matmul(ps_o[0:65, pqo:pqo + pql],
                                         vp[h][:, pkc * 65:(pkc + 1) * 65],
                                         pt[:, slot * 512 + pqo: slot * 512 + pqo + pql],
                                         start=(pi - 1 == 0), stop=False)
                            i.then_inc(S["pvc"], 1)
                    pkc, pks, pqo, pql = plan[npl - 1]
                    t.wait_ge(S["ex"], ex_n + npl)
                    slot = pkc % PTRING
                    i = t.matmul(ps_o[0:65, pqo:pqo + pql],
                                 vp[h][:, pkc * 65:(pkc + 1) * 65],
                                 pt[:, slot * 512 + pqo: slot * 512 + pqo + pql],
                                 start=False, stop=True)
                    i.then_inc(S["pvc"], 1)
                    ex_n += npl
                    t.wait_ge(S["nmr"], idx + 1)
                    if idx > 0:
                        t.wait_ge(S["nmc"], idx)
                    t.matmul(ps_b2[:], onesrow_f[64:96, :], invsb[64:96, :],
                             start=True, stop=True).then_inc(S["bcv"], 1)

        @block.vector
        def _(v):
            for tb in range(NTB):
                v.wait_ge(S["st"], tb + 1)
                v.tensor_scalar_mul(srow, ps_stat[:, 0:TB], 1.0 / 1024)
                v.tensor_scalar_mul(s2row, ps_stat[:, TB:2 * TB], 1.0 / 1024).then_inc(S["rowm"], 1)
                v.wait_ge(S["rowm"], 3 * tb + 1)
                v.tensor_mul(vrow, srow, srow)
                v.tensor_sub(s2row, s2row, vrow)
                v.tensor_scalar_add(s2row, s2row, EPS).then_inc(S["rm1"], 1)
                v.wait_ge(S["sqd"], tb + 1)
                v.reciprocal(arow, stdrow)
                v.tensor_copy(mrow_b[0:1, :], srow).then_inc(S["rowm"], 1)
                v.tensor_copy(arow_b[0:1, :], arow).then_inc(S["rowm"], 1)
                v.wait_ge(S["bcm"], tb + 1)
                for c in range(8):
                    v.tensor_sub(xnbf[c][:], xbfr[c][:], ps_bc[:]).then_inc(S["xnd"], 1)
                v.wait_ge(S["bca"], tb + 1)
                v.tensor_copy(ab_sb[:], ps_bc[:]).then_inc(S["aok"], 1)
                for j in range(8):
                    v.wait_ge(S["qkm"], 8 * tb + j + 1)
                    bank = ps_a if (8 * tb + j) % 2 == 0 else ps_b
                    dst = qT if j < 4 else latT
                    e = j % 4
                    v.tensor_mul(dst[e][:, tb * TB:(tb + 1) * TB], bank[:, 0:TB], ab_sb[:]
                                 ).then_inc(S["pj"], 1)
            for g in range(64):
                v.wait_ge(S["kvm"], g + 1)
                bank = ps_a if g % 2 == 0 else ps_b
                if g < 32:
                    e, tbb = g // 8, g % 8
                    i = v.tensor_copy(kT[e][:, tbb * 512:(tbb + 1) * 512], bank[:])
                else:
                    tc = g - 32
                    for h in range(8):
                        i = v.tensor_copy(vp[h][:, tc * 65: tc * 65 + 64],
                                          bank[:, h * 64:(h + 1) * 64])
                i.then_inc(S["kvev"], 1)
            for r in range(64):
                src = qT if r < 32 else kT
                i4, blk = (r % 32) // 8, r % 8
                bank = ps_a if r % 2 == 0 else ps_b
                v.wait_ge(S["sw"], r + 1)
                v.tensor_copy(rtmp0[:], bank[:]).then_inc(S["rpa"], 1)
                v.tensor_mul(rtmp1[:], src[i4][:, blk * 512:(blk + 1) * 512],
                             cCb[:, blk * 512:(blk + 1) * 512])
                v.wait_ge(S["rpa"], r + 1)
                v.tensor_mul(rtmp0[:], rtmp0[:], cSb[:, blk * 512:(blk + 1) * 512])
                v.tensor_add(src[i4][:, blk * 512:(blk + 1) * 512], rtmp0[:], rtmp1[:]
                             ).then_inc(S["rp"], 1)
            cum = 0
            for h in range(8):
                for qb in range(8):
                    idx = h * 8 + qb
                    cum += len(attn_plan(qb))
                    v.wait_ge(S["pvc"], cum)
                    v.reciprocal(invsb[64:65, :], ps_o[64:65, :]).then_inc(S["nmr"], 1)
                    v.wait_ge(S["bcv"], idx + 1)
                    v.tensor_copy(invb_sb[:], ps_b2[:]).then_inc(S["nmc"], 1)
                    v.wait_ge(S["nmc"], idx + 1)
                    if idx >= 2:
                        v.wait_ge(S[f"so{idx % 2}"], 16 * (idx // 2))
                    v.tensor_mul(obuf[idx % 2][:], ps_o[0:64, :], invb_sb[:]
                                 ).then_inc(S["nm"], 1)

    print("MLA sbuf remaining:", nc.sbuf_bytes_remaining)
    return nc



def _mla_rope_tables():
    inv = 1.0 / 10000.0 ** (np.arange(0, 64, 2) / 64.0)
    ang = np.outer(inv, np.arange(T, dtype=np.float64))
    c, s = np.cos(ang), np.sin(ang)
    C = np.empty((64, T), np.float32)
    Sg = np.empty((64, T), np.float32)
    C[0::2] = c
    C[1::2] = c
    Sg[0::2] = -s
    Sg[1::2] = s
    return (np.concatenate([C, C], 0).astype(BF),
            np.concatenate([Sg, Sg], 0).astype(BF))


def _mla_swap():
    m = np.zeros((128, 128), np.float32)
    for i in range(0, 128, 2):
        m[i + 1, i] = 1.0
        m[i, i + 1] = 1.0
    return m.astype(BF)


_CACHE = {}


def _get(name, builder):
    if name not in _CACHE:
        _CACHE[name] = builder()
    return _CACHE[name]


def run_mla(inputs, xT_percore):
    wq = inputs["L0_attn_wq"]
    wdn = inputs["L0_attn_w_down"]
    wku = inputs["L0_attn_wk_up"]
    wvu = inputs["L0_attn_wv_up"]
    C2, S2 = _get("rope", _mla_rope_tables)
    swp = _get("swap", _mla_swap)
    nc = _get("mla", build_mla)
    in_maps = []
    for c in range(NCORE):
        b, p = c // 2, c % 2
        hs = slice(p * 512, (p + 1) * 512)
        in_maps.append({
            "xT": xT_percore[c],
            "wq": np.ascontiguousarray(wq[:, hs]),
            "wdn": np.ascontiguousarray(wdn),
            "wku": np.ascontiguousarray(wku[:, hs]),
            "wvu": np.ascontiguousarray(wvu[:, hs]),
            "ropeC": C2, "ropeS": S2, "swp": swp,
        })
    res = run_bass_kernel_spmd(nc, in_maps, core_ids=list(range(NCORE)))
    return [np.asarray(res.results[c]["outT"]) for c in range(NCORE)]




def build_mlafin():
    nc = bass.Bass("TRN2", target_bir_lowering=False, num_devices=NCORE)
    TH = T // 2
    xT = nc.dram_tensor("xT", [E, TH], F32, kind="ExternalInput").ap()
    moT = nc.dram_tensor("moT", [E, TH], BF16, kind="ExternalInput").ap()
    wo = nc.dram_tensor("wo", [E, E], F32, kind="ExternalInput").ap()
    bo_d = nc.dram_tensor("bo", [128, 8], F32, kind="ExternalInput").ap()
    x1T_d = nc.dram_tensor("x1T", [E, TH], F32, kind="ExternalOutput").ap()
    r0T_d = nc.dram_tensor("r0T", [E, TH], F32, kind="ExternalOutput").ap()

    TB = 512
    NTB = TH // TB

    mo_sb = [nc.alloc_sbuf_tensor(f"mo{c}", [128, TH], BF16) for c in range(8)]
    wo_sb = [nc.alloc_sbuf_tensor(f"wos{c}", [128, E], BF16) for c in range(8)]
    xf = [nc.alloc_sbuf_tensor(f"xf{c}", [128, TB], F32) for c in range(8)]
    yb = [nc.alloc_sbuf_tensor(f"yb{c}", [128, TB], BF16) for c in range(8)]
    ysq = [nc.alloc_sbuf_tensor(f"ysq{c}", [128, TB], BF16) for c in range(8)]
    x1f = [nc.alloc_sbuf_tensor(f"x1f{c}", [128, TB], F32) for c in range(8)]
    x1b = [nc.alloc_sbuf_tensor(f"x1b{c}", [128, TB], BF16) for c in range(8)]
    tmpf = [nc.alloc_sbuf_tensor(f"tmpf{c}", [128, TB], F32) for c in range(8)]
    rowbuf = nc.alloc_sbuf_tensor("rowbuf", [1, 6 * TB], F32)
    srow = rowbuf[:, 0:TB]
    s2row = rowbuf[:, TB:2 * TB]
    vrow = rowbuf[:, 2 * TB:3 * TB]
    stdrow = rowbuf[:, 3 * TB:4 * TB]
    arow = rowbuf[:, 4 * TB:5 * TB]
    rowbuf_b = nc.alloc_sbuf_tensor("rowbuf_b", [32, 2 * TB], BF16)
    mrow_b = rowbuf_b[:, 0:TB]
    arow_b = rowbuf_b[:, TB:2 * TB]
    ab_sb = nc.alloc_sbuf_tensor("ab_sb", [128, TB], F32)
    mb_sb = nc.alloc_sbuf_tensor("mb_sb", [128, TB], F32)
    onesrow_b = nc.alloc_sbuf_tensor("onesrow_b", [32, 128], BF16)
    onescol_b = nc.alloc_sbuf_tensor("onescol_b", [128, 1], BF16)
    bo_sb = nc.alloc_sbuf_tensor("bo_sb", [128, 8], F32)

    ps_a = nc.alloc_psum_tensor("ps_a", [128, 512], F32)
    ps_b = nc.alloc_psum_tensor("ps_b", [128, 512], F32)
    ps_stat = nc.alloc_psum_tensor("ps_stat", [1, 2 * TB], F32)
    ps_bc = nc.alloc_psum_tensor("ps_bc", [128, TB], F32)
    ps_bc2 = nc.alloc_psum_tensor("ps_bc2", [128, TB], F32)

    S = {n: nc.alloc_semaphore(n) for n in
         ["lw", "gi", "xl", "mv", "ye", "sq", "st", "rm1", "sqd", "rowm", "bc",
          "x1d", "x1s", "sq2", "st2", "rm12", "sqd2", "rowm2", "bc2", "r0d",
          "so", "fx", "gch"]}

    with nc.Block() as block:

        @block.gpsimd
        def _(g):
            for c in range(8):
                g.dma_start(out=wo_sb[c][:], in_=wo[c * 128:(c + 1) * 128, :]).then_inc(S["lw"], 16)
            for c in range(8):
                g.dma_start(out=mo_sb[c][:], in_=moT[c * 128:(c + 1) * 128, :]).then_inc(S["lw"], 16)
            g.dma_start(out=bo_sb[:], in_=bo_d[:]).then_inc(S["lw"], 16)
            g.memset(onesrow_b[:], 0.0).then_inc(S["gch"], 1)
            g.wait_ge(S["gch"], 1)
            g.memset(onesrow_b[0:1, :], 1.0).then_inc(S["gi"], 1)
            g.memset(onescol_b[:], 1.0).then_inc(S["gi"], 1)
            g.memset(rowbuf_b[:], 0.0).then_inc(S["gi"], 1)

        @block.sync
        def _(sync):
            for tb in range(NTB):
                if tb > 0:
                    sync.wait_ge(S["x1d"], 8 * tb)
                for c in range(8):
                    sync.dma_start(out=xf[c][:],
                                   in_=xT[c * 128:(c + 1) * 128, tb * TB:(tb + 1) * TB]
                                   ).then_inc(S["xl"], 16)
                sync.wait_ge(S["x1d"], 8 * (tb + 1))
                for c in range(8):
                    sync.dma_start(out=x1T_d[c * 128:(c + 1) * 128, tb * TB:(tb + 1) * TB],
                                   in_=x1f[c][:]).then_inc(S["so"], 16)
                sync.wait_ge(S["r0d"], 8 * (tb + 1))
                for c in range(8):
                    sync.dma_start(out=r0T_d[c * 128:(c + 1) * 128, tb * TB:(tb + 1) * TB],
                                   in_=tmpf[c][:]).then_inc(S["so"], 16)
            sync.wait_ge(S["so"], 16 * 16 * NTB)

        @block.tensor
        def _(t):
            t.wait_ge(S["lw"], 16 * 17)
            t.wait_ge(S["gi"], 3)
            pj = 0
            for tb in range(NTB):
                for e in range(8):
                    bank = ps_a if pj % 2 == 0 else ps_b
                    if pj >= 2:
                        t.wait_ge(S["ye"], pj - 1)
                    for c in range(8):
                        i = t.matmul(bank[:], wo_sb[c][:, e * 128:(e + 1) * 128],
                                     mo_sb[c][:, tb * TB:(tb + 1) * TB],
                                     start=(c == 0), stop=(c == 7))
                    i.then_inc(S["mv"], 1)
                    pj += 1
                t.wait_ge(S["sq"], 8 * (tb + 1))
                if tb > 0:
                    t.wait_ge(S["rm12"], tb)
                    t.wait_ge(S["rowm2"], 2 * tb)
                for c in range(8):
                    t.matmul(ps_stat[:, 0:TB], onescol_b[:], yb[c][:],
                             start=(c == 0), stop=(c == 7))
                for c in range(8):
                    i = t.matmul(ps_stat[:, TB:2 * TB], onescol_b[:], ysq[c][:],
                                 start=(c == 0), stop=(c == 7))
                i.then_inc(S["st"], 1)
                t.wait_ge(S["rowm"], 2 * tb + 1)
                t.matmul(ps_bc[:], onesrow_b[:], mrow_b[0:32, :], start=True, stop=True)
                i = t.matmul(ps_bc2[:], onesrow_b[:], arow_b[0:32, :], start=True, stop=True)
                i.then_inc(S["bc"], 1)
                t.wait_ge(S["sq2"], 8 * (tb + 1))
                t.wait_ge(S["rm1"], tb + 1)
                for c in range(8):
                    t.matmul(ps_stat[:, 0:TB], onescol_b[:], x1b[c][:],
                             start=(c == 0), stop=(c == 7))
                for c in range(8):
                    i = t.matmul(ps_stat[:, TB:2 * TB], onescol_b[:], ysq[c][:],
                                 start=(c == 0), stop=(c == 7))
                i.then_inc(S["st2"], 1)
                t.wait_ge(S["rowm"], 2 * tb + 2)
                t.wait_ge(S["rowm2"], 2 * tb + 1)
                t.matmul(ps_bc[:], onesrow_b[:], mrow_b[0:32, :], start=True, stop=True)
                i = t.matmul(ps_bc2[:], onesrow_b[:], arow_b[0:32, :], start=True, stop=True)
                i.then_inc(S["bc2"], 1)

        @block.scalar
        def _(sc):
            for tb in range(NTB):
                sc.wait_ge(S["ye"], 8 * tb + 8)
                for c in range(8):
                    sc.activation(ysq[c][:], yb[c][:], AF.Square).then_inc(S["sq"], 1)
                sc.wait_ge(S["rm1"], tb + 1)
                sc.activation(stdrow, s2row, AF.Sqrt).then_inc(S["sqd"], 1)
                sc.wait_ge(S["x1s"], 8 * (tb + 1))
                for c in range(8):
                    sc.activation(ysq[c][:], x1b[c][:], AF.Square).then_inc(S["sq2"], 1)
                sc.wait_ge(S["rm12"], tb + 1)
                sc.activation(stdrow, s2row, AF.Sqrt).then_inc(S["sqd2"], 1)

        @block.vector
        def _(v):
            for tb in range(NTB):
                if tb > 0:
                    v.wait_ge(S["so"], 16 * 16 * tb)
                for e in range(8):
                    v.wait_ge(S["mv"], 8 * tb + e + 1)
                    bank = ps_a if (8 * tb + e) % 2 == 0 else ps_b
                    v.tensor_scalar(yb[e][:], bank[:], bo_sb[:, e:e + 1], None, ALU.add
                                    ).then_inc(S["ye"], 1)
                v.wait_ge(S["st"], tb + 1)
                v.tensor_scalar_mul(srow, ps_stat[:, 0:TB], 1.0 / 1024)
                v.tensor_scalar_mul(s2row, ps_stat[:, TB:2 * TB], 1.0 / 1024)
                v.tensor_mul(vrow, srow, srow)
                v.tensor_sub(s2row, s2row, vrow)
                v.tensor_scalar_add(s2row, s2row, EPS).then_inc(S["rm1"], 1)
                v.wait_ge(S["sqd"], tb + 1)
                v.reciprocal(arow, stdrow)
                v.tensor_copy(mrow_b[0:1, :], srow)
                v.tensor_copy(arow_b[0:1, :], arow).then_inc(S["rowm"], 1)
                v.wait_ge(S["bc"], tb + 1)
                v.wait_ge(S["xl"], 128 * (tb + 1))
                v.tensor_copy(ab_sb[:], ps_bc2[:])
                v.tensor_copy(mb_sb[:], ps_bc[:]).then_inc(S["rowm"], 1)
                for c in range(8):
                    v.tensor_sub(tmpf[c][:], yb[c][:], mb_sb[:])
                for c in range(8):
                    v.tensor_mul(tmpf[c][:], tmpf[c][:], ab_sb[:])
                for c in range(8):
                    v.tensor_add(x1f[c][:], xf[c][:], tmpf[c][:]).then_inc(S["x1d"], 1)
                for c in range(8):
                    v.tensor_copy(x1b[c][:], x1f[c][:]).then_inc(S["x1s"], 1)
                v.wait_ge(S["st2"], tb + 1)
                v.tensor_scalar_mul(srow, ps_stat[:, 0:TB], 1.0 / 1024)
                v.tensor_scalar_mul(s2row, ps_stat[:, TB:2 * TB], 1.0 / 1024)
                v.tensor_mul(vrow, srow, srow)
                v.tensor_sub(s2row, s2row, vrow)
                v.tensor_scalar_add(s2row, s2row, EPS).then_inc(S["rm12"], 1)
                v.wait_ge(S["sqd2"], tb + 1)
                v.reciprocal(arow, stdrow)
                v.tensor_copy(mrow_b[0:1, :], srow)
                v.tensor_copy(arow_b[0:1, :], arow).then_inc(S["rowm2"], 1)
                v.wait_ge(S["bc2"], tb + 1)
                v.tensor_copy(ab_sb[:], ps_bc2[:])
                v.tensor_copy(mb_sb[:], ps_bc[:]).then_inc(S["rowm2"], 1)
                for c in range(8):
                    v.tensor_sub(tmpf[c][:], x1b[c][:], mb_sb[:])
                for c in range(8):
                    v.tensor_mul(tmpf[c][:], tmpf[c][:], ab_sb[:]).then_inc(S["r0d"], 1)

    print("MLAFIN sbuf remaining:", nc.sbuf_bytes_remaining)
    return nc


def run_addln(xT_half_percore, moT_half_percore, wo, bo=None):
    nc = _get("mlafin", build_mlafin)
    if bo is None:
        bocol = np.zeros((128, 8), np.float32)
    else:
        bocol = np.ascontiguousarray(np.asarray(bo, np.float32).reshape(8, 128).T)
    in_maps = []
    for c in range(NCORE):
        in_maps.append({
            "xT": xT_half_percore[c],
            "moT": moT_half_percore[c],
            "wo": np.ascontiguousarray(np.asarray(wo, np.float32)),
            "bo": bocol,
        })
    res = run_bass_kernel_spmd(nc, in_maps, core_ids=list(range(NCORE)))
    return ([np.asarray(res.results[c]["x1T"]) for c in range(NCORE)],
            [np.asarray(res.results[c]["r0T"]) for c in range(NCORE)])


def run_mlafin(inputs, xT_percore, mla_outs):
    TH = T // 2
    xh, mh = [], []
    for c in range(NCORE):
        b, p = c // 2, c % 2
        moT = np.concatenate([mla_outs[2 * b], mla_outs[2 * b + 1]], axis=0)
        xh.append(np.ascontiguousarray(xT_percore[c][:, p * TH:(p + 1) * TH]))
        mh.append(np.ascontiguousarray(moT[:, p * TH:(p + 1) * TH]))
    return run_addln(xh, mh, inputs["L0_attn_wo"])






IFFN_DEBUG = False


def build_iffn():
    nc = bass.Bass("TRN2", target_bir_lowering=False, num_devices=NCORE)
    TH = 2048
    rpT = nc.dram_tensor("rpT", [E, TH + 1], F32, kind="ExternalInput").ap()
    W1_d = nc.dram_tensor("W1p", [E, 5632], F32, kind="ExternalInput").ap()
    b1_d = nc.dram_tensor("b1bl", [128, 44], F32, kind="ExternalInput").ap()
    W2_d = nc.dram_tensor("W2p", [2816, E], F32, kind="ExternalInput").ap()
    b2_d = nc.dram_tensor("b2c", [128, 8], F32, kind="ExternalInput").ap()
    gc_d = nc.dram_tensor("gc", [128, 8], F32, kind="ExternalInput").ap()
    bb_d = nc.dram_tensor("bb", [128, 8], BF16, kind="ExternalInput").ap()
    grow_d = nc.dram_tensor("g_row", [32, E], BF16, kind="ExternalInput").ap()
    rnT_d = nc.dram_tensor("rnT", [E, TH], F32, kind="ExternalOutput").ap()
    if IFFN_DEBUG:
        dbg_d = nc.dram_tensor("dbg", [128, 1536], F32, kind="ExternalOutput").ap()

    TB = 512
    NTB = TH // TB

    x2b = [nc.alloc_sbuf_tensor(f"fx2{c}", [128, TH], BF16) for c in range(8)]
    xnb = x2b
    xsq = [nc.alloc_sbuf_tensor(f"fsq{c}", [128, TB], BF16) for c in range(8)]
    w1_sb = [nc.alloc_sbuf_tensor(f"fw1{c}", [128, 5632], BF16) for c in range(8)]
    w2buf = [nc.alloc_sbuf_tensor(f"fw2{i}", [128, E], BF16) for i in range(2)]
    pblk = [nc.alloc_sbuf_tensor(f"fp{j}", [128, TB], BF16) for j in range(22)]
    r1blk = [nc.alloc_sbuf_tensor(f"fr1{c}", [128, TB], F32) for c in range(8)]
    hA = nc.alloc_sbuf_tensor("fhA", [128, TB], F32)
    hG = nc.alloc_sbuf_tensor("fhG", [128, TB], F32)
    hG2 = nc.alloc_sbuf_tensor("fhG2", [128, TB], F32)
    rowbuf = nc.alloc_sbuf_tensor("frow", [1, 6 * TB], F32)
    srow = rowbuf[:, 0:TB]
    s2row = rowbuf[:, TB:2 * TB]
    vrow = rowbuf[:, 2 * TB:3 * TB]
    stdrow = rowbuf[:, 3 * TB:4 * TB]
    arow = rowbuf[:, 4 * TB:5 * TB]
    rowbuf_b = nc.alloc_sbuf_tensor("frow_b", [32, 2 * TB], BF16)
    mrow_b = rowbuf_b[:, 0:TB]
    arow_b = rowbuf_b[:, TB:2 * TB]
    gcol_sb = nc.alloc_sbuf_tensor("fgcol", [128, 8], F32)
    b2_sb = nc.alloc_sbuf_tensor("fb2", [128, 8], F32)
    bb_sb = nc.alloc_sbuf_tensor("fbb", [128, 8], BF16)
    grow_sb = nc.alloc_sbuf_tensor("fgrow", [32, E], BF16)
    b1_sb = nc.alloc_sbuf_tensor("fb1", [128, 44], F32)
    bW1_sb = nc.alloc_sbuf_tensor("fbW1", [128, 44], F32)
    btot_sb = nc.alloc_sbuf_tensor("fbtot", [128, 44], F32)
    ab_sb = nc.alloc_sbuf_tensor("fab", [128, TB], F32)
    dbg_sb = nc.alloc_sbuf_tensor("fdbg", [128, 1536], F32) if IFFN_DEBUG else None
    onescol_b = nc.alloc_sbuf_tensor("fones", [128, 1], BF16)
    onesrow_b = nc.alloc_sbuf_tensor("fonesr", [32, 128], BF16)

    B = [nc.alloc_psum_tensor(f"fB{i}", [128, 512], F32) for i in range(8)]

    S = {n: nc.alloc_semaphore("f_" + n) for n in
         ["lw", "gi", "xl", "sq", "st", "rm1", "sqd", "rw", "bcm", "xnd", "bca",
          "aok", "bwm", "bwv", "hm", "he", "gl", "pm", "w2l", "w2m", "ye", "r1l",
          "so", "w1l", "gch"]}

    with nc.Block() as block:

        @block.gpsimd
        def _(g):
            for c in range(8):
                off = 1 if c < 4 else 0
                g.dma_start(out=x2b[c][:], in_=rpT[c * 128:(c + 1) * 128, off:off + TH]
                            ).then_inc(S["xl"], 16)
            g.dma_start(out=gcol_sb[:], in_=gc_d[:]).then_inc(S["lw"], 16)
            g.dma_start(out=b2_sb[:], in_=b2_d[:]).then_inc(S["lw"], 16)
            g.dma_start(out=bb_sb[:], in_=bb_d[:]).then_inc(S["lw"], 16)
            g.dma_start(out=grow_sb[:], in_=grow_d[:]).then_inc(S["lw"], 16)
            g.dma_start(out=b1_sb[:], in_=b1_d[:]).then_inc(S["lw"], 16)
            g.memset(onescol_b[:], 1.0)
            if IFFN_DEBUG:
                g.memset(dbg_sb[:], 0.0)
            g.memset(onesrow_b[:], 0.0).then_inc(S["gch"], 1)
            g.wait_ge(S["gch"], 1)
            g.memset(onesrow_b[0:1, :], 1.0).then_inc(S["gch"], 1)
            g.wait_ge(S["gch"], 2)
            g.memset(rowbuf_b[:], 0.0).then_inc(S["gi"], 1)
            for c in range(8):
                g.dma_start(out=w1_sb[c][:], in_=W1_d[c * 128:(c + 1) * 128, :]).then_inc(S["w1l"], 16)
            for tb in range(NTB):
                if tb >= 1:
                    g.wait_ge(S["so"], 128 * tb)
                for c in range(8):
                    g.dma_start(out=r1blk[c][:],
                                in_=rpT[c * 128:(c + 1) * 128, 1 + tb * TB:1 + (tb + 1) * TB]
                                ).then_inc(S["r1l"], 16)
                for j in range(22):
                    gidx = tb * 22 + j
                    if gidx >= 2:
                        g.wait_ge(S["w2m"], gidx - 1)
                    g.dma_start(out=w2buf[gidx % 2][:],
                                in_=W2_d[j * 128:(j + 1) * 128, :]).then_inc(S["w2l"], 16)

        @block.sync
        def _(sync):
            for tb in range(NTB):
                sync.wait_ge(S["ye"], 8 * (tb + 1))
                for c in range(8):
                    sync.dma_start(out=rnT_d[c * 128:(c + 1) * 128, tb * TB:(tb + 1) * TB],
                                   in_=r1blk[c][:]).then_inc(S["so"], 16)
            if IFFN_DEBUG:
                sync.wait_ge(S["so"], 16 * 8 * NTB + 1)
                sync.dma_start(out=dbg_d, in_=dbg_sb[:]).then_inc(S["so"], 16)
                sync.wait_ge(S["so"], 16 * 8 * NTB + 17)
            else:
                sync.wait_ge(S["so"], 16 * 8 * NTB)

        @block.scalar
        def _(sc):
            sc.wait_ge(S["xl"], 16 * 8)
            for tb in range(NTB):
                if tb > 0:
                    sc.wait_ge(S["st"], tb)
                for c in range(8):
                    sc.activation(xsq[c][:], x2b[c][:, tb * TB:(tb + 1) * TB], AF.Square
                                  ).then_inc(S["sq"], 1)
                sc.wait_ge(S["rm1"], tb + 1)
                sc.activation(stdrow, s2row, AF.Sqrt).then_inc(S["sqd"], 1)
                for j in range(22):
                    gidx = tb * 22 + j
                    sc.wait_ge(S["he"], 2 * gidx + 2)
                    sc.activation(hG2[:], hG[:], AF.Gelu, bias=btot_sb[:, 2 * j + 1:2 * j + 2]
                                  ).then_inc(S["gl"], 1)

        @block.tensor
        def _(t):
            t.wait_ge(S["gi"], 1)
            t.wait_ge(S["lw"], 16 * 5)
            t.wait_ge(S["w1l"], 16 * 8)
            for j in range(44):
                for c in range(8):
                    i = t.matmul(B[4][:, j:j + 1], w1_sb[c][:, j * 128:(j + 1) * 128],
                                 bb_sb[:, c:c + 1], start=(c == 0), stop=(c == 7))
            i.then_inc(S["bwm"], 1)
            t.wait_ge(S["xl"], 16 * 8)
            hm_n = 0
            for tb in range(NTB):
                t.wait_ge(S["sq"], 8 * (tb + 1))
                if tb > 0:
                    t.wait_ge(S["rw"], 2 * tb)
                    t.wait_ge(S["ye"], 8 * tb)
                for c in range(8):
                    t.matmul(B[2][0:1, :], onescol_b[:], x2b[c][:, tb * TB:(tb + 1) * TB],
                             start=(c == 0), stop=(c == 7))
                for c in range(8):
                    i = t.matmul(B[3][0:1, :], onescol_b[:], xsq[c][:],
                                 start=(c == 0), stop=(c == 7))
                i.then_inc(S["st"], 1)
                t.wait_ge(S["rw"], 2 * tb + 2)
                if tb == 0:
                    t.wait_ge(S["bwv"], 1)
                else:
                    t.wait_ge(S["aok"], tb)
                for c in range(8):
                    t.wait_ge(S["xnd"], 8 * tb + c)
                    t.matmul(B[4][:, 0:TB], grow_sb[0:32, c * 128:(c + 1) * 128],
                             mrow_b[0:32, :], start=True, stop=True).then_inc(S["bcm"], 1)
                t.wait_ge(S["xnd"], 8 * (tb + 1))
                t.matmul(B[4][:, 0:TB], onesrow_b[:], arow_b[0:32, :], start=True, stop=True
                         ).then_inc(S["bca"], 1)
                t.wait_ge(S["aok"], tb + 1)
                for j in range(22):
                    gidx = tb * 22 + j
                    if gidx >= 1:
                        t.wait_ge(S["he"], 2 * gidx)
                    for c in range(8):
                        t.matmul(B[0][:], w1_sb[c][:, (2 * j) * 128:(2 * j + 1) * 128],
                                 xnb[c][:, tb * TB:(tb + 1) * TB], start=(c == 0), stop=(c == 7))
                    for c in range(8):
                        i = t.matmul(B[1][:], w1_sb[c][:, (2 * j + 1) * 128:(2 * j + 2) * 128],
                                     xnb[c][:, tb * TB:(tb + 1) * TB], start=(c == 0), stop=(c == 7))
                    i.then_inc(S["hm"], 1)
                t.wait_ge(S["pm"], 22 * (tb + 1))
                if tb > 0:
                    t.wait_ge(S["ye"], 8 * tb)
                for j in range(22):
                    gidx = tb * 22 + j
                    t.wait_ge(S["w2l"], 16 * (gidx + 1))
                    for e in range(8):
                        i = t.matmul(B[e][:], w2buf[gidx % 2][:, e * 128:(e + 1) * 128],
                                     pblk[j][:], start=(j == 0), stop=(j == 21))
                    i.then_inc(S["w2m"], 1)

        @block.vector
        def _(v):
            v.wait_ge(S["bwm"], 1)
            v.tensor_copy(bW1_sb[:], B[4][:, 0:44])
            v.wait_ge(S["lw"], 16 * 5)
            v.tensor_add(btot_sb[:], bW1_sb[:], b1_sb[:]).then_inc(S["bwv"], 1)
            for tb in range(NTB):
                v.wait_ge(S["st"], tb + 1)
                v.tensor_scalar_mul(srow, B[2][0:1, :], 1.0 / 1024)
                v.tensor_scalar_mul(s2row, B[3][0:1, :], 1.0 / 1024).then_inc(S["rw"], 1)
                v.wait_ge(S["rw"], 2 * tb + 1)
                v.tensor_mul(vrow, srow, srow)
                v.tensor_sub(s2row, s2row, vrow)
                v.tensor_scalar_add(s2row, s2row, EPS).then_inc(S["rm1"], 1)
                v.wait_ge(S["sqd"], tb + 1)
                v.reciprocal(arow, stdrow)
                v.tensor_copy(mrow_b[0:1, :], srow)
                v.tensor_copy(arow_b[0:1, :], arow).then_inc(S["rw"], 1)
                for c in range(8):
                    v.wait_ge(S["bcm"], 8 * tb + c + 1)
                    if IFFN_DEBUG and tb == 3 and c == 0:
                        v.tensor_copy(dbg_sb[:, 0:512], B[4][:, 0:TB])
                        v.tensor_copy(dbg_sb[0:1, 512:1024], mrow_b[0:1, :])
                        v.tensor_copy(dbg_sb[0:1, 1024:1536], srow).then_inc(S["so"], 1)
                    v.scalar_tensor_tensor(xnb[c][:, tb * TB:(tb + 1) * TB],
                                           x2b[c][:, tb * TB:(tb + 1) * TB],
                                           gcol_sb[:, c:c + 1], B[4][:, 0:TB],
                                           ALU.mult, ALU.subtract).then_inc(S["xnd"], 1)
                v.wait_ge(S["bca"], tb + 1)
                v.tensor_copy(ab_sb[:], B[4][:, 0:TB]).then_inc(S["aok"], 1)
                if tb > 0:
                    v.wait_ge(S["w2m"], 22 * tb)
                for j in range(22):
                    gidx = tb * 22 + j
                    v.wait_ge(S["hm"], gidx + 1)
                    v.tensor_mul(hA[:], B[0][:], ab_sb[:]).then_inc(S["he"], 1)
                    v.tensor_mul(hG[:], B[1][:], ab_sb[:]).then_inc(S["he"], 1)
                    v.wait_ge(S["gl"], gidx + 1)
                    v.scalar_tensor_tensor(pblk[j][:], hA[:], btot_sb[:, 2 * j:2 * j + 1],
                                           hG2[:], ALU.add, ALU.mult).then_inc(S["pm"], 1)
                v.wait_ge(S["w2m"], 22 * (tb + 1))
                v.wait_ge(S["r1l"], 16 * 8 * (tb + 1))
                for e in range(8):
                    v.scalar_tensor_tensor(r1blk[e][:], B[e][:], b2_sb[:, e:e + 1],
                                           r1blk[e][:], ALU.add, ALU.add).then_inc(S["ye"], 1)

    print("IFFN sbuf remaining:", nc.sbuf_bytes_remaining)
    return nc


def _iffn_host(inputs, layer):
    key = ("iffn_host", layer)
    if key in _CACHE:
        return _CACHE[key]
    W1 = np.asarray(inputs["L0_ffn_W1"][layer])
    b1 = np.asarray(inputs["L0_ffn_b1"][layer])
    W2 = np.asarray(inputs["L0_ffn_W2"][layer])
    b2 = np.asarray(inputs["L0_ffn_b2"][layer])
    g = np.asarray(inputs["L0_ffn_ln_g"][layer])
    bl = np.asarray(inputs["L0_ffn_ln_b"][layer])
    DI = 2730
    W1p = np.zeros((E, 5632), np.float32)
    b1bl = np.zeros((128, 44), np.float32)
    for j in range(22):
        lo, hi = j * 128, min((j + 1) * 128, DI)
        n = hi - lo
        if n > 0:
            W1p[:, (2 * j) * 128:(2 * j) * 128 + n] = W1[:, lo:hi]
            W1p[:, (2 * j + 1) * 128:(2 * j + 1) * 128 + n] = W1[:, DI + lo:DI + hi]
            b1bl[0:n, 2 * j] = b1[lo:hi]
            b1bl[0:n, 2 * j + 1] = b1[DI + lo:DI + hi]
    W2p = np.zeros((2816, E), np.float32)
    W2p[0:DI, :] = W2
    r = {
        "W1p": W1p, "b1bl": b1bl, "W2p": W2p,
        "b2c": np.ascontiguousarray(b2.reshape(8, 128).T).astype(np.float32),
        "gc": np.ascontiguousarray(g.reshape(8, 128).T).astype(np.float32),
        "bb": np.ascontiguousarray(bl.reshape(8, 128).T).astype(BF),
        "g_row": np.ascontiguousarray(
            np.concatenate([g[None, :], np.zeros((31, E))], 0)).astype(BF),
    }
    _CACHE[key] = r
    return r


def run_iffn(inputs, layer, rp_full):
    nc = _get("iffn", build_iffn)
    host = _iffn_host(inputs, layer)
    TH = 2048
    in_maps = []
    for c in range(NCORE):
        b, p = c // 2, c % 2
        rpT = np.zeros((E, TH + 1), np.float32)
        rpT[:, 1:] = rp_full[b][:, p * TH:(p + 1) * TH]
        if p == 1:
            rpT[:, 0] = rp_full[b][:, TH - 1]
        m = {"rpT": rpT}
        m.update(host)
        in_maps.append(m)
    res = run_bass_kernel_spmd(nc, in_maps, core_ids=list(range(NCORE)))
    out = np.zeros((4, E, T), np.float32)
    for c in range(NCORE):
        b, p = c // 2, c % 2
        out[b][:, p * TH:(p + 1) * TH] = np.asarray(res.results[c]["rnT"])
    return out




RGA = 2304
NCH = RGA // 128
TBR = 384
NBR = RGA // TBR


IATTN_DEBUG = False


def build_iattn():
    nc = bass.Bass("TRN2", target_bir_lowering=False, num_devices=NCORE)
    rT = nc.dram_tensor("rT", [E, RGA + 1], F32, kind="ExternalInput").ap()
    Wqkv_d = nc.dram_tensor("Wqkv", [E, 3072], F32, kind="ExternalInput").ap()
    CS_d = [nc.dram_tensor(f"CS{i}", [128, RGA], BF16, kind="ExternalInput").ap() for i in range(4)]
    qs_d = nc.dram_tensor("qs_c", [128, 1], F32, kind="ExternalInput").ap()
    ks_d = nc.dram_tensor("ks_c", [128, 1], F32, kind="ExternalInput").ap()
    swp_d = nc.dram_tensor("swp2", [128, 128], BF16, kind="ExternalInput").ap()
    msk_d = nc.dram_tensor("msk", [4 * 128, 128], BF16, kind="ExternalInput").ap()
    blk2r_d = nc.dram_tensor("blk2r", [32, 128], BF16, kind="ExternalInput").ap()
    Wo_d = nc.dram_tensor("Wo", [E, E], F32, kind="ExternalInput").ap()
    rnT_d = nc.dram_tensor("rnT", [E, 2048], F32, kind="ExternalOutput").ap()
    if IATTN_DEBUG:
        dbgo_d = nc.dram_tensor("dbgo", [E, 2048], BF16, kind="ExternalOutput").ap()

    wkv_buf = [nc.alloc_sbuf_tensor(f"awkv{i}", [128, 8 * 512], BF16) for i in range(2)]
    qf = [nc.alloc_sbuf_tensor(f"aqf{i}", [128, RGA], BF16) for i in range(8)]
    kf = [nc.alloc_sbuf_tensor(f"akf{i}", [128, RGA], BF16) for i in range(8)]
    vp = [nc.alloc_sbuf_tensor(f"avp{h}", [128, NCH * 65], BF16) for h in range(16)]
    sqs = nc.alloc_sbuf_tensor("asqs", [128, RGA], BF16)
    rowq = nc.alloc_sbuf_tensor("arowq", [2, TBR], F32)
    stdq = nc.alloc_sbuf_tensor("astdq", [2, TBR], F32)
    rstdb = nc.alloc_sbuf_tensor("arstd", [32, TBR], BF16)
    qs_sb = nc.alloc_sbuf_tensor("aqs", [128, 1], F32)
    ks_sb = nc.alloc_sbuf_tensor("aks", [128, 1], F32)
    swp_sb = nc.alloc_sbuf_tensor("aswp", [128, 128], BF16)
    ident = nc.alloc_sbuf_tensor("aident", [128, 128], BF16)
    blk2 = nc.alloc_sbuf_tensor("ablk2", [128, 2], BF16)
    blk2r = nc.alloc_sbuf_tensor("ablk2r", [32, 128], BF16)
    msk_sb = [nc.alloc_sbuf_tensor(f"amsk{i}", [128, 128], BF16) for i in range(4)]
    rt0 = nc.alloc_sbuf_tensor("art0", [128, TBR], BF16)
    rt1 = nc.alloc_sbuf_tensor("art1", [128, TBR], BF16)
    mcol = nc.alloc_sbuf_tensor("amcol", [128, 1], F32)
    negm8 = nc.alloc_sbuf_tensor("anegm", [128, 1], F32)
    invsb = nc.alloc_sbuf_tensor("ainvsb", [96, 128], F32)
    onesf = nc.alloc_sbuf_tensor("aonesf", [96, 64], F32)
    invb = nc.alloc_sbuf_tensor("ainvb", [64, 128], F32)
    Pe = nc.alloc_sbuf_tensor("aPe", [128, 384], BF16)
    PT = nc.alloc_sbuf_tensor("aPT", [128, 384], BF16)
    rblk = [nc.alloc_sbuf_tensor(f"arb{c}", [128, 512], F32) for c in range(8)]

    rsh = [nc.alloc_sbuf_tensor(f"arsh{c}", [128, RGA], BF16) for c in range(8)]
    rsh_addr = nc.lookup_mloc(rsh[0]).addr
    t_cs = [nc.alloc_sbuf_tensor_at(f"acs{i}", [128, RGA], BF16, offset=rsh_addr + 16384 + i * 4608)
            for i in range(4)]
    wo_sb = [nc.alloc_sbuf_tensor_at(f"awo{c}", [128, E], BF16, offset=rsh_addr + c * 2048)
             for c in range(8)]
    wkv_addr = nc.lookup_mloc(wkv_buf[0]).addr
    oT = [nc.alloc_sbuf_tensor_at(f"aoT{i}", [128, 2048], BF16, offset=wkv_addr + i * 4096)
          for i in range(4)]
    oT += [nc.alloc_sbuf_tensor(f"aoU{i}", [128, 2048], BF16) for i in range(4)]

    ps_a = nc.alloc_psum_tensor("aps_a", [128, 512], F32)
    ps_b = nc.alloc_psum_tensor("aps_b", [128, 512], F32)
    ps_s0 = nc.alloc_psum_tensor("aps_s0", [128, 384], F32)
    ps_s1 = nc.alloc_psum_tensor("aps_s1", [128, 384], F32)
    ps_tr = nc.alloc_psum_tensor("aps_tr", [128, 384], BF16)
    ps_oT = nc.alloc_psum_tensor("aps_oT", [65, 128], F32)
    ps_nx = nc.alloc_psum_tensor("aps_nx", [64, TBR], F32)
    ps_n = ps_nx[0:2, 0:TBR]
    ps_n2 = ps_nx[0:64, 0:128]
    ps_bc = nc.alloc_psum_tensor("aps_bc", [128, TBR], F32)

    S = {n: nc.alloc_semaphore("a_" + n) for n in
         ["lw", "gi", "rl", "qm", "qe", "sq", "nm", "nre", "nv1", "nsq", "ntl",
          "nb", "qn", "sw", "swe", "rp", "sS", "mx", "mr", "ex", "iv", "pn", "tr",
          "te", "pv", "oe", "wom", "woe", "rbl", "rpt", "wol", "so", "gch", "vch",
          "bcv"]}
    gch = [0]
    vch = [0]

    NQK = 4 * 4 * NBR
    NV = NCH * 2
    NTL = 16
    NSUB = 16 * 8 * 2

    def q_tile_done_qe(qk, i8):
        return qk * 48 + (i8 // 4) * 24 + (i8 % 4) * NBR + NBR

    with nc.Block() as block:

        @block.gpsimd
        def _(g):
            for c in range(8):
                off = 1 if c < 4 else 0
                g.dma_start(out=rsh[c][:], in_=rT[c * 128:(c + 1) * 128, off:off + RGA]
                            ).then_inc(S["rl"], 16)
            g.dma_start(out=qs_sb[:], in_=qs_d[:]).then_inc(S["lw"], 16)
            g.dma_start(out=ks_sb[:], in_=ks_d[:]).then_inc(S["lw"], 16)
            g.dma_start(out=swp_sb[:], in_=swp_d[:]).then_inc(S["lw"], 16)
            for i in range(4):
                g.dma_start(out=msk_sb[i][:], in_=msk_d[i * 128:(i + 1) * 128, :]).then_inc(S["lw"], 16)
            g.memset(ident[:], 0.0).then_inc(S["gch"], 1)
            gch[0] += 1
            g.wait_ge(S["gch"], gch[0])
            g.affine_select(out=ident[:], in_=ident[:], compare_op=ALU.not_equal,
                            fill=1.0, base=0, pattern=[[-1, 128]], channel_multiplier=1
                            ).then_inc(S["gi"], 1)
            g.memset(blk2[:], 0.0).then_inc(S["gch"], 1)
            gch[0] += 1
            g.wait_ge(S["gch"], gch[0])
            g.memset(blk2[0:64, 0:1], 1.0).then_inc(S["gch"], 1)
            gch[0] += 1
            g.wait_ge(S["gch"], gch[0])
            g.memset(blk2[64:128, 1:2], 1.0).then_inc(S["gi"], 1)
            g.dma_start(out=blk2r[:], in_=blk2r_d[:]).then_inc(S["lw"], 16)
            g.memset(rowq[:], 0.0).then_inc(S["gi"], 1)
            g.memset(rstdb[:], 0.0).then_inc(S["gi"], 1)
            g.memset(invsb[64:96, :], 0.0).then_inc(S["gi"], 1)
            g.memset(onesf[64:96, :], 0.0).then_inc(S["gch"], 1)
            gch[0] += 1
            g.wait_ge(S["gch"], gch[0])
            g.memset(onesf[64:65, :], 1.0).then_inc(S["gi"], 1)
            for h in range(16):
                g.memset(vp[h][:].rearrange("p (a b) -> p a b", b=65)[:, :, 64:65], 1.0
                         ).then_inc(S["gi"], 1)
            for grp in range(6):
                if grp == 2:
                    g.wait_ge(S["qm"], 24)
                elif grp == 3:
                    g.wait_ge(S["qm"], 48)
                elif grp == 4:
                    g.wait_ge(S["qm"], 72)
                elif grp == 5:
                    g.wait_ge(S["qm"], 96)
                for c in range(8):
                    g.dma_start(out=wkv_buf[grp % 2][:, c * 512:(c + 1) * 512],
                                in_=Wqkv_d[c * 128:(c + 1) * 128, grp * 512:(grp + 1) * 512]
                                ).then_inc(S["lw"], 16)
            g.wait_ge(S["qm"], NQK + NV)
            for i in range(4):
                g.dma_start(out=t_cs[i][:], in_=CS_d[i][:]).then_inc(S["rpt"], 16)
            for c in range(8):
                g.dma_start(out=wo_sb[c][:], in_=Wo_d[c * 128:(c + 1) * 128, :]).then_inc(S["wol"], 16)
            for tb in range(4):
                if tb >= 1:
                    g.wait_ge(S["so"], 128 * tb)
                for c in range(8):
                    g.dma_start(out=rblk[c][:],
                                in_=rT[c * 128:(c + 1) * 128, 257 + tb * 512: 257 + (tb + 1) * 512]
                                ).then_inc(S["rbl"], 16)

        @block.sync
        def _(sync):
            if IATTN_DEBUG:
                sync.wait_ge(S["oe"], NSUB)
                for c in range(8):
                    sync.dma_start(out=dbgo_d[c * 128:(c + 1) * 128, :], in_=oT[c][:]
                                   ).then_inc(S["so"], 16)
            for tb in range(4):
                sync.wait_ge(S["wom"], 8 * (tb + 1))
                for c in range(8):
                    sync.dma_start(out=rnT_d[c * 128:(c + 1) * 128, tb * 512:(tb + 1) * 512],
                                   in_=rblk[c][:]).then_inc(S["so"], 16)
            sync.wait_ge(S["so"], 16 * 32 + (16 * 8 if IATTN_DEBUG else 0))

        @block.scalar
        def _(sc):
            for qk in range(2):
                src_t = qf if qk == 0 else kf
                for i8 in range(8):
                    idx = qk * 8 + i8
                    sc.wait_ge(S["qe"], q_tile_done_qe(qk, i8))
                    if idx > 0:
                        sc.wait_ge(S["nm"], idx * NBR)
                    sc.activation(sqs[:], src_t[i8][:], AF.Square).then_inc(S["sq"], 1)
                    for blk in range(NBR):
                        sc.wait_ge(S["nv1"], idx * NBR + blk + 1)
                        sc.activation(stdq[:], rowq[:], AF.Sqrt).then_inc(S["nsq"], 1)
            for si in range(NSUB):
                sc.wait_ge(S["mx"], si + 1)
                bank = ps_s0 if si % 2 == 0 else ps_s1
                sc.activation(Pe[:], bank[:], AF.Exp, bias=negm8[:], scale=8.0
                              ).then_inc(S["ex"], 1)

        @block.tensor
        def _(t):
            t.wait_ge(S["lw"], 16 * 16)
            t.wait_ge(S["gi"], 22)
            t.wait_ge(S["rl"], 16 * 8)
            qm_n = 0
            for grp in range(4):
                t.wait_ge(S["lw"], 16 * (8 + 8 * (grp + 1)))
                for e in range(4):
                    for tbb in range(NBR):
                        bank = ps_a if qm_n % 2 == 0 else ps_b
                        if qm_n >= 2:
                            t.wait_ge(S["qe"], qm_n - 1)
                        wb = wkv_buf[grp % 2]
                        for c in range(8):
                            i = t.matmul(bank[:, 0:TBR],
                                         wb[:, c * 512 + e * 128: c * 512 + (e + 1) * 128],
                                         rsh[c][:, tbb * TBR:(tbb + 1) * TBR],
                                         start=(c == 0), stop=(c == 7))
                        i.then_inc(S["qm"], 1)
                        qm_n += 1
            for tc in range(NCH):
                for vh in range(2):
                    t.wait_ge(S["lw"], 16 * (8 + 8 * (5 + vh)))
                    bank = ps_a if qm_n % 2 == 0 else ps_b
                    t.wait_ge(S["qe"], qm_n - 1)
                    wb = wkv_buf[vh]
                    for c in range(8):
                        i = t.matmul(bank[:], rsh[c][:, tc * 128:(tc + 1) * 128],
                                     wb[:, c * 512:(c + 1) * 512],
                                     start=(c == 0), stop=(c == 7))
                    i.then_inc(S["qm"], 1)
                    qm_n += 1
            sw_n = 0
            for qk in range(2):
                src_t = qf if qk == 0 else kf
                qsc = qs_sb if qk == 0 else ks_sb
                tb1 = t_cs[0] if qk == 0 else t_cs[2]
                tb2 = t_cs[1] if qk == 0 else t_cs[3]
                for i8 in range(8):
                    idx = qk * 8 + i8
                    t.wait_ge(S["sq"], idx + 1)
                    for blk in range(NBR):
                        bidx = idx * NBR + blk
                        if bidx >= 1:
                            t.wait_ge(S["nre"], bidx)
                        t.matmul(ps_n, blk2[:], sqs[:, blk * TBR:(blk + 1) * TBR],
                                 start=True, stop=True).then_inc(S["nm"], 1)
                        t.wait_ge(S["ntl"], bidx + 1)
                        if bidx >= 1:
                            t.wait_ge(S["qn"], bidx)
                        t.matmul(ps_bc[:], blk2r[:], rstdb[:],
                                 start=True, stop=True).then_inc(S["nb"], 1)
                        t.wait_ge(S["qn"], bidx + 1)
                        bank = ps_a if sw_n % 2 == 0 else ps_b
                        if sw_n >= 2:
                            t.wait_ge(S["swe"], sw_n - 1)
                        t.matmul(bank[:, 0:TBR], swp_sb[:],
                                 src_t[i8][:, blk * TBR:(blk + 1) * TBR],
                                 start=True, stop=True).then_inc(S["sw"], 1)
                        sw_n += 1
            t.wait_ge(S["rp"], NQK)
            t.wait_ge(S["qe"], NQK + NV)
            si = 0
            for h in range(16):
                i8, ro = h // 2, (h % 2) * 64
                for wj in range(1, 9):
                    for s in range(2):
                        ks = (wj - 1) * 256 + s * 128
                        qo = wj * 256 + s * 128
                        bank = ps_s0 if si % 2 == 0 else ps_s1
                        if si >= 2:
                            t.wait_ge(S["ex"], si - 1)
                        t.matmul(bank[:], qf[i8][ro:ro + 64, qo:qo + 128],
                                 kf[i8][ro:ro + 64, ks:ks + 384], start=True, stop=False)
                        m0 = msk_sb[2] if wj == 1 else msk_sb[0]
                        t.matmul(bank[:, 0:128], ident[:], m0[:], start=False, stop=False)
                        if wj == 1 and s == 0:
                            t.matmul(bank[:, 128:256], ident[:], msk_sb[3][:],
                                     start=False, stop=False)
                        i = t.matmul(bank[:, 256:384], ident[:], msk_sb[1][:],
                                     start=False, stop=True)
                        i.then_inc(S["sS"], 1)
                        t.wait_ge(S["ex"], si + 1)
                        if si >= 1:
                            t.wait_ge(S["te"], 3 * si)
                        for ck in range(3):
                            i = t.transpose(ps_tr[:, ck * 128:(ck + 1) * 128],
                                            Pe[:, ck * 128:(ck + 1) * 128], ident[:])
                        i.then_inc(S["tr"], 1)
                        t.wait_ge(S["te"], 3 * si + 3)
                        if si >= 1:
                            t.wait_ge(S["oe"], si)
                        for ck in range(3):
                            kch = ks // 128 + ck
                            i = t.matmul(ps_oT[0:65, :], vp[h][:, kch * 65:(kch + 1) * 65],
                                         PT[:, ck * 128:(ck + 1) * 128],
                                         start=(ck == 0), stop=(ck == 2))
                        i.then_inc(S["pv"], 1)
                        t.wait_ge(S["iv"], si + 1)
                        if si >= 1:
                            t.wait_ge(S["pn"], si)
                        t.matmul(ps_n2,
                                 onesf[64:96, :], invsb[64:96, :],
                                 start=True, stop=True).then_inc(S["bcv"], 1)
                        si += 1
            t.wait_ge(S["wol"], 16 * 8)
            t.wait_ge(S["oe"], NSUB)
            wo_n = 0
            for tb in range(4):
                for e in range(8):
                    bank = ps_a if wo_n % 2 == 0 else ps_b
                    if wo_n >= 2:
                        t.wait_ge(S["woe"], wo_n - 1)
                    for c in range(8):
                        i = t.matmul(bank[:], wo_sb[c][:, e * 128:(e + 1) * 128],
                                     oT[c][:, tb * 512:(tb + 1) * 512],
                                     start=(c == 0), stop=(c == 7))
                    i.then_inc(S["wom"], 1)
                    wo_n += 1

        @block.vector
        def _(v):
            qm_n = 0
            for grp in range(4):
                for e in range(4):
                    for tbb in range(NBR):
                        v.wait_ge(S["qm"], qm_n + 1)
                        bank = ps_a if qm_n % 2 == 0 else ps_b
                        dst = qf if grp < 2 else kf
                        ti = (grp % 2) * 4 + e
                        v.tensor_copy(dst[ti][:, tbb * TBR:(tbb + 1) * TBR], bank[:, 0:TBR]
                                      ).then_inc(S["qe"], 1)
                        qm_n += 1
            for tc in range(NCH):
                for vh in range(2):
                    v.wait_ge(S["qm"], qm_n + 1)
                    bank = ps_a if qm_n % 2 == 0 else ps_b
                    for hh in range(8):
                        i = v.tensor_copy(vp[vh * 8 + hh][:, tc * 65: tc * 65 + 64],
                                          bank[:, hh * 64:(hh + 1) * 64])
                    i.then_inc(S["qe"], 1)
                    qm_n += 1
            v.wait_ge(S["rpt"], 64)
            sw_n = 0
            for qk in range(2):
                src_t = qf if qk == 0 else kf
                qsc = qs_sb if qk == 0 else ks_sb
                tb1 = t_cs[0] if qk == 0 else t_cs[2]
                tb2 = t_cs[1] if qk == 0 else t_cs[3]
                for i8 in range(8):
                    idx = qk * 8 + i8
                    for blk in range(NBR):
                        bidx = idx * NBR + blk
                        sl = slice(blk * TBR, (blk + 1) * TBR)
                        v.wait_ge(S["nm"], bidx + 1)
                        v.tensor_scalar(rowq[:], ps_n, 1e-24, None, ALU.max
                                        ).then_inc(S["nre"], 1)
                        v.wait_ge(S["nre"], bidx + 1)
                        v.engine_nop().then_inc(S["nv1"], 1)
                        v.wait_ge(S["nsq"], bidx + 1)
                        v.reciprocal(rowq[:], stdq[:]).then_inc(S["vch"], 1)
                        vch[0] += 1
                        v.wait_ge(S["vch"], vch[0])
                        v.tensor_copy(rstdb[0:2, :], rowq[:]).then_inc(S["ntl"], 1)
                        v.wait_ge(S["nb"], bidx + 1)
                        v.scalar_tensor_tensor(src_t[i8][:, sl], src_t[i8][:, sl],
                                               qsc[:], ps_bc[:],
                                               ALU.mult, ALU.mult).then_inc(S["qn"], 1)
                        v.wait_ge(S["sw"], sw_n + 1)
                        bank = ps_a if sw_n % 2 == 0 else ps_b
                        v.tensor_copy(rt0[:], bank[:, 0:TBR]).then_inc(S["swe"], 1)
                        v.tensor_mul(rt1[:], src_t[i8][:, sl], tb1[:, sl])
                        v.wait_ge(S["swe"], sw_n + 1)
                        v.tensor_mul(rt0[:], rt0[:], tb2[:, sl]).then_inc(S["vch"], 1)
                        vch[0] += 1
                        v.wait_ge(S["vch"], vch[0])
                        v.tensor_add(src_t[i8][:, sl], rt0[:], rt1[:]).then_inc(S["rp"], 1)
                        sw_n += 1
            for si in range(NSUB):
                v.wait_ge(S["sS"], si + 1)
                bank = ps_s0 if si % 2 == 0 else ps_s1
                v.reduce_max(mcol[:], bank[:], axis=AX.X).then_inc(S["mr"], 1)
                v.wait_ge(S["mr"], si + 1)
                v.tensor_scalar_mul(negm8[:], mcol[:], -8.0).then_inc(S["mx"], 1)
                v.wait_ge(S["tr"], si + 1)
                for ck in range(3):
                    i = v.tensor_copy(PT[:, ck * 128:(ck + 1) * 128],
                                      ps_tr[:, ck * 128:(ck + 1) * 128])
                    i.then_inc(S["te"], 1)
                v.wait_ge(S["pv"], si + 1)
                v.reciprocal(invsb[64:65, :], ps_oT[64:65, :]).then_inc(S["iv"], 1)
                v.wait_ge(S["bcv"], si + 1)
                v.tensor_copy(invb[:], ps_n2).then_inc(S["pn"], 1)
                v.wait_ge(S["pn"], si + 1)
                h = si // 16
                wj = 1 + (si % 16) // 2
                s = si % 2
                i8, ro = h // 2, (h % 2) * 64
                v.tensor_mul(oT[i8][ro:ro + 64,
                                    (wj - 1) * 256 + s * 128: (wj - 1) * 256 + s * 128 + 128],
                             ps_oT[0:64, :], invb[:]).then_inc(S["oe"], 1)
            wo_n = 0
            for tb in range(4):
                v.wait_ge(S["rbl"], 128 * (tb + 1))
                for e in range(8):
                    v.wait_ge(S["wom"], wo_n + 1)
                    bank = ps_a if wo_n % 2 == 0 else ps_b
                    v.tensor_add(rblk[e][:], bank[:], rblk[e][:]).then_inc(S["woe"], 1)
                    wo_n += 1

    print("IATTN sbuf remaining:", nc.sbuf_bytes_remaining)
    return nc


def _iattn_host(inputs, layer):
    key = ("iattn_host", layer)
    if key in _CACHE:
        return _CACHE[key]
    Wqkv = np.ascontiguousarray(np.asarray(inputs["L0_ffn_Wqkv"][layer], np.float32))
    qs = np.asarray(inputs["L0_ffn_q_scale"][layer], np.float32)
    ks = np.asarray(inputs["L0_ffn_k_scale"][layer], np.float32)
    Wo = np.ascontiguousarray(np.asarray(inputs["L0_ffn_Wo"][layer], np.float32))
    r = {
        "Wqkv": Wqkv, "Wo": Wo,
        "qs_c": np.ascontiguousarray(np.tile(qs, 2)[:, None]).astype(np.float32),
        "ks_c": np.ascontiguousarray(np.tile(ks, 2)[:, None]).astype(np.float32),
    }
    _CACHE[key] = r
    return r


def _iattn_const():
    if "iattn_const" in _CACHE:
        return _CACHE["iattn_const"]
    swp = np.zeros((128, 128), np.float32)
    for b0 in range(0, 128, 64):
        for d in range(32):
            swp[b0 + d + 32, b0 + d] = 1.0
            swp[b0 + d, b0 + d + 32] = 1.0
    swp = swp.astype(BF)
    NEG = -1e9
    ir = np.arange(128)[:, None]
    jr = np.arange(128)[None, :]
    m_lo = np.where(jr >= ir, 0.0, NEG).astype(np.float32)
    m_hi = np.where(jr <= ir, 0.0, NEG).astype(np.float32)
    full = np.full((128, 128), NEG, np.float32)
    zero = np.zeros((128, 128), np.float32)
    mask_even = np.concatenate([m_lo, m_hi, full, full], 0).astype(BF)
    mask_odd = np.concatenate([m_lo, m_hi, m_lo, zero], 0).astype(BF)
    tabs = {}
    for p in range(2):
        kvstart = p * 2048 - 256
        t = np.arange(kvstart, kvstart + RGA, dtype=np.float64)
        dd = np.arange(64)
        inv_freq = 1.0 / 10000.0 ** (np.arange(0, 64, 2, dtype=np.float64) / 64.0)
        f = inv_freq[dd % 32]
        base = (np.arange(0, 64, 2, dtype=np.float64) + 0.4 * 64) / (1.4 * 64)
        bb = base[dd % 32]
        ang = f[:, None] * t[None, :]
        power = (t - 2048.0) / 128.0
        scale = bb[:, None] ** power[None, :]
        sig = np.where(dd < 32, -1.0, 1.0)[:, None]
        cs1q = np.cos(ang) * scale
        cs2q = sig * np.sin(ang) * scale
        cs1k = np.cos(ang) / scale
        cs2k = sig * np.sin(ang) / scale
        tabs[p] = [np.concatenate([x, x], 0).astype(BF) for x in (cs1q, cs2q, cs1k, cs2k)]
    r = {"swp2": swp, "mask_even": mask_even, "mask_odd": mask_odd, "tabs": tabs}
    _CACHE["iattn_const"] = r
    return r


def run_iattn(inputs, layer, r_full):
    nc = _get("iattn", build_iattn)
    host = _iattn_host(inputs, layer)
    consts = _iattn_const()
    in_maps = []
    for c in range(NCORE):
        b, p = c // 2, c % 2
        rTc = np.zeros((E, RGA + 1), np.float32)
        if p == 0:
            rTc[:, 257:] = r_full[b][:, 0:2048]
        else:
            rTc[:, 1:] = r_full[b][:, 1792:4096]
            rTc[:, 0] = r_full[b][:, 1791]
        b2r = np.zeros((32, 128), np.float32)
        b2r[0, 0:64] = 1.0
        b2r[1, 64:128] = 1.0
        m = {"rT": rTc, "swp2": consts["swp2"], "blk2r": b2r.astype(BF),
             "msk": consts["mask_even"] if p == 0 else consts["mask_odd"]}
        for i in range(4):
            m[f"CS{i}"] = consts["tabs"][p][i]
        m.update(host)
        in_maps.append(m)
    res = run_bass_kernel_spmd(nc, in_maps, core_ids=list(range(NCORE)))
    out = np.zeros((4, E, T), np.float32)
    for c in range(NCORE):
        b, p = c // 2, c % 2
        out[b][:, p * 2048:(p + 1) * 2048] = np.asarray(res.results[c]["rnT"])
    if IATTN_DEBUG:
        run_iattn.dbg = [np.asarray(res.results[c]["dbgo"]) for c in range(NCORE)]
    return out



NSA_DEBUG = False


def build_nsa():
    nc = bass.Bass("TRN2", target_bir_lowering=False, num_devices=NCORE)
    xnT = nc.dram_tensor("xnT", [E, T], F32, kind="ExternalInput").ap()
    Wq_d = nc.dram_tensor("Wq", [E, 512], F32, kind="ExternalInput").ap()
    Wk_d = nc.dram_tensor("Wk", [E, 512], F32, kind="ExternalInput").ap()
    Wv_d = nc.dram_tensor("Wv", [E, 512], F32, kind="ExternalInput").ap()
    Wg_d = nc.dram_tensor("Wgr", [128, 24], F32, kind="ExternalInput").ap()
    bq_d = nc.dram_tensor("bq", [128, 4], F32, kind="ExternalInput").ap()
    bk_d = nc.dram_tensor("bk", [128, 4], F32, kind="ExternalInput").ap()
    bv_d = nc.dram_tensor("bvp", [32, 512], BF16, kind="ExternalInput").ap()
    bg_d = nc.dram_tensor("bg", [3, 1], F32, kind="ExternalInput").ap()
    smask_d = nc.dram_tensor("smask", [3 * 128, 128], BF16, kind="ExternalInput").ap()
    selm_d = nc.dram_tensor("selm", [32, 192], BF16, kind="ExternalInput").ap()
    nsaT_d = nc.dram_tensor("nsaT", [512, T], BF16, kind="ExternalOutput").ap()
    selk_dram = nc.dram_tensor("selk_scratch", [32, 32], F32, kind="Internal")
    if NSA_DEBUG:
        ndbg_d = nc.dram_tensor("ndbg", [16, 128], F32, kind="ExternalOutput").ap()
        gbdbg_d = nc.dram_tensor("gbdbg", [192, T], BF16, kind="ExternalOutput").ap()
        cmdbg_d = nc.dram_tensor("cmdbg", [512, T], BF16, kind="ExternalOutput").ap()
        pdbg_d = nc.dram_tensor("pdbg", [4 * 128, T], BF16, kind="ExternalOutput").ap()

    NTC = T // 128
    PR = 8

    xnb = [nc.alloc_sbuf_tensor(f"nxn{c}", [128, T], BF16) for c in range(8)]
    qT = [nc.alloc_sbuf_tensor(f"nqT{i}", [128, T], BF16) for i in range(4)]
    kT = [nc.alloc_sbuf_tensor(f"nkT{i}", [128, T], BF16) for i in range(4)]
    vp = [nc.alloc_sbuf_tensor(f"nvp{h}", [128, NTC * 65], BF16) for h in range(8)]
    w_sb = [nc.alloc_sbuf_tensor(f"nwb{c}", [128, 512], BF16) for c in range(8)]
    wg_sb = nc.alloc_sbuf_tensor("nwg", [128, 24], BF16)
    xmean = [nc.alloc_sbuf_tensor(f"nxm{c}", [128, 128], BF16) for c in range(8)]
    xmf = nc.alloc_sbuf_tensor("nxmf", [128, 128], F32)
    ckT = [nc.alloc_sbuf_tensor(f"nck{i}", [128, 128], BF16) for i in range(4)]
    cvp = [nc.alloc_sbuf_tensor(f"ncv{h}", [128, 65], BF16) for h in range(8)]
    gates = nc.alloc_sbuf_tensor("ngate", [32, T], BF16)
    selm_sb = nc.alloc_sbuf_tensor("nselm", [32, 192], BF16)
    bq_sb = nc.alloc_sbuf_tensor("nbq", [128, 4], F32)
    bk_sb = nc.alloc_sbuf_tensor("nbk", [128, 4], F32)
    bv_sb = nc.alloc_sbuf_tensor("nbv", [32, 512], BF16)
    bg_sb = nc.alloc_sbuf_tensor("nbg", [3, 1], F32)
    onesrow_b = nc.alloc_sbuf_tensor("nones", [32, 128], BF16)
    onesf = nc.alloc_sbuf_tensor("nonesf", [96, 128], F32)
    negc0 = nc.alloc_sbuf_tensor("nnegc0", [128, 1], F32)
    invsb = nc.alloc_sbuf_tensor("ninvsb", [96, 512], F32)
    invb = nc.alloc_sbuf_tensor("ninvb", [64, 512], F32)
    invb64 = nc.alloc_sbuf_tensor("ninvb64", [64, 128], F32)
    invb128 = nc.alloc_sbuf_tensor("ninvb128", [128, 128], BF16)
    identb = nc.alloc_sbuf_tensor("nidb", [128, 128], BF16)
    identf = nc.alloc_sbuf_tensor("nidf", [128, 128], F32)
    smask = [nc.alloc_sbuf_tensor(f"nsm{i}", [128, 128], BF16) for i in range(3)]
    bsacc = nc.alloc_sbuf_tensor("nbs", [128, 1], F32)
    bstmp = nc.alloc_sbuf_tensor("nbst", [128, 1], F32)
    bsrow = nc.alloc_sbuf_tensor("nbsr", [1, 128], F32)
    top8 = nc.alloc_sbuf_tensor("ntop", [1, 8], F32)
    sel01 = nc.alloc_sbuf_tensor("nsel", [1, 128], F32)
    selkf = nc.alloc_sbuf_tensor("nselkf", [128, 32], F32)
    prb = nc.alloc_sbuf_tensor("nprb", [128, 128], BF16)
    pc = nc.alloc_sbuf_tensor("npc", [128, 128], BF16)
    pdl = nc.alloc_sbuf_tensor("npdl", [128, 128], BF16)
    pdo = nc.alloc_sbuf_tensor("npdo", [128, 128], BF16)
    slide_st = nc.alloc_sbuf_tensor("nslid", [64, 512], BF16)
    sel_sb = nc.alloc_sbuf_tensor("nselo", [64, 512], BF16)
    osum = nc.alloc_sbuf_tensor("nosum", [64, 128], F32)
    otmp = nc.alloc_sbuf_tensor("notmp", [64, 128], F32)
    obuf = [nc.alloc_sbuf_tensor(f"nob{i}", [64, 512], BF16) for i in range(2)]
    gbT = [nc.alloc_sbuf_tensor_at(f"ngbT{j}", [64, T], BF16,
                                   offset=nc.lookup_mloc(xnb[j]).addr) for j in range(3)]
    comp_st = nc.alloc_sbuf_tensor_at("ncomp", [64, T], BF16,
                                      offset=nc.lookup_mloc(xnb[3]).addr)
    pt = nc.alloc_sbuf_tensor_at("npt", [128, PR * 512], BF16,
                                 offset=nc.lookup_mloc(xnb[4]).addr)
    vselp = nc.alloc_sbuf_tensor_at("nvsel", [128, NTC * 65], BF16,
                                    offset=nc.lookup_mloc(xnb[5]).addr)

    ps_a = nc.alloc_psum_tensor("nps_a", [128, 512], F32)
    ps_b = nc.alloc_psum_tensor("nps_b", [128, 512], F32)
    ps_s0 = nc.alloc_psum_tensor("nps_s0", [128, 512], F32)
    ps_s1 = nc.alloc_psum_tensor("nps_s1", [128, 512], F32)
    ps_o = nc.alloc_psum_tensor("nps_o", [128, 512], F32)
    ps_g = nc.alloc_psum_tensor("nps_g", [64, 512], F32)
    ps_c = nc.alloc_psum_tensor("nps_c", [128, 128], F32)
    ps_o2 = nc.alloc_psum_tensor("nps_o2", [128, 128], F32)

    S = {n: nc.alloc_semaphore("n_" + n) for n in
         ["lw", "cw", "xl", "gi", "gld", "pm", "pe", "gm", "ge", "xm",
          "cqk", "cex", "cpv", "cnr", "cbc", "cnc",
          "bsp", "bse", "thr", "sdm", "sde", "vs",
          "sqk", "sex", "spvc", "snr", "sbc", "snc", "sev",
          "dqk", "dex", "dpv", "dnr", "dbc", "dnc",
          "cmb", "so0", "so1", "vch", "gch", "dbs"]}

    with nc.Block() as block:

        @block.gpsimd
        def _(g):
            gch = 0
            for c in range(8):
                g.dma_start(out=xnb[c][:], in_=xnT[c * 128:(c + 1) * 128, :]).then_inc(S["xl"], 16)
            g.dma_start(out=bq_sb[:], in_=bq_d[:]).then_inc(S["cw"], 16)
            g.dma_start(out=bk_sb[:], in_=bk_d[:]).then_inc(S["cw"], 16)
            g.dma_start(out=bv_sb[:], in_=bv_d[:]).then_inc(S["cw"], 16)
            g.dma_start(out=bg_sb[:], in_=bg_d[:]).then_inc(S["cw"], 16)
            g.dma_start(out=wg_sb[:], in_=Wg_d[:]).then_inc(S["cw"], 16)
            for i in range(3):
                g.dma_start(out=smask[i][:], in_=smask_d[i * 128:(i + 1) * 128, :]
                            ).then_inc(S["cw"], 16)
            g.dma_start(out=selm_sb[:], in_=selm_d[:]).then_inc(S["cw"], 16)
            g.memset(onesrow_b[:], 0.0).then_inc(S["gch"], 1)
            gch += 1
            g.wait_ge(S["gch"], gch)
            g.memset(onesrow_b[0:1, :], 1.0).then_inc(S["gi"], 1)
            g.memset(onesf[:], 0.0).then_inc(S["gch"], 1)
            gch += 1
            g.wait_ge(S["gch"], gch)
            g.memset(onesf[64:65, :], 1.0).then_inc(S["gi"], 1)
            g.memset(invsb[64:96, :], 0.0).then_inc(S["gi"], 1)
            g.memset(identb[:], 0.0).then_inc(S["gch"], 1)
            gch += 1
            g.wait_ge(S["gch"], gch)
            g.affine_select(out=identb[:], in_=identb[:], compare_op=ALU.not_equal,
                            fill=1.0, base=0, pattern=[[-1, 128]], channel_multiplier=1
                            ).then_inc(S["gi"], 1)
            g.memset(identf[:], 0.0).then_inc(S["gch"], 1)
            gch += 1
            g.wait_ge(S["gch"], gch)
            g.affine_select(out=identf[:], in_=identf[:], compare_op=ALU.not_equal,
                            fill=1.0, base=0, pattern=[[-1, 128]], channel_multiplier=1
                            ).then_inc(S["gi"], 1)
            for h in range(8):
                g.memset(vp[h][:].rearrange("p (a b) -> p a b", b=65)[:, :, 64:65], 1.0
                         ).then_inc(S["gi"], 1)
            for h in range(8):
                g.memset(cvp[h][:, 64:65], 1.0).then_inc(S["gi"], 1)
            g.memset(bsacc[:], 0.0).then_inc(S["gi"], 1)
            g.memset(gates[:], 0.0).then_inc(S["gi"], 1)
            g.memset(negc0[:], -C0).then_inc(S["gi"], 1)
            for c in range(8):
                g.dma_start(out=w_sb[c][:], in_=Wq_d[c * 128:(c + 1) * 128, :]).then_inc(S["lw"], 16)
            g.wait_ge(S["pm"], 32)
            for c in range(8):
                g.dma_start(out=w_sb[c][:], in_=Wk_d[c * 128:(c + 1) * 128, :]).then_inc(S["lw"], 16)
            g.wait_ge(S["pm"], 68)
            for c in range(8):
                g.dma_start(out=w_sb[c][:], in_=Wv_d[c * 128:(c + 1) * 128, :]).then_inc(S["lw"], 16)

        @block.sync
        def _(sync):
            dram_ap = selk_dram.ap()
            if NSA_DEBUG:
                sync.wait_ge(S["pe"], 197)
                for j in range(3):
                    sync.dma_start(out=gbdbg_d[j * 64:(j + 1) * 64, :], in_=gbT[j][:]
                                   ).then_inc(S["dbs"], 16)
                sync.dma_start(out=pdbg_d[0:32, :], in_=gates[:]).then_inc(S["dbs"], 16)
                sync.dma_start(out=pdbg_d[128:256, :], in_=qT[0][:]).then_inc(S["dbs"], 16)
                sync.dma_start(out=pdbg_d[256:384, :], in_=kT[0][:]).then_inc(S["dbs"], 16)
                sync.dma_start(out=pdbg_d[384:512, 0:NTC * 65], in_=vp[0][:]).then_inc(S["dbs"], 16)
                sync.dma_start(out=pdbg_d[32:64, 0:128], in_=ckT[0][0:32, :]).then_inc(S["dbs"], 16)
                sync.dma_start(out=pdbg_d[64:96, 0:65], in_=cvp[0][0:32, :]).then_inc(S["dbs"], 16)
                sync.dma_start(out=pdbg_d[96:128, 0:128], in_=xmean[0][0:32, :]).then_inc(S["dbs"], 16)
            for h in range(8):
                sync.wait_ge(S["thr"], h + 1)
                if NSA_DEBUG:
                    sync.dma_start(out=ndbg_d[h:h + 1, :], in_=bsrow[:]).then_inc(S["dbs"], 16)
                    sync.dma_start(out=ndbg_d[8 + h:9 + h, :], in_=sel01[:]).then_inc(S["dbs"], 16)
                    sync.wait_ge(S["cnc"], 32 * (h + 1))
                    sync.dma_start(out=cmdbg_d[h * 64:(h + 1) * 64, :], in_=comp_st[:]
                                   ).then_inc(S["dbs"], 16)
                for a in range(4):
                    src = sel01[:].rearrange("p (t a) -> p a t", a=4)[:, a, :]
                    sync.dma_start(out=dram_ap[h * 4 + a:h * 4 + a + 1, :], in_=src,
                                   allow_slow_non_contiguous=True).then_inc(S["sdm"], 16)
                sync.wait_ge(S["sdm"], 64 * (h + 1))
                for a in range(4):
                    src2 = dram_ap[h * 4 + a:h * 4 + a + 1, :].broadcast_to([32, 32])
                    sync.dma_start(out=selkf[a * 32:(a + 1) * 32, :], in_=src2
                                   ).then_inc(S["sde"], 16)
                for qb in range(8):
                    iq = h * 8 + qb
                    sync.wait_ge(S["cmb"], 4 * (iq + 1))
                    sync.dma_start(out=nsaT_d[h * 64:(h + 1) * 64, qb * 512:(qb + 1) * 512],
                                   in_=obuf[qb % 2][:]).then_inc(S[f"so{iq % 2}"], 16)
            sync.wait_ge(S["so0"], 16 * 32)
            sync.wait_ge(S["so1"], 16 * 32)
            if NSA_DEBUG:
                sync.wait_ge(S["dbs"], 16 * 34)

        @block.tensor
        def _(t):
            t.wait_ge(S["gi"], 24)
            t.wait_ge(S["xl"], 128)
            t.wait_ge(S["cw"], 144)
            for tb in range(8):
                if tb:
                    t.wait_ge(S["ge"], tb)
                for c in range(8):
                    mm = t.matmul(ps_g[0:3, :], wg_sb[:, 3 * c:3 * c + 3],
                                  xnb[c][:, tb * 512:(tb + 1) * 512],
                                  start=(c == 0), stop=(c == 7))
                    if c == 7:
                        mm.then_inc(S["gm"], 1)
            banks = [ps_a, ps_b]
            n = 0
            t.wait_ge(S["lw"], 128)
            for e in range(4):
                for tb in range(8):
                    if n >= 2:
                        t.wait_ge(S["pe"], n - 1)
                    bank = banks[n % 2]
                    for c in range(8):
                        mm = t.matmul(bank[:], w_sb[c][:, e * 128:(e + 1) * 128],
                                      xnb[c][:, tb * 512:(tb + 1) * 512],
                                      start=(c == 0), stop=(c == 7))
                    mm.then_inc(S["pm"], 1)
                    n += 1
            t.wait_ge(S["lw"], 256)
            for e in range(4):
                for tb in range(8):
                    t.wait_ge(S["pe"], n - 1)
                    bank = banks[n % 2]
                    for c in range(8):
                        mm = t.matmul(bank[:], w_sb[c][:, e * 128:(e + 1) * 128],
                                      xnb[c][:, tb * 512:(tb + 1) * 512],
                                      start=(c == 0), stop=(c == 7))
                    mm.then_inc(S["pm"], 1)
                    n += 1
            t.wait_ge(S["xm"], 8)
            for i in range(4):
                t.wait_ge(S["pe"], n - 1)
                bank = banks[n % 2]
                for c in range(8):
                    mm = t.matmul(bank[0:128, 0:128], w_sb[c][:, i * 128:(i + 1) * 128],
                                  xmean[c][:], start=(c == 0), stop=(c == 7))
                mm.then_inc(S["pm"], 1)
                n += 1
            t.wait_ge(S["lw"], 384)
            for tc in range(32):
                t.wait_ge(S["pe"], n - 1)
                bank = banks[n % 2]
                for c in range(8):
                    t.matmul(bank[:], xnb[c][:, tc * 128:(tc + 1) * 128], w_sb[c][:],
                             start=(c == 0), stop=False)
                t.matmul(bank[:], onesrow_b[:], bv_sb[:], start=False, stop=True
                         ).then_inc(S["pm"], 1)
                n += 1
            t.wait_ge(S["pe"], n - 1)
            bank = banks[n % 2]
            for c in range(8):
                t.matmul(bank[:], xmean[c][:], w_sb[c][:], start=(c == 0), stop=False)
            t.matmul(bank[:], onesrow_b[:], bv_sb[:], start=False, stop=True
                     ).then_inc(S["pm"], 1)
            n += 1
            t.wait_ge(S["ge"], 8)
            for qt in range(32):
                for j in range(3):
                    t.wait_ge(S["pe"], n - 1)
                    bank = banks[n % 2]
                    t.matmul(bank[0:64, 0:128], selm_sb[:, j * 64:(j + 1) * 64],
                             gates[:, qt * 128:(qt + 1) * 128],
                             start=True, stop=True).then_inc(S["pm"], 1)
                    n += 1
            t.wait_ge(S["pe"], n)
            for h in range(8):
                i4, ro = h // 2, (h % 2) * 64
                if h:
                    t.wait_ge(S["dnc"], 32 * h)
                    t.wait_ge(S["dex"], 64 * h)
                for qt in range(32):
                    idx = h * 32 + qt
                    if idx:
                        t.wait_ge(S["cex"], idx)
                    t.matmul(ps_c[:], ckT[i4][ro:ro + 64, :],
                             qT[i4][ro:ro + 64, qt * 128:(qt + 1) * 128],
                             start=True, stop=True).then_inc(S["cqk"], 1)
                    t.wait_ge(S["cex"], idx + 1)
                    if idx:
                        t.wait_ge(S["cnc"], idx)
                    t.matmul(ps_o2[0:65, :], cvp[h][:], pc[:], start=True, stop=True
                             ).then_inc(S["cpv"], 1)
                    t.wait_ge(S["cnr"], idx + 1)
                    t.matmul(ps_s1[0:128, 0:128], onesf[64:96, :], invsb[64:96, 0:128],
                             start=True, stop=True).then_inc(S["cbc"], 1)
                t.wait_ge(S["cnc"], 32 * (h + 1))
                t.matmul(ps_c[0:1, 0:128], bsacc[:], identf[:], start=True, stop=True
                         ).then_inc(S["bsp"], 1)
                t.wait_ge(S["vs"], h + 1)
                for qb in range(8):
                    iq = h * 8 + qb
                    if iq:
                        t.wait_ge(S["dnc"], 4 * iq)
                    if iq:
                        t.wait_ge(S["sev"], iq)
                    for kc in range(NTC + 1):
                        if kc < NTC:
                            ns = iq * NTC + kc
                            if ns >= 2:
                                t.wait_ge(S["sex"], ns - 1)
                            sbank = [ps_s0, ps_s1][ns % 2]
                            t.matmul(sbank[:], kT[i4][ro:ro + 64, kc * 128:(kc + 1) * 128],
                                     qT[i4][ro:ro + 64, qb * 512:(qb + 1) * 512],
                                     start=True, stop=True).then_inc(S["sqk"], 1)
                        if kc >= 1:
                            kp = kc - 1
                            np_ = iq * NTC + kp
                            t.wait_ge(S["sex"], np_ + 1)
                            t.matmul(ps_o[0:65, :], vselp[:, kp * 65:(kp + 1) * 65],
                                     pt[:, (np_ % PR) * 512:(np_ % PR) * 512 + 512],
                                     start=(kp == 0), stop=(kp == NTC - 1)).then_inc(S["spvc"], 1)
                    t.wait_ge(S["snr"], iq + 1)
                    if iq:
                        t.wait_ge(S["snc"], iq)
                    t.matmul(ps_g[0:64, :], onesf[64:96, 0:64], invsb[64:96, 0:512],
                             start=True, stop=True).then_inc(S["sbc"], 1)
                    if qb == 0:
                        t.wait_ge(S["bse"], h + 1)
                    for s4 in range(4):
                        qt = qb * 4 + s4
                        i4x = iq * 4 + s4
                        if i4x:
                            t.wait_ge(S["dex"], 2 * i4x)
                        lc = max(qt - 1, 0)
                        mi = 2 if qt == 0 else 0
                        t.matmul(ps_c[:], kT[i4][ro:ro + 64, lc * 128:(lc + 1) * 128],
                                 qT[i4][ro:ro + 64, qt * 128:(qt + 1) * 128],
                                 start=True, stop=False)
                        t.matmul(ps_c[:], identb[:], smask[mi][:], start=False, stop=True
                                 ).then_inc(S["dqk"], 1)
                        t.matmul(ps_a[0:128, 0:128], kT[i4][ro:ro + 64, qt * 128:(qt + 1) * 128],
                                 qT[i4][ro:ro + 64, qt * 128:(qt + 1) * 128],
                                 start=True, stop=False)
                        t.matmul(ps_a[0:128, 0:128], identb[:], smask[1][:], start=False, stop=True
                                 ).then_inc(S["dqk"], 1)
                        if i4x:
                            t.wait_ge(S["dnc"], i4x)
                        t.wait_ge(S["dex"], 2 * i4x + 1)
                        t.matmul(ps_o2[0:65, :], vp[h][:, lc * 65:(lc + 1) * 65], pdl[:],
                                 start=True, stop=False)
                        t.wait_ge(S["dex"], 2 * i4x + 2)
                        t.matmul(ps_o2[0:65, :], vp[h][:, qt * 65:(qt + 1) * 65], pdo[:],
                                 start=False, stop=True).then_inc(S["dpv"], 1)
                        t.wait_ge(S["dnr"], i4x + 1)
                        t.matmul(ps_s1[0:64, 0:128], onesf[64:96, 0:64], invsb[64:96, 0:128],
                                 start=True, stop=True).then_inc(S["dbc"], 1)

        @block.scalar
        def _(sc):
            sc.wait_ge(S["cw"], 144)
            sc.wait_ge(S["gi"], 24)
            for tb in range(8):
                sc.wait_ge(S["gm"], tb + 1)
                sc.activation(gates[0:3, tb * 512:(tb + 1) * 512], ps_g[0:3, :],
                              AF.Sigmoid, bias=bg_sb[:]).then_inc(S["ge"], 1)
            for h in range(8):
                for qt in range(32):
                    idx = h * 32 + qt
                    sc.wait_ge(S["cqk"], idx + 1)
                    if idx:
                        sc.wait_ge(S["cnc"], idx)
                    sc.activation(pc[:], ps_c[:], AF.Exp, bias=negc0[0:128, :], scale=0.125
                                  ).then_inc(S["cex"], 1)
                for qb in range(8):
                    iq = h * 8 + qb
                    for kc in range(NTC):
                        ns = iq * NTC + kc
                        sc.wait_ge(S["sqk"], ns + 1)
                        if ns >= PR:
                            sc.wait_ge(S["spvc"], ns - PR + 1)
                        sbank = [ps_s0, ps_s1][ns % 2]
                        sc.activation(pt[:, (ns % PR) * 512:(ns % PR) * 512 + 512], sbank[:],
                                      AF.Exp, bias=negc0[0:128, :], scale=0.125
                                      ).then_inc(S["sex"], 1)
                    for s4 in range(4):
                        i4x = iq * 4 + s4
                        sc.wait_ge(S["dqk"], 2 * i4x + 1)
                        sc.activation(pdl[:], ps_c[:], AF.Exp, bias=negc0[0:128, :], scale=0.125
                                      ).then_inc(S["dex"], 1)
                        sc.wait_ge(S["dqk"], 2 * i4x + 2)
                        sc.activation(pdo[:], ps_a[0:128, 0:128], AF.Exp,
                                      bias=negc0[0:128, :], scale=0.125).then_inc(S["dex"], 1)

        @block.vector
        def _(v):
            vch = [0]

            def selfw(tag):
                v.wait_ge(S["vch"], vch[0])

            v.wait_ge(S["cw"], 144)
            v.wait_ge(S["xl"], 128)
            v.wait_ge(S["gi"], 24)
            banks = [ps_a, ps_b]
            n = 0
            for e in range(4):
                for tb in range(8):
                    v.wait_ge(S["pm"], n + 1)
                    v.tensor_scalar(qT[e][:, tb * 512:(tb + 1) * 512], banks[n % 2][:],
                                    bq_sb[:, e:e + 1], None, ALU.add).then_inc(S["pe"], 1)
                    n += 1
            for e in range(4):
                for tb in range(8):
                    v.wait_ge(S["pm"], n + 1)
                    v.tensor_scalar(kT[e][:, tb * 512:(tb + 1) * 512], banks[n % 2][:],
                                    bk_sb[:, e:e + 1], None, ALU.add).then_inc(S["pe"], 1)
                    n += 1
            for c in range(8):
                v.reduce_sum(xmf[:].rearrange("p (n o) -> p n o", o=1),
                             xnb[c][:].rearrange("p (n w) -> p n w", w=32),
                             axis=AX.X).then_inc(S["vch"], 1)
                vch[0] += 1
                selfw("xm")
                v.tensor_scalar(xmean[c][:], xmf[:], 1.0 / 32.0, None, ALU.mult
                                ).then_inc(S["xm"], 1)
            for i in range(4):
                v.wait_ge(S["pm"], n + 1)
                v.tensor_scalar(ckT[i][:], banks[n % 2][0:128, 0:128],
                                bk_sb[:, i:i + 1], None, ALU.add).then_inc(S["pe"], 1)
                n += 1
            for tc in range(32):
                v.wait_ge(S["pm"], n + 1)
                for h8 in range(8):
                    cp = v.tensor_copy(vp[h8][:, tc * 65:tc * 65 + 64],
                                       banks[n % 2][:, h8 * 64:(h8 + 1) * 64])
                cp.then_inc(S["pe"], 1)
                n += 1
            v.wait_ge(S["pm"], n + 1)
            for h8 in range(8):
                cp = v.tensor_copy(cvp[h8][:, 0:64], banks[n % 2][:, h8 * 64:(h8 + 1) * 64])
            cp.then_inc(S["pe"], 1)
            n += 1
            for qt in range(32):
                for j in range(3):
                    v.wait_ge(S["pm"], n + 1)
                    v.tensor_copy(gbT[j][:, qt * 128:(qt + 1) * 128], banks[n % 2][0:64, 0:128]
                                  ).then_inc(S["pe"], 1)
                    n += 1
            for h in range(8):
                for qt in range(32):
                    idx = h * 32 + qt
                    v.wait_ge(S["cpv"], idx + 1)
                    v.reciprocal(invsb[64:65, 0:128], ps_o2[64:65, :]).then_inc(S["cnr"], 1)
                    v.wait_ge(S["cbc"], idx + 1)
                    v.tensor_copy(invb64[:], ps_s1[0:64, 0:128])
                    v.tensor_copy(invb128[:], ps_s1[0:128, 0:128]).then_inc(S["vch"], 1)
                    vch[0] += 1
                    selfw("c")
                    v.tensor_mul(comp_st[:, qt * 128:(qt + 1) * 128], ps_o2[0:64, :], invb64[:])
                    v.tensor_mul(prb[:], pc[:], invb128[:]).then_inc(S["vch"], 1)
                    vch[0] += 1
                    selfw("c2")
                    v.reduce_sum(bstmp[:], prb[:], axis=AX.X).then_inc(S["vch"], 1)
                    vch[0] += 1
                    selfw("c3")
                    v.tensor_add(bsacc[:], bstmp[:], bsacc[:]).then_inc(S["cnc"], 1)
                v.wait_ge(S["bsp"], h + 1)
                if h:
                    v.wait_ge(S["sdm"], 64 * h)
                v.tensor_copy(bsrow[:], ps_c[0:1, 0:128]).then_inc(S["bse"], 1)
                v.wait_ge(S["bse"], h + 1)
                v.max(top8[:], bsrow[:]).then_inc(S["vch"], 1)
                vch[0] += 1
                selfw("t8")
                v.tensor_scalar(sel01[:], bsrow[:], top8[0:1, 7:8], None, ALU.is_ge
                                ).then_inc(S["thr"], 1)
                v.tensor_scalar(bsacc[:], bsacc[:], 0.0, None, ALU.mult)
                v.wait_ge(S["sde"], 64 * (h + 1))
                if h:
                    v.wait_ge(S["spvc"], 256 * h)
                for tc in range(32):
                    ts = v.tensor_scalar(vselp[:, tc * 65:(tc + 1) * 65],
                                         vp[h][:, tc * 65:(tc + 1) * 65],
                                         selkf[:, tc:tc + 1], None, ALU.mult)
                ts.then_inc(S["vs"], 1)
                for qb in range(8):
                    iq = h * 8 + qb
                    v.wait_ge(S["spvc"], 32 * (iq + 1))
                    v.reciprocal(invsb[64:65, 0:512], ps_o[64:65, :]).then_inc(S["snr"], 1)
                    v.wait_ge(S["sbc"], iq + 1)
                    v.tensor_copy(invb[:], ps_g[0:64, :]).then_inc(S["snc"], 1)
                    v.wait_ge(S["snc"], iq + 1)
                    v.tensor_mul(sel_sb[:], ps_o[0:65 - 1, :], invb[:]).then_inc(S["sev"], 1)
                    for s4 in range(4):
                        i4x = iq * 4 + s4
                        v.wait_ge(S["dpv"], i4x + 1)
                        v.reciprocal(invsb[64:65, 0:128], ps_o2[64:65, :]).then_inc(S["dnr"], 1)
                        v.wait_ge(S["dbc"], i4x + 1)
                        v.tensor_copy(invb64[:], ps_s1[0:64, 0:128]).then_inc(S["vch"], 1)
                        vch[0] += 1
                        selfw("d")
                        v.tensor_mul(slide_st[:, s4 * 128:(s4 + 1) * 128], ps_o2[0:64, :],
                                     invb64[:]).then_inc(S["dnc"], 1)
                    for s4 in range(4):
                        qt = qb * 4 + s4
                        if iq >= 2 and s4 == 0:
                            v.wait_ge(S[f"so{iq % 2}"], 16 * (iq // 2))
                        v.tensor_mul(osum[:], comp_st[:, qt * 128:(qt + 1) * 128],
                                     gbT[0][:, qt * 128:(qt + 1) * 128])
                        v.tensor_mul(otmp[:], sel_sb[:, s4 * 128:(s4 + 1) * 128],
                                     gbT[1][:, qt * 128:(qt + 1) * 128])
                        v.tensor_add(osum[:], osum[:], otmp[:])
                        v.tensor_mul(otmp[:], slide_st[:, s4 * 128:(s4 + 1) * 128],
                                     gbT[2][:, qt * 128:(qt + 1) * 128])
                        v.tensor_add(obuf[qb % 2][:, s4 * 128:(s4 + 1) * 128], osum[:], otmp[:]
                                     ).then_inc(S["cmb"], 1)

    print("NSA sbuf remaining:", nc.sbuf_bytes_remaining)
    return nc


def _nsa_host(inputs):
    Wq = np.asarray(inputs["L1_attn_Wq"], np.float32)
    Wk = np.asarray(inputs["L1_attn_Wk"], np.float32)
    Wv = np.asarray(inputs["L1_attn_Wv"], np.float32)
    Wg = np.asarray(inputs["L1_attn_Wg"], np.float32)
    bq = np.asarray(inputs["L1_attn_bq"], np.float32)
    bk = np.asarray(inputs["L1_attn_bk"], np.float32)
    bv = np.asarray(inputs["L1_attn_bv"], np.float32)
    bg = np.asarray(inputs["L1_attn_bg"], np.float32)
    wgr = np.ascontiguousarray(Wg.reshape(8, 128, 3).transpose(1, 0, 2).reshape(128, 24))
    kk = np.arange(128)[:, None]
    qq = np.arange(128)[None, :]
    sm = np.zeros((3 * 128, 128), np.float32)
    sm[0:128] = np.where(kk > qq, 0.0, -1e9)
    sm[128:256] = np.where(kk <= qq, 0.0, -1e9)
    sm[256:384] = -1e9
    selm = np.zeros((32, 192), np.float32)
    for j in range(3):
        selm[j, j * 64:(j + 1) * 64] = 1.0
    out = []
    for ph in range(2):
        sl = slice(ph * 512, (ph + 1) * 512)
        bvp = np.zeros((32, 512), BF)
        bvp[0] = bv[sl].astype(BF)
        out.append({
            "Wq": np.ascontiguousarray(Wq[:, sl]),
            "Wk": np.ascontiguousarray(Wk[:, sl]),
            "Wv": np.ascontiguousarray(Wv[:, sl]),
            "Wgr": wgr,
            "bq": np.ascontiguousarray(bq[sl].reshape(4, 128).T),
            "bk": np.ascontiguousarray(bk[sl].reshape(4, 128).T),
            "bvp": bvp,
            "bg": bg.reshape(3, 1),
            "smask": sm.astype(BF),
            "selm": selm.astype(BF),
        })
    return out


def run_nsa(inputs, xnT_percore):
    nc = _get("nsa", build_nsa)
    host = _nsa_host(inputs)
    in_maps = []
    for c in range(NCORE):
        m = {"xnT": xnT_percore[c]}
        m.update(host[c % 2])
        in_maps.append(m)
    res = run_bass_kernel_spmd(nc, in_maps, core_ids=list(range(NCORE)))
    if NSA_DEBUG:
        run_nsa.dbg = [{k: np.asarray(res.results[c][k]) for k in ("ndbg", "gbdbg", "cmdbg", "pdbg")}
                       for c in range(NCORE)]
    return [np.asarray(res.results[c]["nsaT"]) for c in range(NCORE)]



def build_rwkv():
    nc = bass.Bass("TRN2", target_bir_lowering=False, num_devices=NCORE)
    TH = T // 2
    xfT = nc.dram_tensor("xfT", [E, TH + 1], F32, kind="ExternalInput").ap()
    x3T = nc.dram_tensor("x3T", [E, TH], F32, kind="ExternalInput").ap()
    wrT_d = nc.dram_tensor("wrT", [E, E], F32, kind="ExternalInput").ap()
    wkT_d = nc.dram_tensor("wkT", [E, E], F32, kind="ExternalInput").ap()
    wvT_d = nc.dram_tensor("wvT", [E, E], F32, kind="ExternalInput").ap()
    mu_d = nc.dram_tensor("mucols", [128, 32], F32, kind="ExternalInput").ap()
    x4T_d = nc.dram_tensor("x4T", [E, TH], F32, kind="ExternalOutput").ap()

    TB = 512
    NTB = TH // TB

    wrT = [nc.alloc_sbuf_tensor(f"kwr{c}", [128, E], BF16) for c in range(8)]
    wkT = [nc.alloc_sbuf_tensor(f"kwk{c}", [128, E], BF16) for c in range(8)]
    wvT = [nc.alloc_sbuf_tensor(f"kwv{c}", [128, E], BF16) for c in range(8)]
    mu_sb = nc.alloc_sbuf_tensor("kmu", [128, 32], F32)
    xfin = [nc.alloc_sbuf_tensor(f"kxf{c}", [128, TB + 1], F32) for c in range(8)]
    xf3 = [nc.alloc_sbuf_tensor(f"kx3{c}", [128, TB], F32) for c in range(8)]
    xrb = [nc.alloc_sbuf_tensor(f"kxr{c}", [128, TB], BF16) for c in range(8)]
    xkb = [nc.alloc_sbuf_tensor(f"kxk{c}", [128, TB], BF16) for c in range(8)]
    rb = [nc.alloc_sbuf_tensor(f"krb{c}", [128, TB], F32) for c in range(8)]
    krl = [nc.alloc_sbuf_tensor(f"krl{c}", [128, TB], BF16) for c in range(8)]
    kab = [nc.alloc_sbuf_tensor(f"kka{c}", [128, TB], BF16) for c in range(8)]
    yb = [nc.alloc_sbuf_tensor(f"kyb{c}", [128, TB], BF16) for c in range(8)]
    ysq = [nc.alloc_sbuf_tensor(f"kys{c}", [128, TB], BF16) for c in range(8)]
    x4f = [nc.alloc_sbuf_tensor(f"kx4{c}", [128, TB], F32) for c in range(8)]
    tmpm = nc.alloc_sbuf_tensor("ktmp", [128, TB], F32)
    tmp1 = nc.alloc_sbuf_tensor("ktm1", [128, TB], F32)
    rowbuf = nc.alloc_sbuf_tensor("krow", [1, 6 * TB], F32)
    srow = rowbuf[:, 0:TB]
    s2row = rowbuf[:, TB:2 * TB]
    vrow = rowbuf[:, 2 * TB:3 * TB]
    stdrow = rowbuf[:, 3 * TB:4 * TB]
    arow = rowbuf[:, 4 * TB:5 * TB]
    rowbuf_b = nc.alloc_sbuf_tensor("krow_b", [32, 2 * TB], BF16)
    mrow_b = rowbuf_b[:, 0:TB]
    arow_b = rowbuf_b[:, TB:2 * TB]
    ab_sb = nc.alloc_sbuf_tensor("kab_sb", [128, TB], F32)
    mb_sb = nc.alloc_sbuf_tensor("kmb_sb", [128, TB], F32)
    onesrow_b = nc.alloc_sbuf_tensor("kones", [32, 128], BF16)
    onescol_b = nc.alloc_sbuf_tensor("konec", [128, 1], BF16)

    ps_a = nc.alloc_psum_tensor("kps_a", [128, 512], F32)
    ps_b = nc.alloc_psum_tensor("kps_b", [128, 512], F32)
    ps_stat = nc.alloc_psum_tensor("kps_st", [1, 2 * TB], F32)
    ps_bc = nc.alloc_psum_tensor("kps_bc", [128, TB], F32)
    ps_bc2 = nc.alloc_psum_tensor("kps_b2", [128, TB], F32)

    S = {n: nc.alloc_semaphore("k_" + n) for n in
         ["lw", "cw", "gi", "xfl", "x3l", "mx", "mv", "yes", "ksq", "yev",
          "sq", "st", "rm1", "sqd", "rowm", "bc", "x4d", "so", "gch"]}

    with nc.Block() as block:

        @block.gpsimd
        def _(g):
            g.dma_start(out=mu_sb[:], in_=mu_d[:]).then_inc(S["cw"], 16)
            for c in range(8):
                g.dma_start(out=wrT[c][:], in_=wrT_d[c * 128:(c + 1) * 128, :]).then_inc(S["lw"], 16)
            for c in range(8):
                g.dma_start(out=wkT[c][:], in_=wkT_d[c * 128:(c + 1) * 128, :]).then_inc(S["lw"], 16)
            for c in range(8):
                g.dma_start(out=wvT[c][:], in_=wvT_d[c * 128:(c + 1) * 128, :]).then_inc(S["lw"], 16)
            g.memset(onesrow_b[:], 0.0).then_inc(S["gch"], 1)
            g.wait_ge(S["gch"], 1)
            g.memset(onesrow_b[0:1, :], 1.0).then_inc(S["gi"], 1)
            g.memset(onescol_b[:], 1.0).then_inc(S["gi"], 1)
            g.memset(rowbuf_b[:], 0.0).then_inc(S["gi"], 1)

        @block.sync
        def _(sync):
            for tb in range(NTB):
                if tb > 0:
                    sync.wait_ge(S["mx"], 8 * tb)
                for c in range(8):
                    sync.dma_start(out=xfin[c][:],
                                   in_=xfT[c * 128:(c + 1) * 128, tb * TB:(tb + 1) * TB + 1]
                                   ).then_inc(S["xfl"], 16)
                if tb > 0:
                    sync.wait_ge(S["x4d"], 8 * tb)
                for c in range(8):
                    sync.dma_start(out=xf3[c][:],
                                   in_=x3T[c * 128:(c + 1) * 128, tb * TB:(tb + 1) * TB]
                                   ).then_inc(S["x3l"], 16)
                sync.wait_ge(S["x4d"], 8 * (tb + 1))
                for c in range(8):
                    sync.dma_start(out=x4T_d[c * 128:(c + 1) * 128, tb * TB:(tb + 1) * TB],
                                   in_=x4f[c][:]).then_inc(S["so"], 16)
            sync.wait_ge(S["so"], 16 * 8 * NTB)

        @block.tensor
        def _(t):
            t.wait_ge(S["lw"], 16 * 24)
            t.wait_ge(S["gi"], 3)
            for tb in range(NTB):
                t.wait_ge(S["mx"], 8 * (tb + 1))
                for e in range(8):
                    if e >= 2:
                        t.wait_ge(S["yes"], 16 * tb + e - 1)
                    elif tb > 0:
                        t.wait_ge(S["yev"], 8 * tb + e - 1)
                    bank = ps_a if (24 * tb + e) % 2 == 0 else ps_b
                    for c in range(8):
                        i = t.matmul(bank[:], wrT[c][:, e * 128:(e + 1) * 128], xrb[c][:],
                                     start=(c == 0), stop=(c == 7))
                    i.then_inc(S["mv"], 1)
                for e in range(8):
                    if e >= 2:
                        t.wait_ge(S["yes"], 16 * tb + 8 + e - 1)
                    else:
                        t.wait_ge(S["yes"], 16 * tb + e + 7)
                    bank = ps_a if (24 * tb + 8 + e) % 2 == 0 else ps_b
                    for c in range(8):
                        i = t.matmul(bank[:], wkT[c][:, e * 128:(e + 1) * 128], xkb[c][:],
                                     start=(c == 0), stop=(c == 7))
                    i.then_inc(S["mv"], 1)
                t.wait_ge(S["ksq"], 8 * (tb + 1))
                for e in range(8):
                    if e >= 2:
                        t.wait_ge(S["yev"], 8 * tb + e - 1)
                    else:
                        t.wait_ge(S["yes"], 16 * tb + e + 15)
                    bank = ps_a if (24 * tb + 16 + e) % 2 == 0 else ps_b
                    for c in range(8):
                        i = t.matmul(bank[:], wvT[c][:, e * 128:(e + 1) * 128], kab[c][:],
                                     start=(c == 0), stop=(c == 7))
                    i.then_inc(S["mv"], 1)
                t.wait_ge(S["sq"], 8 * (tb + 1))
                if tb > 0:
                    t.wait_ge(S["rm1"], tb)
                    t.wait_ge(S["rowm"], 2 * tb)
                for c in range(8):
                    t.matmul(ps_stat[:, 0:TB], onescol_b[:], yb[c][:],
                             start=(c == 0), stop=(c == 7))
                for c in range(8):
                    i = t.matmul(ps_stat[:, TB:2 * TB], onescol_b[:], ysq[c][:],
                                 start=(c == 0), stop=(c == 7))
                i.then_inc(S["st"], 1)
                t.wait_ge(S["rowm"], 2 * tb + 1)
                t.matmul(ps_bc[:], onesrow_b[:], mrow_b[0:32, :], start=True, stop=True)
                i = t.matmul(ps_bc2[:], onesrow_b[:], arow_b[0:32, :], start=True, stop=True)
                i.then_inc(S["bc"], 1)

        @block.scalar
        def _(sc):
            for tb in range(NTB):
                for e in range(8):
                    sc.wait_ge(S["mv"], 24 * tb + e + 1)
                    bank = ps_a if (24 * tb + e) % 2 == 0 else ps_b
                    sc.activation(rb[e][:], bank[:], AF.Sigmoid).then_inc(S["yes"], 1)
                for e in range(8):
                    sc.wait_ge(S["mv"], 24 * tb + 8 + e + 1)
                    bank = ps_a if (24 * tb + 8 + e) % 2 == 0 else ps_b
                    sc.activation(krl[e][:], bank[:], AF.Relu).then_inc(S["yes"], 1)
                    sc.activation(kab[e][:], krl[e][:], AF.Square).then_inc(S["ksq"], 1)
                for e in range(8):
                    sc.wait_ge(S["yev"], 8 * tb + e + 1)
                    sc.activation(ysq[e][:], yb[e][:], AF.Square).then_inc(S["sq"], 1)
                sc.wait_ge(S["rm1"], tb + 1)
                sc.activation(stdrow, s2row, AF.Sqrt).then_inc(S["sqd"], 1)

        @block.vector
        def _(v):
            v.wait_ge(S["cw"], 16)
            v.wait_ge(S["gi"], 3)
            for tb in range(NTB):
                v.wait_ge(S["xfl"], 128 * (tb + 1))
                if tb > 0:
                    v.wait_ge(S["mv"], 24 * (tb - 1) + 16)
                for c in range(8):
                    v.tensor_scalar(tmpm[:], xfin[c][:, 0:TB], mu_sb[:, 8 + c:9 + c],
                                    None, ALU.mult)
                    v.scalar_tensor_tensor(xrb[c][:], xfin[c][:, 1:TB + 1], mu_sb[:, c:c + 1],
                                           tmpm[:], ALU.mult, ALU.add)
                    v.tensor_scalar(tmpm[:], xfin[c][:, 0:TB], mu_sb[:, 24 + c:25 + c],
                                    None, ALU.mult)
                    v.scalar_tensor_tensor(xkb[c][:], xfin[c][:, 1:TB + 1], mu_sb[:, 16 + c:17 + c],
                                           tmpm[:], ALU.mult, ALU.add).then_inc(S["mx"], 1)
                for e in range(8):
                    v.wait_ge(S["mv"], 24 * tb + 16 + e + 1)
                    v.wait_ge(S["yes"], 16 * tb + e + 1)
                    if tb > 0 and e == 0:
                        v.wait_ge(S["so"], 16 * 8 * tb)
                    bank = ps_a if (24 * tb + 16 + e) % 2 == 0 else ps_b
                    v.tensor_mul(yb[e][:], bank[:], rb[e][:]).then_inc(S["yev"], 1)
                v.wait_ge(S["st"], tb + 1)
                v.tensor_scalar_mul(srow, ps_stat[:, 0:TB], 1.0 / 1024)
                v.tensor_scalar_mul(s2row, ps_stat[:, TB:2 * TB], 1.0 / 1024)
                v.tensor_mul(vrow, srow, srow)
                v.tensor_sub(s2row, s2row, vrow)
                v.tensor_scalar_add(s2row, s2row, EPS).then_inc(S["rm1"], 1)
                v.wait_ge(S["sqd"], tb + 1)
                v.reciprocal(arow, stdrow)
                v.tensor_copy(mrow_b[0:1, :], srow)
                v.tensor_copy(arow_b[0:1, :], arow).then_inc(S["rowm"], 1)
                v.wait_ge(S["bc"], tb + 1)
                v.wait_ge(S["x3l"], 128 * (tb + 1))
                v.tensor_copy(ab_sb[:], ps_bc2[:])
                v.tensor_copy(mb_sb[:], ps_bc[:]).then_inc(S["rowm"], 1)
                for c in range(8):
                    v.tensor_sub(tmp1[:], yb[c][:], mb_sb[:])
                    v.tensor_mul(tmp1[:], tmp1[:], ab_sb[:])
                    v.tensor_add(x4f[c][:], xf3[c][:], tmp1[:]).then_inc(S["x4d"], 1)

    print("RWKV sbuf remaining:", nc.sbuf_bytes_remaining)
    return nc


def run_rwkv(inputs, xf_full, x3_full):
    nc = _get("rwkv", build_rwkv)
    TH = T // 2
    mu_r = np.asarray(inputs["L1_ffn_mu_r"], np.float32)
    mu_k = np.asarray(inputs["L1_ffn_mu_k"], np.float32)
    mu = np.zeros((128, 32), np.float32)
    mu[:, 0:8] = mu_r.reshape(8, 128).T
    mu[:, 8:16] = (1.0 - mu_r).reshape(8, 128).T
    mu[:, 16:24] = mu_k.reshape(8, 128).T
    mu[:, 24:32] = (1.0 - mu_k).reshape(8, 128).T
    wrT = np.ascontiguousarray(np.asarray(inputs["L1_ffn_W_r"], np.float32).T)
    wkT = np.ascontiguousarray(np.asarray(inputs["L1_ffn_W_k"], np.float32).T)
    wvT = np.ascontiguousarray(np.asarray(inputs["L1_ffn_W_v"], np.float32).T)
    in_maps = []
    for c in range(NCORE):
        b, p = c // 2, c % 2
        xfTc = np.zeros((E, TH + 1), np.float32)
        xfTc[:, 1:] = xf_full[b][:, p * TH:(p + 1) * TH]
        if p == 1:
            xfTc[:, 0] = xf_full[b][:, TH - 1]
        in_maps.append({
            "xfT": xfTc,
            "x3T": np.ascontiguousarray(x3_full[b][:, p * TH:(p + 1) * TH]),
            "wrT": wrT, "wkT": wkT, "wvT": wvT,
            "mucols": mu,
        })
    res = run_bass_kernel_spmd(nc, in_maps, core_ids=list(range(NCORE)))
    out = np.zeros((4, E, T), np.float32)
    for c in range(NCORE):
        b, p = c // 2, c % 2
        out[b][:, p * TH:(p + 1) * TH] = np.asarray(res.results[c]["x4T"])
    return out


def kernel(**inputs):
    TH = T // 2
    x = np.asarray(inputs["L0_attn_x"], np.float32)
    xT_percore = [np.ascontiguousarray(x[c // 2].T) for c in range(NCORE)]

    mla_outs = run_mla(inputs, xT_percore)
    x1h, r0h = run_mlafin(inputs, xT_percore, mla_outs)

    r = np.zeros((4, E, T), np.float32)
    for c in range(NCORE):
        b, p = c // 2, c % 2
        r[b][:, p * TH:(p + 1) * TH] = r0h[c]
    for L in range(6):
        r = run_iattn(inputs, L, r)
        r = run_iffn(inputs, L, r)

    r6h = [np.ascontiguousarray(r[c // 2][:, (c % 2) * TH:((c % 2) + 1) * TH]).astype(BF)
           for c in range(NCORE)]
    eye = np.eye(E, dtype=np.float32)
    x2h, xn3h = run_addln(x1h, r6h, eye)

    xn3T_percore = [np.ascontiguousarray(np.concatenate([xn3h[2 * (c // 2)],
                                                         xn3h[2 * (c // 2) + 1]], axis=1))
                    for c in range(NCORE)]
    nsa_outs = run_nsa(inputs, xn3T_percore)
    moh = []
    for c in range(NCORE):
        b, p = c // 2, c % 2
        o1T = np.concatenate([nsa_outs[2 * b], nsa_outs[2 * b + 1]], axis=0)
        moh.append(np.ascontiguousarray(o1T[:, p * TH:(p + 1) * TH]))
    x3h, xfh = run_addln(x2h, moh, inputs["L1_attn_Wo"], inputs["L1_attn_bo"])

    xf_full = np.zeros((4, E, T), np.float32)
    x3_full = np.zeros((4, E, T), np.float32)
    for c in range(NCORE):
        b, p = c // 2, c % 2
        xf_full[b][:, p * TH:(p + 1) * TH] = xfh[c]
        x3_full[b][:, p * TH:(p + 1) * TH] = x3h[c]
    x4 = run_rwkv(inputs, xf_full, x3_full)
    return np.ascontiguousarray(x4.transpose(0, 2, 1))


